# version 01
# baseline (speedup 1.0000x reference)
import math
import numpy as np
import ml_dtypes
import contextlib
import concourse.bass as bass
import concourse.mybir as mybir
from concourse.bass_utils import run_bass_kernel_spmd

F32 = mybir.dt.float32
BF16 = mybir.dt.bfloat16
U8 = mybir.dt.uint8
AF = mybir.ActivationFunctionType
ALU = mybir.AluOpType
AX = mybir.AxisListType

BF = ml_dtypes.bfloat16

N_CORES = 8
D = 1024
T = 2048
NSEQ = 2
KC = D // 128
NEG = -1.0e9

ENGINES = ("sync", "scalar", "vector", "tensor", "gpsimd")
SIM_NO_CC = False


def cdiv(a, b):
    return (a + b - 1) // b



NDMA = 12
NSW = 64
BIG_CONSTS = {"rope0_C", "rope0_S", "rope2_C", "rope2_S",
              "xpq_C", "xpq_S", "xpk_C", "xpk_S"}


class Prog:
    def __init__(self, nc):
        self.nc = nc
        self.stream = {e: [] for e in ENGINES}
        self.dmanames = [f"dma{i}" for i in range(NDMA)]
        self.swnames = [f"sw{i}" for i in range(NSW)]
        self.semnames = ["e_" + e for e in ENGINES] + self.dmanames + self.swnames
        self.counts = {s: 0 for s in self.semnames}
        self.hw = {e: {s: 0 for s in self.semnames} for e in ENGINES}
        self.sems = {}
        self.last_w = {}
        self.readers = {}
        self.pending = {e: {"r": [], "w": []} for e in ENGINES}
        self.n_ops = 0
        self._dma_rr = 0
        self._sw_next = 0

    def _plan_waits(self, eng, events):
        planned = []
        for (s, c) in events:
            if c > self.hw[eng][s]:
                self.hw[eng][s] = c
                planned.append((s, c))
        return planned

    def _dep_events(self, reads, writes):
        ev = []
        for r in reads:
            e = self.last_w.get(r)
            if e is not None:
                ev.append(e)
        for w in writes:
            e = self.last_w.get(w)
            if e is not None:
                ev.append(e)
            ev.extend(self.readers.get(w, ()))
        return ev

    def _record(self, reads, writes, event):
        for w in writes:
            self.last_w[w] = event
            self.readers[w] = []
        for r in reads:
            self.readers.setdefault(r, []).append(event)

    def do(self, eng, emit, reads=(), writes=(), signal=True, dma=False):
        reads = list(reads)
        writes = list(writes)
        dep = self._dep_events(reads, writes)
        self.n_ops += 1
        if dma:
            if eng == "gpsimd":
                assert self._sw_next < NSW, "SWDGE sem pool exhausted (add barrier)"
                s = self.swnames[self._sw_next]
                self._sw_next += 1
            else:
                s = self.dmanames[self._dma_rr % NDMA]
                self._dma_rr += 1
                dep = dep + [(s, self.counts[s])]
        planned = self._plan_waits(eng, dep)
        sems = self.sems
        if signal:
            s = s if dma else ("e_" + eng)
            delta = 16 if dma else 1
            self.counts[s] += delta
            event = (s, self.counts[s])
            allr = self.pending[eng]["r"] + reads
            allw = self.pending[eng]["w"] + writes
            self.pending[eng] = {"r": [], "w": []}
            self._record(allr, allw, event)

            def closure(h, planned=planned, emit=emit, sn=s, delta=delta):
                for (n, c) in planned:
                    h.wait_ge(sems[n], c)
                emit(h).then_inc(sems[sn], delta)
        else:
            event = None
            self.pending[eng]["r"].extend(reads)
            self.pending[eng]["w"].extend(writes)

            def closure(h, planned=planned, emit=emit):
                for (n, c) in planned:
                    h.wait_ge(sems[n], c)
                emit(h)
        self.stream[eng].append(closure)
        return event

    def dma(self, eng, out, in_, reads=(), writes=(), **kw):
        return self.do(eng, lambda h: h.dma_start(out=out, in_=in_, **kw),
                       reads=reads, writes=writes, dma=True)

    def wait_event(self, eng, event):
        planned = self._plan_waits(eng, [event])
        sems = self.sems
        if planned:
            def closure(h, planned=planned):
                for (n, c) in planned:
                    h.wait_ge(sems[n], c)
            self.stream[eng].append(closure)

    def barrier(self):
        sems = self.sems
        dmaev = [(s, self.counts[s]) for s in self.dmanames if self.counts[s] > 0]
        dmaev += [(s, 16) for s in self.swnames[:self._sw_next]]
        for e in ENGINES:
            self.wait_event(e, ("e_" + e, self.counts["e_" + e]))
            for ev in dmaev:
                self.wait_event(e, ev)
        arr = [self.do(e, lambda h: h.nop()) for e in ENGINES]
        for e in ENGINES:
            for ev in arr:
                self.wait_event(e, ev)
        self.last_w = {}
        self.readers = {}

    def build(self, stack):
        nc = self.nc
        for s in self.semnames:
            self.sems[s] = stack.enter_context(nc.semaphore(s))
        block = stack.enter_context(nc.Block())
        for e in ENGINES:
            ops = self.stream[e]

            def body(h, ops=ops):
                for f in ops:
                    f(h)
            getattr(block, e)(body)



def _col128(v):
    n = v.shape[0]
    assert n % 128 == 0
    return np.ascontiguousarray(v.reshape(n // 128, 128).T.astype(np.float32))


def host_constants():
    c = {}
    c["ones_col"] = np.ones((128, 1), np.float32)
    c["ones_row"] = np.ones((1, 128), np.float32)
    c["ident_bf"] = np.eye(128, dtype=BF)
    c["top64_row"] = np.concatenate(
        [np.ones((1, 64), np.float32), np.zeros((1, 64), np.float32)], axis=1)
    c["bot64_row"] = np.concatenate(
        [np.zeros((1, 64), np.float32), np.ones((1, 64), np.float32)], axis=1)
    c["neg_big"] = np.full((1, 8), -3.0e38, np.float32)
    c["eps_col"] = np.full((128, 1), 1.0e-5, np.float32)
    c["zeros8"] = np.zeros((1, 8), np.float32)
    hs = 64
    pos = np.arange(T, dtype=np.float64)[:, None]
    i = np.arange(1, hs // 2 + 1, dtype=np.float64)[None]
    theta = 1.0 / 10000.0 ** (2.0 * (i - 1.0) / hs)
    ang = pos * theta
    cos = np.cos(ang).T
    sin = np.sin(ang).T
    c["rope0_C"] = np.tile(np.repeat(cos, 2, axis=0), (2, 1)).astype(BF)
    c["rope0_S"] = np.tile(np.repeat(sin, 2, axis=0), (2, 1)).astype(BF)
    M0 = np.zeros((64, 64), np.float64)
    for k in range(32):
        M0[2 * k, 2 * k + 1] = -1.0
        M0[2 * k + 1, 2 * k] = 1.0
    R0 = np.zeros((128, 128), np.float64)
    R0[:64, :64] = M0
    R0[64:, 64:] = M0
    c["rope0_Rt"] = np.ascontiguousarray(R0.T).astype(BF)
    ii = np.arange(128)[:, None]
    jj = np.arange(128)[None, :]
    c["mask_add_qj"] = np.where(jj > ii, np.float32(NEG), np.float32(0)).astype(np.float32)
    c["mask_add_fl"] = np.where(jj < ii, np.float32(NEG), np.float32(0)).astype(np.float32)
    c["mask_keep_jq"] = (jj >= ii).astype(BF)
    half = 32
    inv2 = 1.0 / 10000.0 ** (np.arange(0, half, dtype=np.float64) / half)
    fr2 = np.arange(T, dtype=np.float64)[:, None] * inv2[None, :]
    cos2, sin2 = np.cos(fr2).T, np.sin(fr2).T
    C2 = np.concatenate([cos2, cos2], axis=0)
    S2 = np.concatenate([sin2, sin2], axis=0)
    c["rope2_C"] = np.tile(C2, (2, 1)).astype(BF)
    c["rope2_S"] = np.tile(S2, (2, 1)).astype(BF)
    M2 = np.zeros((64, 64), np.float64)
    for k in range(32):
        M2[k, 32 + k] = -1.0
        M2[32 + k, k] = 1.0
    R2 = np.zeros((128, 128), np.float64)
    R2[:64, :64] = M2
    R2[64:, 64:] = M2
    c["rope2_Rt"] = np.ascontiguousarray(R2.T).astype(BF)
    DH, WIN = 64, 512
    hi = np.arange(0, DH, 2, dtype=np.float64)
    invf = 1.0 / 10000.0 ** (hi / DH)
    t_ = np.arange(T, dtype=np.float64)
    fr = t_[:, None] * invf[None, :]
    fr = np.concatenate([fr, fr], axis=-1)
    base = (hi + 0.4 * DH) / (1.4 * DH)
    power = (t_ - T // 2) / float(WIN // 2)
    scl = base[None, :] ** power[:, None]
    scl = np.concatenate([scl, scl], axis=-1)
    cosx, sinx = np.cos(fr), np.sin(fr)
    c["xpq_C"] = np.tile((cosx * scl).T, (2, 1)).astype(BF)
    c["xpq_S"] = np.tile((sinx * scl).T, (2, 1)).astype(BF)
    c["xpk_C"] = np.tile((cosx / scl).T, (2, 1)).astype(BF)
    c["xpk_S"] = np.tile((sinx / scl).T, (2, 1)).astype(BF)
    c["top64colb"] = np.concatenate([np.ones((64, 1), BF), np.zeros((64, 1), BF)])
    c["bot64colb"] = np.concatenate([np.zeros((64, 1), BF), np.ones((64, 1), BF)])
    c["ones1b"] = np.ones((1, 128), BF)
    c["ones2b"] = np.ones((2, 128), BF)
    c["ones_col_bf"] = np.ones((128, 1), BF)
    c["ident_f32"] = np.eye(128, dtype=np.float32)
    IND = np.zeros((8, 1024), BF)
    for e in range(8):
        IND[e, 128 * e:128 * e + 128] = 1
    c["IND8"] = IND
    return c


def host_weights(inp):
    w = {}
    perm0 = np.concatenate([np.arange(64) + 192 * h + 64 * t
                            for t in range(3) for h in range(16)])
    w["w0_qkv"] = np.ascontiguousarray(inp["L0_attn_W_qkv"][:, perm0])
    w["b0_qkv"] = _col128(inp["L0_attn_b_qkv"][perm0])
    w["w0_proj"] = np.ascontiguousarray(inp["L0_attn_W_proj"])
    w["b0_proj"] = _col128(inp["L0_attn_b_proj"])
    w["w0_WrT"] = np.ascontiguousarray(inp["L0_ffn_Wr"].T.astype(np.float32))
    w["w0_AT"] = np.ascontiguousarray(inp["L0_ffn_A"].reshape(512, 1024).T)
    w["w0_BwR"] = np.ascontiguousarray(
        inp["L0_ffn_Bw"].transpose(0, 2, 1).reshape(512, 1024))
    for nm in ("q1", "q2", "k1", "k2"):
        w[f"w1_{nm}T"] = np.ascontiguousarray(inp[f"L1_attn_{nm}_w"].T)
    w["w1_vT"] = np.ascontiguousarray(inp["L1_attn_v_w"].T)
    w["w1_cT"] = np.ascontiguousarray(inp["L1_attn_c_w"].T)
    w["b1_g"] = _col128(inp["L1_attn_bn_gamma"])
    w["b1_b"] = _col128(inp["L1_attn_bn_beta"])
    w["b1_subln"] = np.ascontiguousarray(
        inp["L1_attn_subln_w"].reshape(128, 1).astype(np.float32))
    for nm in ("lq1", "lk1", "lq2", "lk2"):
        w[f"b1_{nm}"] = np.ascontiguousarray(
            inp[f"L1_attn_{nm}"].reshape(1, 64).astype(np.float32))
    w["w1_Wg"] = _col128(inp["L1_ffn_Wg"][:, 0])
    for nm in ("Wq", "Wk", "Wv", "Wo"):
        w[f"w1_{nm}"] = np.ascontiguousarray(inp[f"L1_ffn_{nm}"])
    w["w2a_qkvT"] = np.ascontiguousarray(inp["L2_attn_w_qkv"].T)
    w["w2a_outT"] = np.ascontiguousarray(inp["L2_attn_w_out"].T)
    w["b2a_out"] = _col128(inp["L2_attn_b_out"])
    w["ln2a_g"] = _col128(inp["L2_attn_ln_g"])
    w["ln2a_b"] = _col128(inp["L2_attn_ln_b"])
    HID = 2730
    HPAD = 2816
    for l in range(2):
        w[f"w2f_qkv{l}"] = np.ascontiguousarray(inp["L2_ffn_wqkv"][l])
        w[f"w2f_wo{l}"] = np.ascontiguousarray(inp["L2_ffn_wo"][l])
        w1 = inp["L2_ffn_w1"][l]
        w1a = np.zeros((1024, HPAD), np.float32); w1a[:, :HID] = w1[:, :HID]
        w1g = np.zeros((1024, HPAD), np.float32); w1g[:, :HID] = w1[:, HID:]
        w[f"w2f_w1a{l}"] = w1a
        w[f"w2f_w1g{l}"] = w1g
        w2p = np.zeros((HPAD, 1024), np.float32); w2p[:HID] = inp["L2_ffn_w2"][l]
        w[f"w2f_w2{l}"] = w2p
        b1 = inp["L2_ffn_b1"][l]
        b1a = np.zeros(HPAD, np.float32); b1a[:HID] = b1[:HID]
        b1g = np.zeros(HPAD, np.float32); b1g[:HID] = b1[HID:]
        w[f"b2f_b1a{l}"] = _col128(b1a)
        w[f"b2f_b1g{l}"] = _col128(b1g)
        w[f"b2f_b2{l}"] = _col128(inp["L2_ffn_b2"][l])
        w[f"b2f_lg{l}"] = _col128(inp["L2_ffn_ln_g"][l])
        w[f"b2f_lb{l}"] = _col128(inp["L2_ffn_ln_b"][l])
        w[f"b2f_qs{l}"] = np.ascontiguousarray(
            np.tile(inp["L2_ffn_q_scale"][l], 2).reshape(128, 1).astype(np.float32))
        w[f"b2f_ks{l}"] = np.ascontiguousarray(
            np.tile(inp["L2_ffn_k_scale"][l], 2).reshape(128, 1).astype(np.float32))
    for nm in ("Wq", "Wk", "Wv", "Wo"):
        w[f"w3_{nm.lower()}T"] = np.ascontiguousarray(inp[f"L3_attn_{nm}"].T)
    for nm in ("bq", "bk", "bv"):
        w[f"b3_{nm}"] = np.ascontiguousarray(
            inp[f"L3_attn_{nm}"].reshape(1, 1024).astype(np.float32))
    w["b3_bo"] = _col128(inp["L3_attn_bo"])
    w["b3_beta"] = np.ascontiguousarray(
        inp["L3_attn_beta"].reshape(1, 1).astype(np.float32))
    w["w3_wg1"] = np.ascontiguousarray(inp["L3_ffn_wg1"])
    w["b3_bg1"] = _col128(inp["L3_ffn_bg1"])
    w["w3_wg2"] = np.ascontiguousarray(inp["L3_ffn_wg2"].astype(np.float32))
    for e in range(8):
        w[f"w3_win{e}"] = np.ascontiguousarray(inp["L3_ffn_w_in"][e])
        w[f"w3_wout{e}"] = np.ascontiguousarray(inp["L3_ffn_w_out"][e])
    return w


BLOCK_WEIGHTS = {
    0: ["w0_qkv", "b0_qkv", "w0_proj", "b0_proj", "w0_WrT", "w0_AT", "w0_BwR"],
    1: ["w1_q1T", "w1_q2T", "w1_k1T", "w1_k2T", "w1_vT", "w1_cT", "b1_g", "b1_b",
        "b1_subln", "b1_lq1", "b1_lk1", "b1_lq2", "b1_lk2", "w1_Wg", "w1_Wq",
        "w1_Wk", "w1_Wv", "w1_Wo"],
    2: ["w2a_qkvT", "w2a_outT", "b2a_out", "ln2a_g", "ln2a_b"]
       + [f"{nm}{l}" for l in range(2) for nm in
          ("w2f_qkv", "w2f_wo", "w2f_w1a", "w2f_w1g", "w2f_w2", "b2f_b1a",
           "b2f_b1g", "b2f_b2", "b2f_lg", "b2f_lb", "b2f_qs", "b2f_ks")],
    3: (["w3_wqT", "w3_wkT", "w3_wvT", "w3_woT", "b3_bq", "b3_bk", "b3_bv",
         "b3_bo", "b3_beta", "w3_wg1", "b3_bg1", "w3_wg2"]
        + [f"w3_win{e}" for e in range(8)] + [f"w3_wout{e}" for e in range(8)]),
}



class MB:
    def __init__(self, consts, weights):
        self.nc = bass.Bass("TRN2", target_bir_lowering=False)
        self.p = Prog(self.nc)
        self.stack = contextlib.ExitStack()
        self.dram = {}
        self.sb = {}
        self.pb = {}
        self.consts = consts
        self.weights = weights
        self._scoped = []
        self._uid = 0

    def uid(self):
        self._uid += 1
        return self._uid

    def din(self, name, shape, dtype=F32):
        ap = self.nc.dram_tensor(name, list(shape), dtype, kind="ExternalInput").ap()
        self.dram[name] = ap
        return ap

    def dout(self, name, shape, dtype=F32):
        ap = self.nc.dram_tensor(name, list(shape), dtype, kind="ExternalOutput").ap()
        self.dram[name] = ap
        return ap

    def dtmp(self, name, shape, dtype=BF16):
        ap = self.nc.dram_tensor(name, list(shape), dtype, kind="Internal").ap()
        self.dram[name] = ap
        return ap

    def sbuf(self, name, shape, dtype):
        h = self.stack.enter_context(self.nc.sbuf_tensor(name, list(shape), dtype))
        self.sb[name] = h
        return h

    def sbuf_scoped(self, name, shape, dtype):
        cm = self.nc.sbuf_tensor(name, list(shape), dtype)
        h = cm.__enter__()
        self._scoped.append(cm)
        self.sb[name] = h
        return h

    def end_stage(self):
        if self._scoped:
            self.p.barrier()
            for cm in reversed(self._scoped):
                cm.__exit__(None, None, None)
            self._scoped = []

    def setup(self, used_weights, block_ids=()):
        nc = self.nc
        for i in range(7):
            self.pb[i] = self.stack.enter_context(
                nc.psum_tensor(f"pb{i}", [128, 512], F32))
        self.pb["t"] = self.stack.enter_context(
            nc.psum_tensor("pbt", [128, 1024], BF16))
        for name, arr in self.consts.items():
            dt = F32 if arr.dtype == np.float32 else BF16
            ap = self.din("c_" + name, arr.shape, dt)
            if name in BIG_CONSTS:
                continue
            t = self.sbuf("k_" + name, arr.shape, dt)
            self.p.dma("sync", t[:], ap[:, :], writes=[("c", name)])
        self.wbf = {}
        for name in used_weights:
            arr = self.weights[name]
            self.din(name, arr.shape, F32)
        for b in block_ids:
            BLOCK_PRE.get(b, lambda mb: None)(self)
        self._x_cms = []
        self._x_gen = 0
        self.x = []
        self._alloc_x()

    def _alloc_x(self):
        self._x_gen += 1
        self._x_cms = [self.nc.sbuf_tensor(f"x{k}g{self._x_gen}",
                                           [128, NSEQ * T], F32)
                       for k in range(KC)]
        self.x = [cm.__enter__() for cm in self._x_cms]

    def free_x(self):
        assert not self._scoped
        self.p.barrier()
        for cm in reversed(self._x_cms):
            cm.__exit__(None, None, None)
        self._x_cms = []

    def realloc_x(self):
        self._alloc_x()

    def stage_weight_bf16(self, name):
        if name in self.wbf:
            return self.wbf[name]
        src = self.dram[name]
        ap = self.dtmp(name + "_bf", src.tensor.shape, BF16)
        self.p.dma("gpsimd", ap[tuple(slice(None) for _ in src.tensor.shape)],
                   src[tuple(slice(None) for _ in src.tensor.shape)],
                   writes=[("wbf", name)])
        self.wbf[name] = ap
        return ap

    def cs(self, name):
        return self.sb["k_" + name]

    def load_const_scoped(self, name):
        arr = self.consts[name]
        dt = F32 if arr.dtype == np.float32 else BF16
        t = self.sbuf_scoped("k_" + name + str(self.uid()), list(arr.shape), dt)
        self.p.dma("sync", t[:], self.dram["c_" + name][:, :],
                   writes=[("c", name)])
        return t

    def load_x(self):
        ap = self.din("x_in", [KC, 128, NSEQ * T], F32)
        for k in range(KC):
            self.p.dma("sync", self.x[k][:], ap[k, :, :], writes=[("x", k)])

    def store_x(self):
        ap = self.dout("x_out", [KC, 128, NSEQ * T], F32)
        evs = [self.p.dma("sync", ap[k, :, :], self.x[k][:], reads=[("x", k)])
               for k in range(KC)]
        for e in evs:
            self.p.wait_event("sync", e)

    def mm(self, out_ap, lhsT, rhs, res_w, reads, first=True, last=True,
           tile_position=None):
        def em(h):
            return h.matmul(out_ap, lhsT, rhs, start=first, stop=last,
                            tile_position=tile_position)
        return self.p.do("tensor", em, reads=list(reads),
                         writes=[res_w], signal=True)

    def tt(self, out, a, bb, op, res_w, reads, eng="vector"):
        self.p.do(eng, lambda h: h.tensor_tensor(out, a, bb, op=op),
                  reads=list(reads), writes=[res_w])

    def ts(self, out, in0, s1, s2, op0, op1, res_w, reads, eng="vector"):
        if s2 is None:
            self.p.do(eng, lambda h: h.tensor_scalar(out, in0, s1, None, op0=op0),
                      reads=list(reads), writes=[res_w])
        else:
            self.p.do(eng, lambda h: h.tensor_scalar(out, in0, s1, s2, op0=op0,
                                                     op1=op1),
                      reads=list(reads), writes=[res_w])

    def stt(self, out, in0, scalar, in1, op0, op1, res_w, reads):
        self.p.do("vector",
                  lambda h: h.scalar_tensor_tensor(out, in0, scalar, in1,
                                                   op0=op0, op1=op1),
                  reads=list(reads), writes=[res_w])

    def act(self, out, in_, func, res_w, reads, bias=None, scale=None, accum=None,
            accum_res=None):
        def em(h):
            kw = {}
            if bias is not None:
                kw["bias"] = bias
            if scale is not None:
                kw["scale"] = scale
            if accum is not None:
                kw["accum_out"] = accum
            return h.activation(out=out, in_=in_, func=func, **kw)
        w = [res_w] + ([accum_res] if accum_res is not None else [])
        self.p.do("scalar", em, reads=list(reads), writes=w)

    def red(self, out, in_, op, res_w, reads, negate=False):
        self.p.do("vector",
                  lambda h: h.tensor_reduce(out, in_, axis=AX.X, op=op,
                                            negate=negate),
                  reads=list(reads), writes=[res_w])

    def copy(self, out, in_, res_w, reads, eng="vector"):
        self.p.do(eng, lambda h: h.tensor_copy(out, in_),
                  reads=list(reads), writes=[res_w])

    def memset(self, ap, val, res_w, eng="vector"):
        self.p.do(eng, lambda h: h.memset(ap, val), writes=[res_w])

    def recip(self, out, in_, res_w, reads):
        self.p.do("vector", lambda h: h.reciprocal(out, in_),
                  reads=list(reads), writes=[res_w])

    def pe_transpose(self, psum_ap, in_ap, res_w, reads):
        ident = self.cs("ident_bf")
        self.p.do("tensor", lambda h: h.transpose(psum_ap, in_ap, ident[:]),
                  reads=list(reads) + [("c", "ident_bf")], writes=[res_w])

    def ln_fm(self, seq, xn_sink, add_xn_into_x=False):
        c0 = seq * T
        ones = self.cs("ones_col")
        onesr = self.cs("ones_row")
        u = self.uid()
        xsq = [self.sbuf_scoped(f"lnsq{u}_{i}", [128, 512], F32) for i in range(2)]
        rows = self.sbuf_scoped(f"lnrow{u}", [1, 512 * 3], F32)
        ab = self.sbuf_scoped(f"lnab{u}", [128, 512], F32)
        bb = self.sbuf_scoped(f"lnbb{u}", [128, 512], F32)
        t1 = self.sbuf_scoped(f"lnt1{u}", [128, 512], F32)
        R = lambda j: rows[0:1, 512 * j:512 * (j + 1)]
        for tc in range(4):
            cols = slice(c0 + tc * 512, c0 + tc * 512 + 512)
            xres = [("x", k) for k in range(KC)]
            for k in range(KC):
                self.mm(self.pb[0][0:1, :], ones[:], self.x[k][:, cols],
                        ("pb", 0), [("x", k), ("c", "ones_col")],
                        first=(k == 0), last=(k == KC - 1))
            for k in range(KC):
                b = xsq[k % 2]
                self.act(b[:], self.x[k][:, cols], AF.Square,
                         ("lnsq", u, k % 2), [("x", k)])
                self.mm(self.pb[1][0:1, :], ones[:], b[:],
                        ("pb", 1), [("lnsq", u, k % 2), ("c", "ones_col")],
                        first=(k == 0), last=(k == KC - 1))
            self.ts(R(0), self.pb[0][0:1, :], 1.0 / D, None, ALU.mult, None,
                    ("lnr", u, 0), [("pb", 0)])
            self.ts(R(1), self.pb[1][0:1, :], 1.0 / D, None, ALU.mult, None,
                    ("lnr", u, 1), [("pb", 1)])
            self.act(R(2), R(0), AF.Square, ("lnr", u, 2), [("lnr", u, 0)])
            self.tt(R(1), R(1), R(2), ALU.subtract, ("lnr", u, 1),
                    [("lnr", u, 1), ("lnr", u, 2)])
            self.act(R(1), R(1), AF.Sqrt, ("lnr", u, 1),
                     [("lnr", u, 1), ("c", "eps_col")],
                     bias=self.cs("eps_col")[0:1, 0:1], scale=1.0)
            self.recip(R(1), R(1), ("lnr", u, 1), [("lnr", u, 1)])
            self.stt(R(2), R(0), -1.0, R(1), ALU.mult, ALU.mult,
                     ("lnr", u, 2), [("lnr", u, 0), ("lnr", u, 1)])
            self.mm(self.pb[2][:, :], onesr[:], R(1), ("pb", 2),
                    [("lnr", u, 1), ("c", "ones_row")])
            self.mm(self.pb[3][:, :], onesr[:], R(2), ("pb", 3),
                    [("lnr", u, 2), ("c", "ones_row")])
            self.copy(ab[:], self.pb[2][:, :], ("lnab", u), [("pb", 2)])
            self.copy(bb[:], self.pb[3][:, :], ("lnbb", u), [("pb", 3)])
            for k in range(KC):
                self.tt(t1[:], self.x[k][:, cols], ab[:], ALU.mult,
                        ("lnt1", u), [("x", k), ("lnab", u)])
                sink_ap, sink_res, post = xn_sink(k, tc)
                self.tt(sink_ap, t1[:], bb[:], ALU.add, sink_res,
                        [("lnt1", u), ("lnbb", u)])
                post()
                if add_xn_into_x:
                    self.tt(self.x[k][:, cols], self.x[k][:, cols], t1[:],
                            ALU.add, ("x", k), [("x", k), ("lnt1", u)])
                    self.tt(self.x[k][:, cols], self.x[k][:, cols], bb[:],
                            ALU.add, ("x", k), [("x", k), ("lnbb", u)])



CUT = None


def pre_block0(mb: MB):
    p = mb.p
    t = mb.sbuf("b0c", [128, 24], F32)
    p.dma("sync", t[:], mb.dram["b0_qkv"][:, :], writes=[("b0c",)])
    t2 = mb.sbuf("b0pc", [128, 8], F32)
    p.dma("sync", t2[:], mb.dram["b0_proj"][:, :], writes=[("b0pc",)])
    for s in range(NSEQ):
        mb.sbuf(f"b0cls{s}", [128, KC], BF16)


def emit_block0(mb: MB):
    p = mb.p
    y_dram = mb.dtmp("b0_y", [NSEQ, KC, 128, T], BF16)
    xn_dram = mb.dtmp("b0_xn", [NSEQ, KC, 128, T], BF16)
    wqkv_bf = mb.stage_weight_bf16("w0_qkv")
    wproj_bf = mb.stage_weight_bf16("w0_proj")
    wrt_bf = mb.stage_weight_bf16("w0_WrT")
    wat_bf = mb.stage_weight_bf16("w0_AT")
    wbw_bf = mb.stage_weight_bf16("w0_BwR")
    clss = [mb.sb[f"b0cls{s}"] for s in range(NSEQ)]
    for s in range(NSEQ):
        c0 = s * T
        uL = mb.uid()
        xst = [mb.sbuf_scoped(f"b0xst{uL}_{i}", [128, 512], BF16) for i in range(2)]
        cls = clss[s]

        def xn_sink(kc, tc, s=s, uL=uL):
            i = (kc + tc) % 2
            ap = xst[i][:, :]
            res = ("xst", uL, i)

            def post():
                p.dma("sync", xn_dram[s, kc, :, tc * 512:(tc + 1) * 512],
                      xst[i][:, :], reads=[res], writes=[("xnd", s, kc)])
                if tc == 0:
                    mb.copy(cls[:, kc:kc + 1], xst[i][:, 0:1], ("cls", uL),
                            [res])
            return ap, res, post
        mb.ln_fm(s, xn_sink, add_xn_into_x=True)
        mb.end_stage()
        if CUT == "ln":
            continue

        wqkv = wqkv_bf
        u = mb.uid()
        wq = [mb.sbuf_scoped(f"b0wq{u}_0", [128, KC, 128], BF16)]
        wk = [mb.sbuf_scoped(f"b0wk{u}_0", [128, KC, 128], BF16)]
        wv = [mb.sbuf_scoped(f"b0wv{u}_0", [128, KC, 128], BF16)]
        xnb = [mb.sbuf_scoped(f"b0xnb{u}_{i}", [128, 1024], BF16) for i in range(2)]
        qp = mb.sbuf_scoped(f"b0qp{u}", [128, 1024], BF16)
        kp = mb.sbuf_scoped(f"b0kp{u}", [128, 1024], BF16)
        vp = mb.sbuf_scoped(f"b0vp{u}", [128, 16, 128], BF16)
        qa = [mb.sbuf_scoped(f"b0qa{u}_{i}", [67, T], BF16) for i in range(2)]
        ka = [mb.sbuf_scoped(f"b0ka{u}_{i}", [67, T], BF16) for i in range(2)]
        pt = mb.sbuf_scoped(f"b0pt{u}", [128, T], BF16)
        rtmp = mb.sbuf_scoped(f"b0rt{u}", [128, 512], BF16)
        ystage = mb.sbuf_scoped(f"b0ys{u}", [128, T], BF16)
        stats = mb.sbuf_scoped(f"b0st{u}", [128, 16], F32)
        lrow = mb.sbuf_scoped(f"b0lr{u}", [128, 16 * 4], F32)
        lbf = mb.sbuf_scoped(f"b0lbf{u}", [128, 32], BF16)
        ltr = mb.sbuf_scoped(f"b0ltr{u}", [16, 128], BF16)
        sdiag = mb.sbuf_scoped(f"b0sd{u}", [128, 128], F32)
        trash = mb.sbuf_scoped(f"b0tr{u}", [128, 512], BF16)
        biasq = mb.sbuf_scoped(f"b0bq{u}", [128, 2], F32)
        ropeC_t = mb.load_const_scoped("rope0_C")
        ropeS_t = mb.load_const_scoped("rope0_S")
        bcol = mb.sb["b0c"]

        def load_pair_w(pair, buf):
            for kind, tile, off in (("q", wq[buf], 0), ("k", wk[buf], 1024),
                                    ("v", wv[buf], 2048)):
                src = wqkv[:, off + 128 * pair: off + 128 * pair + 128]
                src = src.rearrange("(kc p) c -> p kc c", p=128)
                p.dma("scalar", tile[:], src, reads=[("wbf", "w0_qkv")],
                      writes=[("w0s", kind, buf)])

        for pair in range(8):
            buf = 0
            load_pair_w(pair, buf)
            mb.ts(biasq[:, 0:1], bcol[:, pair:pair + 1], 0.125, None, ALU.mult,
                  None, ("bq", u), [("b0c",)])
            ropeC = ropeC_t
            ropeS = ropeS_t
            ropeR = mb.cs("rope0_Rt")
            for half in range(2):
                hcols = slice(c0 + half * 1024, c0 + half * 1024 + 1024)
                for k in range(KC):
                    xb = xnb[k % 2]
                    p.dma("sync", xb[:],
                          xn_dram[s, k, :, half * 1024:half * 1024 + 1024],
                          reads=[("xnd", s, k)], writes=[("xnb", u, k % 2)])
                    for t2_ in range(2):
                        lc = slice(t2_ * 512, t2_ * 512 + 512)
                        mb.mm(mb.pb[t2_][:, :], wq[buf][:, k, :], xb[:, lc],
                              ("pb", t2_), [("w0s", "q", buf), ("xnb", u, k % 2)],
                              first=(k == 0), last=(k == KC - 1))
                        mb.mm(mb.pb[2 + t2_][:, :], wk[buf][:, k, :], xb[:, lc],
                              ("pb", 2 + t2_), [("w0s", "k", buf), ("xnb", u, k % 2)],
                              first=(k == 0), last=(k == KC - 1))
                    for tt_ in range(8 if CUT != "projqk" else 0):
                        trows = slice(tt_ * 128, tt_ * 128 + 128)
                        bank = mb.pb[4 + tt_ // 4]
                        bslc = bank[:, (tt_ % 4) * 128:(tt_ % 4) * 128 + 128]
                        mb.mm(bslc, xb[:, trows], wv[buf][:, k, :],
                              ("pb", 4 + tt_ // 4),
                              [("xnb", u, k % 2), ("w0s", "v", buf)],
                              first=(k == 0 and tt_ % 4 == 0),
                              last=(k == KC - 1 and tt_ % 4 == 3))
                for t2_ in range(2):
                    lc = slice(t2_ * 512, t2_ * 512 + 512)
                    mb.act(qp[:, lc], mb.pb[t2_][:, :], AF.Identity, ("qp", u),
                           [("pb", t2_), ("bq", u)], bias=biasq[:, 0:1],
                           scale=0.125)
                    mb.act(kp[:, lc], mb.pb[2 + t2_][:, :], AF.Identity, ("kp", u),
                           [("pb", 2 + t2_), ("b0c",)],
                           bias=bcol[:, 8 + pair:9 + pair], scale=1.0)
                for tt_ in range(8 if CUT != "projqk" else 0):
                    bank = mb.pb[4 + tt_ // 4]
                    bslc = bank[:, (tt_ % 4) * 128:(tt_ % 4) * 128 + 128]
                    mb.act(vp[:, half * 8 + tt_, :], bslc, AF.Identity, ("vp", u),
                           [("pb", 4 + tt_ // 4)])
                for which, src, srcres, augs in (
                        () if CUT in ("projqk", "projv") else
                        (("q", qp, ("qp", u), qa), ("k", kp, ("kp", u), ka))):
                    for t2_ in range(2):
                        lc = slice(t2_ * 512, t2_ * 512 + 512)
                        gcols = slice(half * 1024 + t2_ * 512,
                                      half * 1024 + t2_ * 512 + 512)
                        mb.mm(mb.pb[6][:, :], ropeR[:], src[:, lc], ("pb", 6),
                              [srcres, ("c", "rope0_Rt")])
                        mb.tt(rtmp[:, :], src[:, lc],
                              ropeC[:, gcols], ALU.mult,
                              ("rtmp", u), [srcres, ("c", "rope0_C")])
                        mb.tt(trash[:, :], mb.pb[6][:, :],
                              ropeS[:, gcols], ALU.mult,
                              ("trash", u), [("pb", 6), ("c", "rope0_S")])
                        for hh in range(2):
                            rr = slice(64 * hh, 64 * hh + 64)
                            mb.tt(augs[hh][0:64, gcols], rtmp[rr, :],
                                  trash[rr, :], ALU.add, ("aug", which, u, hh),
                                  [("rtmp", u), ("trash", u)])
            if (CUT is None) and s == 0 and pair == 0 and "b0_dbg_q" not in mb.dram:
                dq = mb.dtmp("b0_dbg_q", [64, T], BF16)
                dk = mb.dtmp("b0_dbg_k", [64, T], BF16)
                dv = mb.dtmp("b0_dbg_v", [128, 16, 128], BF16)
                mb.dtmp("b0_dbg_L", [2, T], BF16)
                p.dma("sync", dq[:, :], qa[0][0:64, :], reads=[("aug", "q", u, 0)])
                p.dma("sync", dk[:, :], ka[0][0:64, :], reads=[("aug", "k", u, 0)])
                p.dma("sync", dv[:, :, :], vp[:, :, :], reads=[("vp", u)])
            for hh in range(2 if CUT not in ("proj", "pass1") else 0):
                h_aug_q, h_aug_k = qa[hh], ka[hh]
                qres, kres = ("aug", "q", u, hh), ("aug", "k", u, hh)
                mb.memset(h_aug_k[64:67, :], 1.0, kres)
                for qt in range(16):
                    wall = (qt + 1) * 128
                    nch = cdiv(wall, 512)
                    for cch in range(nch):
                        off = cch * 512
                        w = min(512, wall - off)
                        pbs = mb.pb[cch % 2]
                        mb.mm(pbs[:, 0:w], h_aug_q[0:64, qt * 128:qt * 128 + 128],
                              h_aug_k[0:64, off:off + w], ("pb", cch % 2),
                              [qres, kres])
                        if off + w == wall:
                            mb.tt(pbs[:, w - 128:w], pbs[:, w - 128:w],
                                  mb.cs("mask_add_qj")[:, :], ALU.add,
                                  ("pb", cch % 2),
                                  [("pb", cch % 2), ("c", "mask_add_qj")])
                        mb.red(stats[:, cch:cch + 1], pbs[:, 0:w], ALU.max,
                               ("st", u, cch), [("pb", cch % 2)], negate=True)
                        mb.act(trash[:, 0:w], pbs[:, 0:w], AF.Exp,
                               ("trash", u), [("pb", cch % 2), ("st", u, cch)],
                               bias=stats[:, cch:cch + 1],
                               accum=stats[:, 4 + cch:5 + cch],
                               accum_res=("st", u, 4 + cch))
                    strs = [("st", u, i) for i in (0, 1, 2, 3, 4, 5, 6, 7, 8, 9)]
                    if nch == 1:
                        mb.act(stats[:, 12:13], stats[:, 4:5], AF.Ln,
                               ("st", u, 12), strs)
                        mb.tt(lrow[:, qt:qt + 1], stats[:, 0:1], stats[:, 12:13],
                              ALU.subtract, ("lrow", u, qt),
                              [("st", u, 0), ("st", u, 12)])
                    else:
                        mb.red(stats[:, 8:9], stats[:, 0:nch], ALU.min,
                               ("st", u, 8), strs)
                        mb.act(stats[:, 0:nch], stats[:, 0:nch], AF.Exp,
                               ("st", u, 0), strs + [("st", u, 8)],
                               scale=-1.0, bias=stats[:, 8:9])
                        mb.tt(stats[:, 4:4 + nch], stats[:, 4:4 + nch],
                              stats[:, 0:nch], ALU.mult, ("st", u, 4), strs)
                        mb.red(stats[:, 9:10], stats[:, 4:4 + nch], ALU.add,
                               ("st", u, 9), strs)
                        mb.act(stats[:, 12:13], stats[:, 9:10], AF.Ln,
                               ("st", u, 12), strs)
                        mb.tt(lrow[:, qt:qt + 1], stats[:, 8:9], stats[:, 12:13],
                              ALU.subtract, ("lrow", u, qt),
                              [("st", u, 8), ("st", u, 12)])
                lres = [("lrow", u, qt) for qt in range(16)]
                mb.copy(lbf[:, 0:16], lrow[:, 0:16], ("lbf", u), lres)
                mb.copy(lrow[:, 16:32], lbf[:, 0:16], ("lrow2", u), [("lbf", u)])
                mb.tt(lrow[:, 32:48], lrow[:, 0:16], lrow[:, 16:32],
                      ALU.subtract, ("lrow3", u), [("lrow2", u)] + lres)
                mb.copy(lbf[:, 16:32], lrow[:, 32:48], ("lbf2", u), [("lrow3", u)])
                mb.pe_transpose(mb.pb["t"][0:16, 0:128], lbf[:, 0:16],
                                ("pbt",), [("lbf", u)])
                mb.copy(ltr[:, :], mb.pb["t"][0:16, 0:128], ("ltr", u), [("pbt",)])
                p.dma("sync", h_aug_q[64:65, :], ltr[:, :], reads=[("ltr", u)],
                      writes=[qres])
                mb.pe_transpose(mb.pb["t"][0:16, 0:128], lbf[:, 16:32],
                                ("pbt",), [("lbf2", u)])
                mb.copy(ltr[:, :], mb.pb["t"][0:16, 0:128], ("ltr", u), [("pbt",)])
                p.dma("sync", h_aug_q[65:66, :], ltr[:, :], reads=[("ltr", u)],
                      writes=[qres])
                if CUT is None and s == 0 and pair == 0 and hh == 0:
                    p.dma("sync", mb.dram["b0_dbg_L"][:, :], h_aug_q[64:66, :],
                          reads=[qres])
                yb = {0: 3, 1: 4, 2: 5, 3: 6}
                for jt in range(16):
                    chunks = [(jt * 128, 128)]
                    off = jt * 128 + 128
                    while off < T:
                        w = min(512, T - off)
                        chunks.append((off, w))
                        off += w
                    for ci, (off, w) in enumerate(chunks):
                        pbs = mb.pb[ci % 2]
                        mb.mm(pbs[:, 0:w], h_aug_k[0:66, jt * 128:jt * 128 + 128],
                              h_aug_q[0:66, off:off + w], ("pb", ci % 2),
                              [kres, qres])
                        loc = off - jt * 128
                        if ci == 0:
                            mb.ts(sdiag[:, :], pbs[:, 0:128], 80.0, None,
                                  ALU.min, None, ("sdiag", u), [("pb", ci % 2)])
                            mb.act(pt[:, 0:128], sdiag[:, :], AF.Exp,
                                   ("pt", u, 0), [("sdiag", u)])
                            mb.tt(pt[:, 0:128], pt[:, 0:128],
                                  mb.cs("mask_keep_jq")[:, :], ALU.mult,
                                  ("pt", u, 0), [("pt", u, 0), ("c", "mask_keep_jq")])
                        else:
                            mb.act(pt[:, loc:loc + w], pbs[:, 0:w], AF.Exp,
                                   ("pt", u, ci), [("pb", ci % 2)])
                    ptres = [("pt", u, i) for i in range(len(chunks))]
                    for ci4 in range(jt // 4, 4):
                        qs = max(512 * ci4, jt * 128)
                        qe = 512 * ci4 + 512
                        loc = qs - jt * 128
                        bank = mb.pb[yb[ci4]]
                        boff = qs - 512 * ci4
                        mb.mm(bank[0:64, boff:boff + (qe - qs)],
                              vp[:, jt, 64 * hh:64 * hh + 64],
                              pt[:, loc:loc + (qe - qs)],
                              ("pb", yb[ci4]),
                              [("vp", u)] + ptres,
                              first=(jt == 0), last=(jt == min(15, 4 * ci4 + 3)))
                for ci4 in range(4):
                    mb.act(ystage[64 * hh:64 * hh + 64, ci4 * 512:(ci4 + 1) * 512],
                           mb.pb[yb[ci4]][0:64, :], AF.Identity, ("ys", u, hh),
                           [("pb", yb[ci4]), ("b0c",)],
                           bias=bcol[64 * hh:64 * hh + 64, 16 + pair:17 + pair],
                           scale=1.0)
            p.dma("sync", y_dram[s, pair, :, :], ystage[:, :],
                  reads=[("ys", u, 0), ("ys", u, 1)], writes=[("yd", s, pair)])
        mb.end_stage()
        if CUT in ("proj", "pass1", "attn"):
            continue

        u = mb.uid()
        wpk = mb.sbuf_scoped(f"b0wp{u}", [128, KC, 512], BF16)
        ybuf = [mb.sbuf_scoped(f"b0yb{u}_{i}", [128, 512], BF16) for i in range(2)]
        wproj = wproj_bf
        bpc = mb.sb["b0pc"]
        for ocg in range(2):
            src = wproj[:, ocg * 512:(ocg + 1) * 512]
            src = src.rearrange("(kc p) c -> p kc c", p=128)
            p.dma("scalar", wpk[:], src, reads=[("wbf", "w0_proj")],
                  writes=[("wp", u)])
            for tc in range(4):
                for k in range(KC):
                    yb_ = ybuf[k % 2]
                    p.dma("scalar", yb_[:], y_dram[s, k, :, tc * 512:(tc + 1) * 512],
                          reads=[("yd", s, k)], writes=[("ybuf", u, k % 2)])
                    for o4 in range(4):
                        mb.mm(mb.pb[o4][:, :], wpk[:, k, o4 * 128:(o4 + 1) * 128],
                              yb_[:], ("pb", o4), [("wp", u), ("ybuf", u, k % 2)],
                              first=(k == 0), last=(k == KC - 1))
                for o4 in range(4):
                    oc = ocg * 4 + o4
                    cols = slice(c0 + tc * 512, c0 + tc * 512 + 512)
                    mb.stt(mb.x[oc][:, cols], mb.pb[o4][:, :],
                           bpc[:, oc:oc + 1], mb.x[oc][:, cols],
                           ALU.add, ALU.add, ("x", oc),
                           [("pb", o4), ("b0pc",), ("x", oc)])
        mb.end_stage()
        if CUT == "outproj":
            continue

        u = mb.uid()
        wrt = mb.sbuf_scoped(f"b0wr{u}", [128, KC, 8], BF16)
        src = wrt_bf[:, :].rearrange("(kc p) c -> p kc c", p=128)
        p.dma("scalar", wrt[:], src, reads=[("wbf", "w0_WrT")],
              writes=[("wrt", u)])
        grow = mb.sbuf_scoped(f"b0gr{u}", [1, 64], F32)
        gcols = mb.sbuf_scoped(f"b0gc{u}", [128, 4], F32)
        G = lambda j: grow[0:1, 8 * j:8 * (j + 1)]
        for k in range(KC):
            mb.mm(mb.pb[0][0:1, 0:8], cls[:, k:k + 1], wrt[:, k, :], ("pb", 0),
                  [("cls", uL), ("wrt", u)], first=(k == 0), last=(k == KC - 1))
        mb.copy(G(0), mb.pb[0][0:1, 0:8], ("g", u, 0), [("pb", 0)])
        mb.red(G(1)[0:1, 0:1], G(0), ALU.max, ("g", u, 1), [("g", u, 0)],
               negate=True)
        mb.ts(G(2), G(0), G(1)[0:1, 0:1], None, ALU.add, None, ("g", u, 2),
              [("g", u, 0), ("g", u, 1)])
        mb.ts(G(3), G(2), 0.0, None, ALU.is_ge, None, ("g", u, 3), [("g", u, 2)])
        mb.stt(G(4), G(3), -3.0e38, G(0), ALU.mult, ALU.add, ("g", u, 4),
               [("g", u, 3), ("g", u, 0)])
        mb.red(G(5)[0:1, 0:1], G(4), ALU.max, ("g", u, 5), [("g", u, 4)])
        mb.act(G(6), G(0), AF.Exp, ("g", u, 6), [("g", u, 0), ("g", u, 1)],
               bias=G(1)[0:1, 0:1], scale=1.0)
        mb.ts(G(7), G(0), G(5)[0:1, 0:1], None, ALU.is_ge, None, ("g", u, 7),
              [("g", u, 0), ("g", u, 5)])
        mb.tt(G(6), G(6), G(7), ALU.mult, ("g", u, 6), [("g", u, 6), ("g", u, 7)])
        mb.red(G(5)[0:1, 1:2], G(6), ALU.add, ("g", u, 5), [("g", u, 6)])
        mb.recip(G(5)[0:1, 1:2], G(5)[0:1, 1:2], ("g", u, 5), [("g", u, 5)])
        mb.ts(G(6), G(6), G(5)[0:1, 1:2], None, ALU.mult, None, ("g", u, 6),
              [("g", u, 6), ("g", u, 5)])
        for cch in range(4):
            mb.mm(mb.pb[1][:, cch:cch + 1], mb.cs("top64_row")[:],
                  G(6)[0:1, 2 * cch:2 * cch + 1], ("pb", 1),
                  [("g", u, 6), ("c", "top64_row")], first=True, last=False)
            mb.mm(mb.pb[1][:, cch:cch + 1], mb.cs("bot64_row")[:],
                  G(6)[0:1, 2 * cch + 1:2 * cch + 2], ("pb", 1),
                  [("g", u, 6), ("c", "bot64_row")], first=False, last=True)
        mb.copy(gcols[:, :], mb.pb[1][:, 0:4], ("gc", u), [("pb", 1)])
        wat = mb.sbuf_scoped(f"b0wa{u}", [128, KC, 512], BF16)
        src = wat_bf[:, :].rearrange("(kc p) c -> p kc c", p=128)
        p.dma("scalar", wat[:], src, reads=[("wbf", "w0_AT")],
              writes=[("wat", u)])
        mid = mb.sbuf_scoped(f"b0mid{u}", [128, 4, T], BF16)
        xnp = [mb.sbuf_scoped(f"b0xp{u}_{i}", [128, 512], BF16) for i in range(2)]
        for tc in range(4):
            for k in range(KC):
                xb = xnp[k % 2]
                p.dma("sync", xb[:],
                      xn_dram[s, k, :, tc * 512:(tc + 1) * 512],
                      reads=[("xnd", s, k)], writes=[("xnp", u, k % 2)])
                for erc in range(4):
                    mb.mm(mb.pb[erc][:, :], wat[:, k, erc * 128:(erc + 1) * 128],
                          xb[:], ("pb", erc),
                          [("wat", u), ("xnp", u, k % 2)],
                          first=(k == 0), last=(k == KC - 1))
            for erc in range(4):
                mb.ts(mid[:, erc, tc * 512:(tc + 1) * 512], mb.pb[erc][:, :],
                      gcols[:, erc:erc + 1], None, ALU.mult, None,
                      ("mid", u, erc), [("pb", erc), ("gc", u)])
        wbw = mb.sbuf_scoped(f"b0wb{u}", [128, 4, 1024], BF16)
        src = wbw_bf[:, :].rearrange("(ec p) c -> p ec c", p=128)
        p.dma("scalar", wbw[:], src, reads=[("wbf", "w0_BwR")],
              writes=[("wbw", u)])
        for oc in range(KC):
            for tc in range(4):
                cols = slice(c0 + tc * 512, c0 + tc * 512 + 512)
                for erc in range(4):
                    mb.mm(mb.pb[4 + (oc % 2)][:, :],
                          wbw[:, erc, oc * 128:(oc + 1) * 128],
                          mid[:, erc, tc * 512:(tc + 1) * 512],
                          ("pb", 4 + (oc % 2)),
                          [("wbw", u), ("mid", u, erc)],
                          first=(erc == 0), last=(erc == 3))
                mb.stt(mb.x[oc][:, cols], mb.pb[4 + (oc % 2)][:, :], 1.0,
                       mb.x[oc][:, cols], ALU.mult, ALU.add, ("x", oc),
                       [("pb", 4 + (oc % 2)), ("x", oc)])
        mb.end_stage()



def _stats_combine(mb, u, stats, lrow, qt, nch, tag):
    strs = [(tag, u, i) for i in (0, 1, 2, 3, 4, 5, 6, 7, 8, 9)]
    if nch == 1:
        mb.act(stats[:, 12:13], stats[:, 4:5], AF.Ln, (tag, u, 12), strs)
        mb.tt(lrow[:, qt:qt + 1], stats[:, 0:1], stats[:, 12:13],
              ALU.subtract, ("lrow", u, qt), [(tag, u, 0), (tag, u, 12)])
    else:
        mb.red(stats[:, 8:9], stats[:, 0:nch], ALU.min, (tag, u, 8), strs)
        mb.act(stats[:, 0:nch], stats[:, 0:nch], AF.Exp, (tag, u, 0),
               strs + [(tag, u, 8)], scale=-1.0, bias=stats[:, 8:9])
        mb.tt(stats[:, 4:4 + nch], stats[:, 4:4 + nch], stats[:, 0:nch],
              ALU.mult, (tag, u, 4), strs)
        mb.red(stats[:, 9:10], stats[:, 4:4 + nch], ALU.add, (tag, u, 9), strs)
        mb.act(stats[:, 12:13], stats[:, 9:10], AF.Ln, (tag, u, 12),
               [(tag, u, 9)])
        mb.tt(lrow[:, qt:qt + 1], stats[:, 8:9], stats[:, 12:13],
              ALU.subtract, ("lrow", u, qt), [(tag, u, 8), (tag, u, 12)])


def _lrows_emit(mb, u, lrow, lbf, ltr, dst_hi, dst_lo, dst_res, extra=None,
                extra_res=()):
    p = mb.p
    lres = [("lrow", u, qt) for qt in range(16)]
    if extra is not None:
        mb.tt(lrow[:, 0:16], lrow[:, 0:16], extra, ALU.add, ("lrowx", u),
              lres + list(extra_res))
        lres = lres + [("lrowx", u)]
    mb.copy(lbf[:, 0:16], lrow[:, 0:16], ("lbf", u), lres)
    mb.copy(lrow[:, 16:32], lbf[:, 0:16], ("lrow2", u), [("lbf", u)])
    mb.tt(lrow[:, 32:48], lrow[:, 0:16], lrow[:, 16:32], ALU.subtract,
          ("lrow3", u), [("lrow2", u)] + lres)
    mb.copy(lbf[:, 16:32], lrow[:, 32:48], ("lbf2", u), [("lrow3", u)])
    mb.pe_transpose(mb.pb["t"][0:16, 0:128], lbf[:, 0:16], ("pbt",), [("lbf", u)])
    mb.copy(ltr[:, :], mb.pb["t"][0:16, 0:128], ("ltr", u), [("pbt",)])
    p.dma("sync", dst_hi, ltr[:, :], reads=[("ltr", u)], writes=[dst_res])
    mb.pe_transpose(mb.pb["t"][0:16, 0:128], lbf[:, 16:32], ("pbt",), [("lbf2", u)])
    mb.copy(ltr[:, :], mb.pb["t"][0:16, 0:128], ("ltr", u), [("pbt",)])
    p.dma("sync", dst_lo, ltr[:, :], reads=[("ltr", u)], writes=[dst_res])


def pre_block1(mb: MB):
    p = mb.p
    bn_g = mb.sbuf("b1g", [128, KC], F32)
    p.dma("sync", bn_g[:], mb.dram["b1_g"][:, :], writes=[("b1g",)])
    bn_b = mb.sbuf("b1bta", [128, KC], F32)
    p.dma("sync", bn_b[:], mb.dram["b1_b"][:, :], writes=[("b1bta",)])
    sub_c = mb.sbuf("b1sub", [128, 1], F32)
    p.dma("sync", sub_c[:], mb.dram["b1_subln"][:, :], writes=[("b1sub",)])
    lqk = mb.sbuf("b1lqk", [1, 256], F32)
    for i, nm in enumerate(("b1_lq1", "b1_lk1", "b1_lq2", "b1_lk2")):
        p.dma("sync", lqk[0:1, 64 * i:64 * i + 64], mb.dram[nm][:, :],
              writes=[("lqk", i)])
    mb.sbuf("b1bnacc", [128, 16], F32)
    mb.sbuf("b1bncoef", [128, 16], F32)
    mb.sbuf("b1lam", [1, 8], F32)
    mb.sbuf("b1nlam", [128, 1], F32)
    mb.sbuf("b1sc", [128, 1], F32)


def emit_block1(mb: MB):
    p = mb.p
    xn_dram = mb.dtmp("b1_xn", [NSEQ, KC, 128, T], BF16)
    xnbn_dram = mb.dtmp("b1_xnbn", [NSEQ, KC, 128, T], BF16)
    y_dram = mb.dtmp("b1_y", [NSEQ, 16, 128, T], BF16)
    yf_dram = mb.dtmp("b1_yf", [NSEQ, KC, 128, T], BF16)
    xoth = (None if SIM_NO_CC
            else mb.din("xoth", [N_CORES - 1, KC, 128, NSEQ * T], F32))
    for nm in ("w1_q1T", "w1_q2T", "w1_k1T", "w1_k2T", "w1_vT", "w1_cT",
               "w1_Wg", "w1_Wq", "w1_Wk", "w1_Wv", "w1_Wo"):
        mb.stage_weight_bf16(nm)
    bn_g, bn_b, sub_c = mb.sb["b1g"], mb.sb["b1bta"], mb.sb["b1sub"]
    lqk, bnacc, bncoef = mb.sb["b1lqk"], mb.sb["b1bnacc"], mb.sb["b1bncoef"]
    lamt, nlam_c, sc_c = mb.sb["b1lam"], mb.sb["b1nlam"], mb.sb["b1sc"]
    LAMBDA_INIT = 0.8 - 0.6 * math.exp(-0.3 * 1)

    mb.memset(bnacc[:, :], 0.0, ("bnacc",))
    for s in range(NSEQ):
        uL = mb.uid()
        xst = [mb.sbuf_scoped(f"b1xst{uL}_{i}", [128, 512], BF16) for i in range(2)]
        sqb = mb.sbuf_scoped(f"b1sqb{uL}", [128, 512], F32)
        colt = mb.sbuf_scoped(f"b1colt{uL}", [128, 2], F32)

        def xn_sink(kc, tc, s=s, uL=uL):
            i = (kc + tc) % 2
            ap = xst[i][:, :]
            res = ("xst", uL, i)

            def post(kc=kc, tc=tc, i=i, res=res):
                p.dma("sync", xn_dram[s, kc, :, tc * 512:(tc + 1) * 512],
                      xst[i][:, :], reads=[res], writes=[("xnd1", s, kc)])
                mb.red(colt[:, 0:1], xst[i][:, :], ALU.add, ("colt", uL, 0), [res])
                mb.tt(bnacc[:, kc:kc + 1], bnacc[:, kc:kc + 1], colt[:, 0:1],
                      ALU.add, ("bnacc",), [("bnacc",), ("colt", uL, 0)])
                mb.act(sqb[:, :], xst[i][:, :], AF.Square, ("sqb", uL), [res])
                mb.red(colt[:, 1:2], sqb[:, :], ALU.add, ("colt", uL, 1),
                       [("sqb", uL)])
                mb.tt(bnacc[:, 8 + kc:9 + kc], bnacc[:, 8 + kc:9 + kc],
                      colt[:, 1:2], ALU.add, ("bnacc",),
                      [("bnacc",), ("colt", uL, 1)])
            return ap, res, post
        mb.ln_fm(s, xn_sink, add_xn_into_x=False)
        mb.end_stage()

    if not SIM_NO_CC:
        for oc in range(N_CORES - 1):
            for sl in range(NSEQ):
                u = mb.uid()
                xoc = [mb.sbuf_scoped(f"b1fo{u}_{i}", [128, 512], F32)
                       for i in range(2)]
                snk = mb.sbuf_scoped(f"b1fs{u}", [128, 512], F32)
                fsq = mb.sbuf_scoped(f"b1fq{u}", [128, 512], F32)
                fct = mb.sbuf_scoped(f"b1fc{u}", [128, 2], F32)

                def get_piece(kc, tc, phase, oc=oc, sl=sl, u=u, xoc=xoc):
                    i = (kc + tc + phase) % 2
                    res = ("b1fo", u, i)
                    p.dma("sync", xoc[i][:, :],
                          xoth[oc, kc, :, sl * T + tc * 512:
                               sl * T + (tc + 1) * 512],
                          writes=[res])
                    return xoc[i][:, :], res

                def sink(kc, tc, u=u, snk=snk, fsq=fsq, fct=fct):
                    res = ("b1fs", u)

                    def post(kc=kc, res=res):
                        mb.red(fct[:, 0:1], snk[:, :], ALU.add,
                               ("b1fc", u, 0), [res])
                        mb.tt(bnacc[:, kc:kc + 1], bnacc[:, kc:kc + 1],
                              fct[:, 0:1], ALU.add, ("bnacc",),
                              [("bnacc",), ("b1fc", u, 0)])
                        mb.act(fsq[:, :], snk[:, :], AF.Square,
                               ("b1fq", u), [res])
                        mb.red(fct[:, 1:2], fsq[:, :], ALU.add,
                               ("b1fc", u, 1), [("b1fq", u)])
                        mb.tt(bnacc[:, 8 + kc:9 + kc], bnacc[:, 8 + kc:9 + kc],
                              fct[:, 1:2], ALU.add, ("bnacc",),
                              [("bnacc",), ("b1fc", u, 1)])
                    return snk[:, :], res, post
                ln_src(mb, get_piece, sink, tag="b1f")
                mb.end_stage()
    NTOK = (NSEQ * T) if SIM_NO_CC else (16 * T)
    mb.ts(bncoef[:, 0:8], bnacc[:, 0:8], 1.0 / NTOK, None, ALU.mult, None,
          ("bnc", 0), [("bnacc",)])
    mb.ts(bncoef[:, 8:16], bnacc[:, 8:16], 1.0 / NTOK, None, ALU.mult, None,
          ("bnc", 1), [("bnacc",)])
    mb.act(bnacc[:, 0:8], bncoef[:, 0:8], AF.Square, ("bnacc",), [("bnc", 0)])
    mb.tt(bncoef[:, 8:16], bncoef[:, 8:16], bnacc[:, 0:8], ALU.subtract,
          ("bnc", 1), [("bnc", 1), ("bnacc",)])
    mb.act(bncoef[:, 8:16], bncoef[:, 8:16], AF.Sqrt, ("bnc", 1),
           [("bnc", 1), ("c", "eps_col")], bias=mb.cs("eps_col")[:, 0:1],
           scale=1.0)
    mb.recip(bncoef[:, 8:16], bncoef[:, 8:16], ("bnc", 1), [("bnc", 1)])
    mb.tt(bncoef[:, 8:16], bncoef[:, 8:16], bn_g[:, :], ALU.mult, ("bnc", 1),
          [("bnc", 1), ("b1g",)])
    mb.tt(bnacc[:, 0:8], bncoef[:, 0:8], bncoef[:, 8:16], ALU.mult, ("bnacc",),
          [("bnc", 0), ("bnc", 1)])
    mb.stt(bncoef[:, 0:8], bnacc[:, 0:8], -1.0, bn_b[:, :], ALU.mult, ALU.add,
           ("bnc", 0), [("bnacc",), ("b1bta",)])
    L8 = lambda j: lamt[0:1, j:j + 1]
    mb.tt(lqk[0:1, 0:64], lqk[0:1, 0:64], lqk[0:1, 64:128], ALU.mult,
          ("lqk", 0), [("lqk", 0), ("lqk", 1)])
    mb.red(L8(0), lqk[0:1, 0:64], ALU.add, ("lam", 0), [("lqk", 0)])
    mb.tt(lqk[0:1, 128:192], lqk[0:1, 128:192], lqk[0:1, 192:256], ALU.mult,
          ("lqk", 2), [("lqk", 2), ("lqk", 3)])
    mb.red(L8(1), lqk[0:1, 128:192], ALU.add, ("lam", 1), [("lqk", 2)])
    mb.act(L8(2), L8(0), AF.Exp, ("lam", 2), [("lam", 0)])
    mb.act(L8(3), L8(1), AF.Exp, ("lam", 3), [("lam", 1)])
    mb.tt(L8(4), L8(2), L8(3), ALU.subtract, ("lam", 4), [("lam", 2), ("lam", 3)])
    mb.ts(L8(5), L8(4), LAMBDA_INIT, None, ALU.add, None, ("lam", 5), [("lam", 4)])
    mb.mm(mb.pb[0][:, 0:1], mb.cs("ones_row")[:], L8(5), ("pb", 0),
          [("lam", 5), ("c", "ones_row")])
    mb.ts(nlam_c[:, :], mb.pb[0][:, 0:1], -1.0, None, ALU.mult, None,
          ("nlam",), [("pb", 0)])
    mb.ts(sc_c[:, :], sub_c[:, :], 1.0 - LAMBDA_INIT, None, ALU.mult, None,
          ("scc",), [("b1sub",)])

    for s in range(NSEQ):
        u = mb.uid()
        xch = [mb.sbuf_scoped(f"b1xch{u}_{i}", [128, T], BF16) for i in range(2)]
        for kc in range(KC):
            xb = xch[kc % 2]
            p.dma("sync", xb[:], xn_dram[s, kc, :, :], reads=[("xnd1", s, kc)],
                  writes=[("xch", u, kc % 2)])
            mb.ts(xb[:], xb[:], bncoef[:, 8 + kc:9 + kc],
                  bncoef[:, kc:kc + 1], ALU.mult, ALU.add, ("xch", u, kc % 2),
                  [("xch", u, kc % 2), ("bnc", 0), ("bnc", 1)])
            p.dma("sync", xnbn_dram[s, kc, :, :], xb[:],
                  reads=[("xch", u, kc % 2)], writes=[("xnbnd", s, kc)])
        mb.end_stage()

        u = mb.uid()
        qs_ = {nm: mb.sbuf_scoped(f"b1{nm}{u}", [128, KC, 64], BF16)
               for nm in ("q1s", "q2s", "k1s", "k2s")}
        vs = mb.sbuf_scoped(f"b1vs{u}", [128, KC, 128], BF16)
        xnb = [mb.sbuf_scoped(f"b1xnb{u}_{i}", [128, 512], BF16) for i in range(2)]
        QQ = mb.sbuf_scoped(f"b1QQ{u}", [128, T], BF16)
        KK = mb.sbuf_scoped(f"b1KK{u}", [128, T], BF16)
        vt = mb.sbuf_scoped(f"b1vt{u}", [128, 16, 128], BF16)
        lr2 = mb.sbuf_scoped(f"b1lr2{u}", [2, T], BF16)
        pt = mb.sbuf_scoped(f"b1pt{u}", [128, T], BF16)
        ysb = mb.sbuf_scoped(f"b1ysb{u}", [128, T], F32)
        sqb = mb.sbuf_scoped(f"b1sq{u}", [128, T], BF16)
        ssrow = mb.sbuf_scoped(f"b1ss{u}", [1, T], F32)
        ystage = mb.sbuf_scoped(f"b1ys{u}", [128, T], BF16)
        stats = mb.sbuf_scoped(f"b1st{u}", [128, 16], F32)
        lrow = mb.sbuf_scoped(f"b1lr{u}", [128, 48], F32)
        lbf = mb.sbuf_scoped(f"b1lbf{u}", [128, 32], BF16)
        ltr = mb.sbuf_scoped(f"b1ltr{u}", [16, 128], BF16)
        trash = mb.sbuf_scoped(f"b1tr{u}", [128, 512], BF16)
        for h in range(16):
            for nm, wname in (("q1s", "w1_q1T"), ("q2s", "w1_q2T"),
                              ("k1s", "w1_k1T"), ("k2s", "w1_k2T")):
                src = mb.wbf[wname][:, 64 * h:64 * h + 64]
                src = src.rearrange("(kc p) c -> p kc c", p=128)
                p.dma("scalar", qs_[nm][:], src, reads=[("wbf", wname)],
                      writes=[("w1s", nm, u)])
            src = mb.wbf["w1_vT"][:, 128 * h:128 * h + 128]
            src = src.rearrange("(kc p) c -> p kc c", p=128)
            p.dma("scalar", vs[:], src, reads=[("wbf", "w1_vT")],
                  writes=[("w1s", "vs", u)])
            for qr in range(4):
                for kc in range(KC):
                    xb = xnb[kc % 2]
                    p.dma("sync", xb[:],
                          xnbn_dram[s, kc, :, qr * 512:qr * 512 + 512],
                          reads=[("xnbnd", s, kc)], writes=[("xnb", u, kc % 2)])
                    for bi, nm in ((0, "q1s"), (1, "q2s"), (2, "k1s"),
                                   (3, "k2s")):
                        mb.mm(mb.pb[bi][0:64, :], qs_[nm][:, kc, :], xb[:],
                              ("pb", bi), [("w1s", nm, u), ("xnb", u, kc % 2)],
                              first=(kc == 0), last=(kc == KC - 1))
                    for tt_ in range(4):
                        bslc = mb.pb[4][:, tt_ * 128:tt_ * 128 + 128]
                        mb.mm(bslc, xb[:, tt_ * 128:tt_ * 128 + 128],
                              vs[:, kc, :], ("pb", 4),
                              [("xnb", u, kc % 2), ("w1s", "vs", u)],
                              first=(kc == 0 and tt_ == 0),
                              last=(kc == KC - 1 and tt_ == 3))
                lc = slice(qr * 512, qr * 512 + 512)
                mb.act(QQ[0:64, lc], mb.pb[0][0:64, :], AF.Identity, ("QQ", u),
                       [("pb", 0)], scale=0.125)
                mb.act(QQ[64:128, lc], mb.pb[1][0:64, :], AF.Identity, ("QQ", u),
                       [("pb", 1)], scale=0.125)
                mb.act(KK[0:64, lc], mb.pb[2][0:64, :], AF.Identity, ("KK", u),
                       [("pb", 2)])
                mb.act(KK[64:128, lc], mb.pb[3][0:64, :], AF.Identity, ("KK", u),
                       [("pb", 3)])
                mb.act(vt[:, 4 * qr:4 * qr + 4, :], mb.pb[4][:, :], AF.Identity,
                       ("vt", u), [("pb", 4)])
            for m_ in range(2):
                rows = slice(64 * m_, 64 * m_ + 64)
                qres, kres = ("QQ", u), ("KK", u)
                for qt in range(16):
                    for cch in range(4):
                        pbs = mb.pb[cch % 2]
                        mb.mm(pbs[:, :], QQ[rows, qt * 128:qt * 128 + 128],
                              KK[rows, cch * 512:cch * 512 + 512],
                              ("pb", cch % 2), [qres, kres])
                        mb.red(stats[:, cch:cch + 1], pbs[:, :], ALU.max,
                               ("st1", u, cch), [("pb", cch % 2)], negate=True)
                        mb.act(trash[:, :], pbs[:, :], AF.Exp,
                               ("trash", u), [("pb", cch % 2), ("st1", u, cch)],
                               bias=stats[:, cch:cch + 1],
                               accum=stats[:, 4 + cch:5 + cch],
                               accum_res=("st1", u, 4 + cch))
                    _stats_combine(mb, u, stats, lrow, qt, 4, "st1")
                _lrows_emit(mb, u, lrow, lbf, ltr, lr2[0:1, :], lr2[1:2, :],
                            ("lr2", u))
                yb = {0: 3, 1: 4, 2: 5, 3: 6}
                for jt in range(16):
                    for qc in range(4):
                        pbs = mb.pb[qc % 2]
                        qcs = slice(qc * 512, qc * 512 + 512)
                        mb.mm(pbs[:, :], KK[rows, jt * 128:jt * 128 + 128],
                              QQ[rows, qcs], ("pb", qc % 2), [qres, kres],
                              first=True, last=False)
                        mb.mm(pbs[:, :], mb.cs("ones2b")[:, :], lr2[:, qcs],
                              ("pb", qc % 2), [("lr2", u), ("c", "ones2b")],
                              first=False, last=True)
                        mb.act(pt[:, qcs], pbs[:, :], AF.Exp, ("pt", u, qc),
                               [("pb", qc % 2)])
                        mb.mm(mb.pb[yb[qc]][:, :], vt[:, jt, :], pt[:, qcs],
                              ("pb", yb[qc]), [("vt", u), ("pt", u, qc)],
                              first=(jt == 0), last=(jt == 15))
                if m_ == 0:
                    for qc in range(4):
                        mb.act(ysb[:, qc * 512:(qc + 1) * 512],
                               mb.pb[yb[qc]][:, :], AF.Identity, ("ysb", u),
                               [("pb", yb[qc])])
                else:
                    for qc in range(4):
                        mb.stt(ysb[:, qc * 512:(qc + 1) * 512],
                               mb.pb[yb[qc]][:, :], nlam_c[:, 0:1],
                               ysb[:, qc * 512:(qc + 1) * 512],
                               ALU.mult, ALU.add, ("ysb", u),
                               [("pb", yb[qc]), ("nlam",), ("ysb", u)])
            mb.act(sqb[:, :], ysb[:, :], AF.Square, ("sqb1", u), [("ysb", u)])
            for qc in range(4):
                mb.mm(mb.pb[0][0:1, :], mb.cs("ones_col_bf")[:],
                      sqb[:, qc * 512:(qc + 1) * 512], ("pb", 0),
                      [("sqb1", u), ("c", "ones_col_bf")])
                mb.copy(ssrow[0:1, qc * 512:(qc + 1) * 512], mb.pb[0][0:1, :],
                        ("ssr", u), [("pb", 0)])
            mb.act(ssrow[0:1, :], ssrow[0:1, :], AF.Sqrt, ("ssr", u),
                   [("ssr", u), ("c", "eps_col")], scale=1.0 / 128.0,
                   bias=mb.cs("eps_col")[0:1, 0:1])
            mb.recip(ssrow[0:1, :], ssrow[0:1, :], ("ssr", u), [("ssr", u)])
            for qc in range(4):
                qcs = slice(qc * 512, qc * 512 + 512)
                mb.mm(mb.pb[1][:, :], mb.cs("ones_row")[:], ssrow[0:1, qcs],
                      ("pb", 1), [("ssr", u), ("c", "ones_row")])
                mb.stt(ystage[:, qcs], mb.pb[1][:, :], 1.0, ysb[:, qcs],
                       ALU.mult, ALU.mult, ("yst", u), [("pb", 1), ("ysb", u)])
            p.dma("sync", y_dram[s, h, :, :], ystage[:, :], reads=[("yst", u)],
                  writes=[("yd1", s, h)])
        mb.end_stage()

        u = mb.uid()
        cwk = mb.sbuf_scoped(f"b1cw{u}", [128, 16, 512], BF16)
        ybuf = [mb.sbuf_scoped(f"b1yb{u}_{i}", [128, 512], BF16) for i in range(2)]
        c0 = s * T
        for ocg in range(2):
            src = mb.wbf["w1_cT"][:, ocg * 512:(ocg + 1) * 512]
            src = src.rearrange("(hc p) c -> p hc c", p=128)
            p.dma("scalar", cwk[:], src, reads=[("wbf", "w1_cT")],
                  writes=[("cw", u)])
            mb.ts(cwk[:, :, :], cwk[:, :, :], sc_c[:, 0:1], None, ALU.mult,
                  None, ("cw", u), [("cw", u), ("scc",)])
            for tc in range(4):
                for hc in range(16):
                    yb_ = ybuf[hc % 2]
                    p.dma("scalar", yb_[:],
                          y_dram[s, hc, :, tc * 512:(tc + 1) * 512],
                          reads=[("yd1", s, hc)], writes=[("yb1", u, hc % 2)])
                    for o4 in range(4):
                        mb.mm(mb.pb[o4][:, :], cwk[:, hc, o4 * 128:(o4 + 1) * 128],
                              yb_[:], ("pb", o4), [("cw", u), ("yb1", u, hc % 2)],
                              first=(hc == 0), last=(hc == 15))
                for o4 in range(4):
                    oc = ocg * 4 + o4
                    cols = slice(c0 + tc * 512, c0 + tc * 512 + 512)
                    mb.stt(mb.x[oc][:, cols], mb.pb[o4][:, :], 1.0,
                           mb.x[oc][:, cols], ALU.mult, ALU.add, ("x", oc),
                           [("pb", o4), ("x", oc)])
        mb.end_stage()

        u = mb.uid()
        wgs = mb.sbuf_scoped(f"b1wg{u}", [128, KC], BF16)
        p.dma("scalar", wgs[:], mb.wbf["w1_Wg"][:, :],
              reads=[("wbf", "w1_Wg")], writes=[("wgs", u)])
        xnb = [mb.sbuf_scoped(f"b1fx{u}_{i}", [128, T], BF16) for i in range(2)]
        gtm = mb.sbuf_scoped(f"b1gtm{u}", [128, 16], F32)
        biz = mb.sbuf_scoped(f"b1biz{u}", [1, 8], F32)
        thc = mb.sbuf_scoped(f"b1thc{u}", [128, 1], F32)
        sq_tm = mb.sbuf_scoped(f"b1sqm{u}", [128, 16 * 3], F32)
        augk = mb.sbuf_scoped(f"b1agk{u}", [3, T], BF16)
        augq = mb.sbuf_scoped(f"b1agq{u}", [3, T], BF16)
        penbf = mb.sbuf_scoped(f"b1pen{u}", [128, 16], BF16)
        for kc in range(KC):
            xb = xnb[kc % 2]
            p.dma("sync", xb[:], xn_dram[s, kc, :, :], reads=[("xnd1", s, kc)],
                  writes=[("fx", u, kc % 2)])
            for tt_ in range(16):
                mb.mm(mb.pb[3][:, tt_:tt_ + 1], xb[:, tt_ * 128:tt_ * 128 + 128],
                      wgs[:, kc:kc + 1], ("pb", 3), [("fx", u, kc % 2), ("wgs", u)],
                      first=(kc == 0 and tt_ == 0),
                      last=(kc == KC - 1 and tt_ == 15))
        mb.copy(gtm[:, :], mb.pb[3][:, 0:16], ("gtm", u), [("pb", 3)])
        B8 = lambda j: biz[0:1, j:j + 1]
        mb.memset(B8(0), -100.0, ("biz", u, 0))
        mb.memset(B8(1), 100.0, ("biz", u, 1))
        for it in range(40):
            mb.tt(B8(2), B8(0), B8(1), ALU.add, ("biz", u, 2),
                  [("biz", u, 0), ("biz", u, 1)])
            mb.ts(B8(2), B8(2), 0.5, None, ALU.mult, None, ("biz", u, 2),
                  [("biz", u, 2)])
            mb.mm(mb.pb[0][:, 0:1], mb.cs("ones_row")[:], B8(2), ("pb", 0),
                  [("biz", u, 2), ("c", "ones_row")])
            mb.copy(thc[:, :], mb.pb[0][:, 0:1], ("thc", u), [("pb", 0)])
            mb.ts(sq_tm[:, 0:16], gtm[:, :], thc[:, 0:1], None, ALU.is_ge,
                  None, ("sqm", u, 0), [("gtm", u), ("thc", u)])
            mb.red(sq_tm[:, 16:17], sq_tm[:, 0:16], ALU.add, ("sqm", u, 1),
                   [("sqm", u, 0)])
            mb.mm(mb.pb[1][0:1, 0:1], mb.cs("ones_col")[:], sq_tm[:, 16:17],
                  ("pb", 1), [("sqm", u, 1), ("c", "ones_col")])
            mb.copy(B8(3), mb.pb[1][0:1, 0:1], ("biz", u, 3), [("pb", 1)])
            mb.ts(B8(4), B8(3), 1024.0, None, ALU.is_ge, None, ("biz", u, 4),
                  [("biz", u, 3)])
            mb.tt(B8(5), B8(2), B8(0), ALU.subtract, ("biz", u, 5),
                  [("biz", u, 2), ("biz", u, 0)])
            mb.stt(B8(0), B8(5), B8(4)[0:1, 0:1], B8(0), ALU.mult, ALU.add,
                   ("biz", u, 0), [("biz", u, 5), ("biz", u, 4), ("biz", u, 0)])
            mb.ts(B8(6), B8(4), 1.0, -1.0, ALU.mult, ALU.add, ("biz", u, 6),
                  [("biz", u, 4)])
            mb.tt(B8(7), B8(1), B8(2), ALU.subtract, ("biz", u, 7),
                  [("biz", u, 1), ("biz", u, 2)])
            mb.stt(B8(1), B8(7), B8(6)[0:1, 0:1], B8(1), ALU.mult, ALU.add,
                   ("biz", u, 1), [("biz", u, 7), ("biz", u, 6), ("biz", u, 1)])
        mb.tt(B8(2), B8(0), B8(1), ALU.add, ("biz", u, 2),
              [("biz", u, 0), ("biz", u, 1)])
        mb.ts(B8(2), B8(2), 0.5, None, ALU.mult, None, ("biz", u, 2),
              [("biz", u, 2)])
        mb.mm(mb.pb[0][:, 0:1], mb.cs("ones_row")[:], B8(2), ("pb", 0),
              [("biz", u, 2), ("c", "ones_row")])
        mb.copy(thc[:, :], mb.pb[0][:, 0:1], ("thc", u), [("pb", 0)])
        mb.ts(sq_tm[:, 0:16], gtm[:, :], thc[:, 0:1], None, ALU.is_ge, None,
              ("sqm", u, 0), [("gtm", u), ("thc", u)])
        mb.ts(sq_tm[:, 32:48], sq_tm[:, 0:16], 1.0e9, -1.0e9, ALU.mult, ALU.add,
              ("sqm", u, 2), [("sqm", u, 0)])
        mb.act(sq_tm[:, 16:32], gtm[:, :], AF.Sigmoid, ("sqm", u, 1), [("gtm", u)])
        mb.ts(sq_tm[:, 16:32], sq_tm[:, 16:32], 1.0e-38, None, ALU.max, None,
              ("sqm", u, 1), [("sqm", u, 1)])
        mb.act(sq_tm[:, 16:32], sq_tm[:, 16:32], AF.Ln, ("sqm", u, 1),
               [("sqm", u, 1)])
        mb.tt(sq_tm[:, 16:32], sq_tm[:, 16:32], sq_tm[:, 32:48], ALU.add,
              ("sqm", u, 1), [("sqm", u, 1), ("sqm", u, 2)])
        mb.memset(augk[0:3, :], 1.0, ("augk", u))
        mb.memset(augq[0:3, :], 1.0, ("augq", u))
        mb.copy(penbf[:, :], sq_tm[:, 32:48], ("penbf", u), [("sqm", u, 2)])
        mb.pe_transpose(mb.pb["t"][0:16, 0:128], penbf[:, :], ("pbt",),
                        [("penbf", u)])
        ltr2 = mb.sbuf_scoped(f"b1lt2{u}", [16, 128], BF16)
        mb.copy(ltr2[:, :], mb.pb["t"][0:16, 0:128], ("ltr2", u), [("pbt",)])
        p.dma("sync", augk[0:1, :], ltr2[:, :], reads=[("ltr2", u)],
              writes=[("augk", u)])
        uG = u

        wq_ = mb.sbuf_scoped(f"b1fwq{u}", [128, KC, 128], BF16)
        wk_ = mb.sbuf_scoped(f"b1fwk{u}", [128, KC, 128], BF16)
        wv_ = mb.sbuf_scoped(f"b1fwv{u}", [128, KC, 128], BF16)
        xnb = [mb.sbuf_scoped(f"b1px{u}_{i}", [128, 512], BF16) for i in range(2)]
        QQ = mb.sbuf_scoped(f"b1fQ{u}", [128, T], BF16)
        KK = mb.sbuf_scoped(f"b1fK{u}", [128, T], BF16)
        vt = mb.sbuf_scoped(f"b1fv{u}", [128, 16, 128], BF16)
        pt = mb.sbuf_scoped(f"b1fp{u}", [128, T], BF16)
        ystage = mb.sbuf_scoped(f"b1fys{u}", [128, T], BF16)
        stats = mb.sbuf_scoped(f"b1fst{u}", [128, 16], F32)
        lrow = mb.sbuf_scoped(f"b1flr{u}", [128, 48], F32)
        lbf = mb.sbuf_scoped(f"b1flb{u}", [128, 32], BF16)
        ltr = mb.sbuf_scoped(f"b1flt{u}", [16, 128], BF16)
        trash = mb.sbuf_scoped(f"b1ftr{u}", [128, 512], BF16)
        for pair in range(8):
            for tile, wname in ((wq_, "w1_Wq"), (wk_, "w1_Wk"), (wv_, "w1_Wv")):
                src = mb.wbf[wname][:, 128 * pair:128 * pair + 128]
                src = src.rearrange("(kc p) c -> p kc c", p=128)
                p.dma("scalar", tile[:], src, reads=[("wbf", wname)],
                      writes=[("w1f", wname, u)])
            for qr in range(4):
                for kc in range(KC):
                    xb = xnb[kc % 2]
                    p.dma("sync", xb[:],
                          xn_dram[s, kc, :, qr * 512:qr * 512 + 512],
                          reads=[("xnd1", s, kc)], writes=[("px", u, kc % 2)])
                    mb.mm(mb.pb[0][:, :], wq_[:, kc, :], xb[:], ("pb", 0),
                          [("w1f", "w1_Wq", u), ("px", u, kc % 2)],
                          first=(kc == 0), last=(kc == KC - 1))
                    mb.mm(mb.pb[1][:, :], wk_[:, kc, :], xb[:], ("pb", 1),
                          [("w1f", "w1_Wk", u), ("px", u, kc % 2)],
                          first=(kc == 0), last=(kc == KC - 1))
                    for tt_ in range(4):
                        bslc = mb.pb[2][:, tt_ * 128:tt_ * 128 + 128]
                        mb.mm(bslc, xb[:, tt_ * 128:tt_ * 128 + 128],
                              wv_[:, kc, :], ("pb", 2),
                              [("px", u, kc % 2), ("w1f", "w1_Wv", u)],
                              first=(kc == 0 and tt_ == 0),
                              last=(kc == KC - 1 and tt_ == 3))
                lc = slice(qr * 512, qr * 512 + 512)
                mb.act(QQ[:, lc], mb.pb[0][:, :], AF.Identity, ("fQ", u),
                       [("pb", 0)], scale=0.125)
                mb.act(KK[:, lc], mb.pb[1][:, :], AF.Identity, ("fK", u),
                       [("pb", 1)])
                mb.act(vt[:, 4 * qr:4 * qr + 4, :], mb.pb[2][:, :], AF.Identity,
                       ("fv", u), [("pb", 2)])
            for hh in range(2):
                rows = slice(64 * hh, 64 * hh + 64)
                for qt in range(16):
                    for cch in range(4):
                        pbs = mb.pb[cch % 2]
                        ccs = slice(cch * 512, cch * 512 + 512)
                        mb.mm(pbs[:, :], QQ[rows, qt * 128:qt * 128 + 128],
                              KK[rows, ccs], ("pb", cch % 2),
                              [("fQ", u), ("fK", u)], first=True, last=False)
                        mb.mm(pbs[:, :], mb.cs("ones1b")[:, :], augk[0:1, ccs],
                              ("pb", cch % 2), [("augk", u), ("c", "ones1b")],
                              first=False, last=True)
                        mb.red(stats[:, cch:cch + 1], pbs[:, :], ALU.max,
                               ("st2", u, cch), [("pb", cch % 2)], negate=True)
                        mb.act(trash[:, :], pbs[:, :], AF.Exp, ("trash", u),
                               [("pb", cch % 2), ("st2", u, cch)],
                               bias=stats[:, cch:cch + 1],
                               accum=stats[:, 4 + cch:5 + cch],
                               accum_res=("st2", u, 4 + cch))
                    _stats_combine(mb, u, stats, lrow, qt, 4, "st2")
                _lrows_emit(mb, u, lrow, lbf, ltr, augq[1:2, :], augq[2:3, :],
                            ("augq", u), extra=sq_tm[:, 16:32],
                            extra_res=[("sqm", uG, 1)])
                yb = {0: 3, 1: 4, 2: 5, 3: 6}
                for jt in range(16):
                    for qc in range(4):
                        pbs = mb.pb[qc % 2]
                        qcs = slice(qc * 512, qc * 512 + 512)
                        mb.mm(pbs[:, :], KK[rows, jt * 128:jt * 128 + 128],
                              QQ[rows, qcs], ("pb", qc % 2),
                              [("fQ", u), ("fK", u)], first=True, last=False)
                        mb.mm(pbs[:, :], augk[0:3, jt * 128:jt * 128 + 128],
                              augq[0:3, qcs], ("pb", qc % 2),
                              [("augk", u), ("augq", u)], first=False, last=True)
                        mb.act(pt[:, qcs], pbs[:, :], AF.Exp, ("fp", u, qc),
                               [("pb", qc % 2)])
                        mb.mm(mb.pb[yb[qc]][0:64, :], vt[:, jt, rows],
                              pt[:, qcs], ("pb", yb[qc]),
                              [("fv", u), ("fp", u, qc)],
                              first=(jt == 0), last=(jt == 15))
                for qc in range(4):
                    mb.act(ystage[rows, qc * 512:(qc + 1) * 512],
                           mb.pb[yb[qc]][0:64, :], AF.Identity, ("fys", u, hh),
                           [("pb", yb[qc])])
            p.dma("sync", yf_dram[s, pair, :, :], ystage[:, :],
                  reads=[("fys", u, 0), ("fys", u, 1)], writes=[("yfd", s, pair)])
        mb.end_stage()

        u = mb.uid()
        wok = mb.sbuf_scoped(f"b1wo{u}", [128, KC, 512], BF16)
        ybuf = [mb.sbuf_scoped(f"b1oyb{u}_{i}", [128, 512], BF16) for i in range(2)]
        for ocg in range(2):
            src = mb.wbf["w1_Wo"][:, ocg * 512:(ocg + 1) * 512]
            src = src.rearrange("(kc p) c -> p kc c", p=128)
            p.dma("scalar", wok[:], src, reads=[("wbf", "w1_Wo")],
                  writes=[("wo1", u)])
            for tc in range(4):
                for kc in range(KC):
                    yb_ = ybuf[kc % 2]
                    p.dma("scalar", yb_[:],
                          yf_dram[s, kc, :, tc * 512:(tc + 1) * 512],
                          reads=[("yfd", s, kc)], writes=[("oyb", u, kc % 2)])
                    for o4 in range(4):
                        mb.mm(mb.pb[o4][:, :], wok[:, kc, o4 * 128:(o4 + 1) * 128],
                              yb_[:], ("pb", o4), [("wo1", u), ("oyb", u, kc % 2)],
                              first=(kc == 0), last=(kc == KC - 1))
                for o4 in range(4):
                    oc = ocg * 4 + o4
                    cols = slice(c0 + tc * 512, c0 + tc * 512 + 512)
                    mb.stt(mb.x[oc][:, cols], mb.pb[o4][:, :], 1.0,
                           mb.x[oc][:, cols], ALU.mult, ALU.add, ("x", oc),
                           [("pb", o4), ("x", oc)])
        mb.end_stage()



def ln_src(mb, get_piece, sink, gain_col=None, bias_col=None, tag="",
           affine_reads=()):
    p = mb.p
    ones = mb.cs("ones_col")
    onesr = mb.cs("ones_row")
    u = mb.uid()
    xsq = [mb.sbuf_scoped(f"lsq{tag}{u}_{i}", [128, 512], F32) for i in range(2)]
    rows = mb.sbuf_scoped(f"lsr{tag}{u}", [1, 512 * 3], F32)
    ab = mb.sbuf_scoped(f"lsa{tag}{u}", [128, 512], F32)
    bb = mb.sbuf_scoped(f"lsb{tag}{u}", [128, 512], F32)
    t1 = mb.sbuf_scoped(f"lst{tag}{u}", [128, 512], F32)
    R = lambda j: rows[0:1, 512 * j:512 * (j + 1)]
    for tc in range(4):
        for kc in range(KC):
            ap, res = get_piece(kc, tc, 0)
            mb.mm(mb.pb[0][0:1, :], ones[:], ap, ("pb", 0),
                  [res, ("c", "ones_col")], first=(kc == 0), last=(kc == KC - 1))
            b = xsq[kc % 2]
            mb.act(b[:], ap, AF.Square, ("lsq", u, kc % 2), [res])
            mb.mm(mb.pb[1][0:1, :], ones[:], b[:], ("pb", 1),
                  [("lsq", u, kc % 2), ("c", "ones_col")],
                  first=(kc == 0), last=(kc == KC - 1))
        mb.ts(R(0), mb.pb[0][0:1, :], 1.0 / D, None, ALU.mult, None,
              ("lsr", u, 0), [("pb", 0)])
        mb.ts(R(1), mb.pb[1][0:1, :], 1.0 / D, None, ALU.mult, None,
              ("lsr", u, 1), [("pb", 1)])
        mb.act(R(2), R(0), AF.Square, ("lsr", u, 2), [("lsr", u, 0)])
        mb.tt(R(1), R(1), R(2), ALU.subtract, ("lsr", u, 1),
              [("lsr", u, 1), ("lsr", u, 2)])
        mb.act(R(1), R(1), AF.Sqrt, ("lsr", u, 1),
               [("lsr", u, 1), ("c", "eps_col")],
               bias=mb.cs("eps_col")[0:1, 0:1], scale=1.0)
        mb.recip(R(1), R(1), ("lsr", u, 1), [("lsr", u, 1)])
        mb.stt(R(2), R(0), -1.0, R(1), ALU.mult, ALU.mult, ("lsr", u, 2),
               [("lsr", u, 0), ("lsr", u, 1)])
        mb.mm(mb.pb[2][:, :], onesr[:], R(1), ("pb", 2),
              [("lsr", u, 1), ("c", "ones_row")])
        mb.mm(mb.pb[3][:, :], onesr[:], R(2), ("pb", 3),
              [("lsr", u, 2), ("c", "ones_row")])
        mb.copy(ab[:], mb.pb[2][:, :], ("lsa", u), [("pb", 2)])
        mb.copy(bb[:], mb.pb[3][:, :], ("lsb", u), [("pb", 3)])
        for kc in range(KC):
            ap, res = get_piece(kc, tc, 1)
            mb.tt(t1[:], ap, ab[:], ALU.mult, ("lst", u), [res, ("lsa", u)])
            sap, sres, post = sink(kc, tc)
            if gain_col is not None:
                mb.tt(t1[:], t1[:], bb[:], ALU.add, ("lst", u),
                      [("lst", u), ("lsb", u)])
                mb.ts(sap, t1[:], gain_col(kc), bias_col(kc), ALU.mult, ALU.add,
                      sres, [("lst", u)] + list(affine_reads))
            else:
                mb.tt(sap, t1[:], bb[:], ALU.add, sres,
                      [("lst", u), ("lsb", u)])
            post()


def pre_block2(mb: MB):
    p = mb.p
    lncol = mb.sbuf("b2ln", [128, 16], F32)
    p.dma("sync", lncol[:, 0:8], mb.dram["ln2a_g"][:, :], writes=[("lng",)])
    p.dma("sync", lncol[:, 8:16], mb.dram["ln2a_b"][:, :], writes=[("lnb",)])
    b2ac = mb.sbuf("b2ac", [128, 8], F32)
    p.dma("sync", b2ac[:], mb.dram["b2a_out"][:, :], writes=[("b2ac",)])
    fcol = mb.sbuf("b2fc", [128, 2 * (22 + 22 + 8 + 8 + 8 + 1 + 1)], F32)
    FC = {}
    off = 0
    for l in range(2):
        for nm, wdt in (("b1a", 22), ("b1g", 22), ("b2", 8), ("lg", 8),
                        ("lb", 8), ("qs", 1), ("ks", 1)):
            FC[(nm, l)] = fcol[:, off:off + wdt]
            p.dma("sync", fcol[:, off:off + wdt], mb.dram[f"b2f_{nm}{l}"][:, :],
                  writes=[("fc",)])
            off += wdt
    mb.fc2 = FC


def emit_block2(mb: MB):
    p = mb.p
    xf_dram = mb.dtmp("b2_xf", [NSEQ, KC, 128, T], F32)
    xnh_dram = mb.dtmp("b2_xnh", [NSEQ, KC, 128, T], BF16)
    y_dram = mb.dtmp("b2_y", [NSEQ, KC, 128, T], BF16)
    xspill = mb.dtmp("b2_xs", [KC, 128, NSEQ * T], F32)
    sxb_dram = mb.dtmp("b2_sxb", [KC, 128, T], BF16)
    sxm_dram = mb.dtmp("b2_sxm", [KC, 128, T], BF16)
    y2f_dram = mb.dtmp("b2_y2f", [4, 128, T], BF16)
    for nm in (["w2a_qkvT", "w2a_outT"]
               + [f"{b}{l}" for l in range(2) for b in
                  ("w2f_qkv", "w2f_wo", "w2f_w1a", "w2f_w1g", "w2f_w2")]):
        mb.stage_weight_bf16(nm)
    lncol, b2ac, FC = mb.sb["b2ln"], mb.sb["b2ac"], mb.fc2

    for s in range(NSEQ):
        uL = mb.uid()
        xst = [mb.sbuf_scoped(f"b2xst{uL}_{i}", [128, 512], F32) for i in range(2)]

        def xn_sink(kc, tc, s=s, uL=uL):
            i = (kc + tc) % 2
            res = ("xst2", uL, i)

            def post(kc=kc, tc=tc, i=i, res=res):
                p.dma("sync", xf_dram[s, kc, :, tc * 512:(tc + 1) * 512],
                      xst[i][:, :], reads=[res], writes=[("xfd", s, kc)])
            return xst[i][:, :], res, post
        mb.ln_fm(s, xn_sink, add_xn_into_x=False)
        mb.end_stage()
        uL = mb.uid()
        xpc = [mb.sbuf_scoped(f"b2xp{uL}_{i}", [128, 512], F32) for i in range(2)]
        xob = [mb.sbuf_scoped(f"b2xo{uL}_{i}", [128, 512], BF16) for i in range(2)]

        def get_piece(kc, tc, phase, s=s, uL=uL):
            i = (kc + tc + phase) % 2
            res = ("xpc", uL, i)
            p.dma("sync", xpc[i][:, :],
                  xf_dram[s, kc, :, tc * 512:(tc + 1) * 512],
                  reads=[("xfd", s, kc)], writes=[res])
            return xpc[i][:, :], res

        def sink(kc, tc, s=s, uL=uL):
            i = (kc + tc) % 2
            res = ("xob", uL, i)

            def post(kc=kc, tc=tc, i=i, res=res):
                p.dma("sync", xnh_dram[s, kc, :, tc * 512:(tc + 1) * 512],
                      xob[i][:, :], reads=[res], writes=[("xnhd", s, kc)])
            return xob[i][:, :], res, post
        ln_src(mb, get_piece, sink,
               gain_col=lambda kc: lncol[:, kc:kc + 1],
               bias_col=lambda kc: lncol[:, 8 + kc:9 + kc], tag="b2i",
               affine_reads=[("lng",), ("lnb",)])
        mb.end_stage()

        wqkv = mb.wbf["w2a_qkvT"]
        u = mb.uid()
        ropeC_t = mb.load_const_scoped("rope2_C")
        ropeS_t = mb.load_const_scoped("rope2_S")
        wq = [mb.sbuf_scoped(f"b2wq{u}_0", [128, KC, 128], BF16)]
        wk = [mb.sbuf_scoped(f"b2wk{u}_0", [128, KC, 128], BF16)]
        wv = [mb.sbuf_scoped(f"b2wv{u}_0", [128, KC, 128], BF16)]
        xnb = [mb.sbuf_scoped(f"b2xnb{u}_{i}", [128, 1024], BF16) for i in range(2)]
        qp = mb.sbuf_scoped(f"b2qp{u}", [128, 1024], BF16)
        kp = mb.sbuf_scoped(f"b2kp{u}", [128, 1024], BF16)
        vp = mb.sbuf_scoped(f"b2vp{u}", [128, 16, 128], BF16)
        QQ = mb.sbuf_scoped(f"b2QQ{u}", [128, T], BF16)
        KK = mb.sbuf_scoped(f"b2KK{u}", [128, T], BF16)
        lr2 = mb.sbuf_scoped(f"b2lr2{u}", [2, T], BF16)
        pt = mb.sbuf_scoped(f"b2pt{u}", [128, T], BF16)
        rtmp = mb.sbuf_scoped(f"b2rt{u}", [128, 512], BF16)
        ystage = mb.sbuf_scoped(f"b2ys{u}", [128, T], BF16)
        stats = mb.sbuf_scoped(f"b2st{u}", [128, 16], F32)
        lrow = mb.sbuf_scoped(f"b2lr{u}", [128, 48], F32)
        lbf = mb.sbuf_scoped(f"b2lbf{u}", [128, 32], BF16)
        ltr = mb.sbuf_scoped(f"b2ltr{u}", [16, 128], BF16)
        trash = mb.sbuf_scoped(f"b2tr{u}", [128, 512], BF16)
        for pair in range(8):
            for kind, tile, woff in (("q", wq[0], 0), ("k", wk[0], 1024),
                                     ("v", wv[0], 2048)):
                src = wqkv[:, woff + 128 * pair: woff + 128 * pair + 128]
                src = src.rearrange("(kc p) c -> p kc c", p=128)
                p.dma("scalar", tile[:], src, reads=[("wbf", "w2a_qkvT")],
                      writes=[("w2s", kind, u)])
            for half in range(2):
                for kc in range(KC):
                    xb = xnb[kc % 2]
                    p.dma("sync", xb[:],
                          xnh_dram[s, kc, :, half * 1024:half * 1024 + 1024],
                          reads=[("xnhd", s, kc)], writes=[("xnb2", u, kc % 2)])
                    for t2_ in range(2):
                        lc = slice(t2_ * 512, t2_ * 512 + 512)
                        mb.mm(mb.pb[t2_][:, :], wq[0][:, kc, :], xb[:, lc],
                              ("pb", t2_), [("w2s", "q", u), ("xnb2", u, kc % 2)],
                              first=(kc == 0), last=(kc == KC - 1))
                        mb.mm(mb.pb[2 + t2_][:, :], wk[0][:, kc, :], xb[:, lc],
                              ("pb", 2 + t2_), [("w2s", "k", u), ("xnb2", u, kc % 2)],
                              first=(kc == 0), last=(kc == KC - 1))
                    for tt_ in range(8):
                        trows = slice(tt_ * 128, tt_ * 128 + 128)
                        bank = mb.pb[4 + tt_ // 4]
                        bslc = bank[:, (tt_ % 4) * 128:(tt_ % 4) * 128 + 128]
                        mb.mm(bslc, xb[:, trows], wv[0][:, kc, :],
                              ("pb", 4 + tt_ // 4),
                              [("xnb2", u, kc % 2), ("w2s", "v", u)],
                              first=(kc == 0 and tt_ % 4 == 0),
                              last=(kc == KC - 1 and tt_ % 4 == 3))
                for t2_ in range(2):
                    lc = slice(t2_ * 512, t2_ * 512 + 512)
                    mb.act(qp[:, lc], mb.pb[t2_][:, :], AF.Identity, ("qp2", u),
                           [("pb", t2_)], scale=0.125)
                    mb.act(kp[:, lc], mb.pb[2 + t2_][:, :], AF.Identity,
                           ("kp2", u), [("pb", 2 + t2_)])
                for tt_ in range(8):
                    bank = mb.pb[4 + tt_ // 4]
                    bslc = bank[:, (tt_ % 4) * 128:(tt_ % 4) * 128 + 128]
                    mb.act(vp[:, half * 8 + tt_, :], bslc, AF.Identity,
                           ("vp2", u), [("pb", 4 + tt_ // 4)])
                for which, src_, srcres, dst, dres in (
                        ("q", qp, ("qp2", u), QQ, ("QQ2", u)),
                        ("k", kp, ("kp2", u), KK, ("KK2", u))):
                    for t2_ in range(2):
                        lc = slice(t2_ * 512, t2_ * 512 + 512)
                        gcols = slice(half * 1024 + t2_ * 512,
                                      half * 1024 + t2_ * 512 + 512)
                        mb.mm(mb.pb[6][:, :], mb.cs("rope2_Rt")[:], src_[:, lc],
                              ("pb", 6), [srcres, ("c", "rope2_Rt")])
                        mb.tt(rtmp[:, :], src_[:, lc], ropeC_t[:, gcols],
                              ALU.mult, ("rt2", u), [srcres, ("c", "rope2_C")])
                        mb.tt(trash[:, :], mb.pb[6][:, :], ropeS_t[:, gcols],
                              ALU.mult, ("tr2", u), [("pb", 6), ("c", "rope2_S")])
                        mb.tt(dst[:, gcols], rtmp[:, :], trash[:, :], ALU.add,
                              dres, [("rt2", u), ("tr2", u)])
            for hh in range(2):
                rows = slice(64 * hh, 64 * hh + 64)
                for qt in range(16):
                    for cch in range(4):
                        pbs = mb.pb[cch % 2]
                        ccs = slice(cch * 512, cch * 512 + 512)
                        mb.mm(pbs[:, :], QQ[rows, qt * 128:qt * 128 + 128],
                              KK[rows, ccs], ("pb", cch % 2),
                              [("QQ2", u), ("KK2", u)])
                        mb.red(stats[:, cch:cch + 1], pbs[:, :], ALU.max,
                               ("st3", u, cch), [("pb", cch % 2)], negate=True)
                        mb.act(trash[:, :], pbs[:, :], AF.Exp, ("tr2", u),
                               [("pb", cch % 2), ("st3", u, cch)],
                               bias=stats[:, cch:cch + 1],
                               accum=stats[:, 4 + cch:5 + cch],
                               accum_res=("st3", u, 4 + cch))
                    _stats_combine(mb, u, stats, lrow, qt, 4, "st3")
                _lrows_emit(mb, u, lrow, lbf, ltr, lr2[0:1, :], lr2[1:2, :],
                            ("lr22", u))
                yb = {0: 3, 1: 4, 2: 5, 3: 6}
                for jt in range(16):
                    for qc in range(4):
                        pbs = mb.pb[qc % 2]
                        qcs = slice(qc * 512, qc * 512 + 512)
                        mb.mm(pbs[:, :], KK[rows, jt * 128:jt * 128 + 128],
                              QQ[rows, qcs], ("pb", qc % 2),
                              [("QQ2", u), ("KK2", u)], first=True, last=False)
                        mb.mm(pbs[:, :], mb.cs("ones2b")[:, :], lr2[:, qcs],
                              ("pb", qc % 2), [("lr22", u), ("c", "ones2b")],
                              first=False, last=True)
                        mb.act(pt[:, qcs], pbs[:, :], AF.Exp, ("pt2", u, qc),
                               [("pb", qc % 2)])
                        mb.mm(mb.pb[yb[qc]][0:64, :],
                              vp[:, jt, 64 * hh:64 * hh + 64], pt[:, qcs],
                              ("pb", yb[qc]), [("vp2", u), ("pt2", u, qc)],
                              first=(jt == 0), last=(jt == 15))
                for qc in range(4):
                    mb.act(ystage[rows, qc * 512:(qc + 1) * 512],
                           mb.pb[yb[qc]][0:64, :], AF.Identity, ("ys2", u, hh),
                           [("pb", yb[qc])])
            p.dma("sync", y_dram[s, pair, :, :], ystage[:, :],
                  reads=[("ys2", u, 0), ("ys2", u, 1)], writes=[("y2d", s, pair)])
        mb.end_stage()

        u = mb.uid()
        wpk = mb.sbuf_scoped(f"b2wp{u}", [128, KC, 512], BF16)
        ybuf = [mb.sbuf_scoped(f"b2yb{u}_{i}", [128, 512], BF16) for i in range(2)]
        c0 = s * T
        for ocg in range(2):
            src = mb.wbf["w2a_outT"][:, ocg * 512:(ocg + 1) * 512]
            src = src.rearrange("(kc p) c -> p kc c", p=128)
            p.dma("scalar", wpk[:], src, reads=[("wbf", "w2a_outT")],
                  writes=[("wp2", u)])
            for tc in range(4):
                for kc in range(KC):
                    yb_ = ybuf[kc % 2]
                    p.dma("scalar", yb_[:],
                          y_dram[s, kc, :, tc * 512:(tc + 1) * 512],
                          reads=[("y2d", s, kc)], writes=[("yb2", u, kc % 2)])
                    for o4 in range(4):
                        mb.mm(mb.pb[o4][:, :], wpk[:, kc, o4 * 128:(o4 + 1) * 128],
                              yb_[:], ("pb", o4), [("wp2", u), ("yb2", u, kc % 2)],
                              first=(kc == 0), last=(kc == KC - 1))
                for o4 in range(4):
                    oc = ocg * 4 + o4
                    cols = slice(c0 + tc * 512, c0 + tc * 512 + 512)
                    mb.stt(mb.x[oc][:, cols], mb.pb[o4][:, :],
                           b2ac[:, oc:oc + 1], mb.x[oc][:, cols],
                           ALU.add, ALU.add, ("x", oc),
                           [("pb", o4), ("b2ac",), ("x", oc)])
        mb.end_stage()

    for k in range(KC):
        p.dma("sync", xspill[k, :, :], mb.x[k][:], reads=[("x", k)],
              writes=[("xsp", k)])
    mb.free_x()
    for s in range(NSEQ):
        u0 = mb.uid()
        xf_cms = [mb.nc.sbuf_tensor(f"b2xf{u0}_{k}", [128, T + 8], F32)
                  for k in range(KC)]
        xf = [cm.__enter__() for cm in xf_cms]
        for k in range(KC):
            mb.memset(xf[k][:, 0:1], 0.0, ("xf", k))
            p.dma("sync", xf[k][:, 1:T + 1], xf_dram[s, k, :, :],
                  reads=[("xfd", s, k), ("xsp", k), ("xf", k)],
                  writes=[("xf", k)])
        for l in range(2):
            uL = mb.uid()
            sxp = [mb.sbuf_scoped(f"b2sx{uL}_{i}", [128, 512], BF16)
                   for i in range(2)]
            for kc in range(KC):
                base = 1 if kc < 4 else 0
                for tc in range(4):
                    i = (kc + tc) % 2
                    mb.copy(sxp[i][:, :],
                            xf[kc][:, base + tc * 512:base + tc * 512 + 512],
                            ("sxp", uL, i), [("xf", kc)])
                    p.dma("sync", sxb_dram[kc, :, tc * 512:(tc + 1) * 512],
                          sxp[i][:, :], reads=[("sxp", uL, i)],
                          writes=[("sxbd", kc)])
            u = mb.uid()
            xq_t = mb.load_const_scoped("xpq_C")
            xqs_t = mb.load_const_scoped("xpq_S")
            xk_t = mb.load_const_scoped("xpk_C")
            xks_t = mb.load_const_scoped("xpk_S")
            wq = mb.sbuf_scoped(f"b2fwq{u}", [128, KC, 128], BF16)
            wk = mb.sbuf_scoped(f"b2fwk{u}", [128, KC, 128], BF16)
            wv = mb.sbuf_scoped(f"b2fwv{u}", [128, KC, 128], BF16)
            xnb = [mb.sbuf_scoped(f"b2fxn{u}_{i}", [128, 512], BF16)
                   for i in range(2)]
            QQ = mb.sbuf_scoped(f"b2fQ{u}", [128, T], BF16)
            KK = mb.sbuf_scoped(f"b2fK{u}", [128, T], BF16)
            vt = mb.sbuf_scoped(f"b2fv{u}", [128, 16, 128], BF16)
            sqb = mb.sbuf_scoped(f"b2fsq{u}", [128, T], BF16)
            ssr = [mb.sbuf_scoped(f"b2fss{u}_{i}", [1, T], F32) for i in range(2)]
            RB = mb.sbuf_scoped(f"b2fRB{u}", [128, 512], F32)
            qn = mb.sbuf_scoped(f"b2fqn{u}", [128, 512], BF16)
            lr2 = mb.sbuf_scoped(f"b2flr2{u}", [2, T], BF16)
            pt = mb.sbuf_scoped(f"b2fpt{u}", [128, 1024], BF16)
            rtmp = mb.sbuf_scoped(f"b2frt{u}", [128, 512], BF16)
            ystage = mb.sbuf_scoped(f"b2fys{u}", [128, T], BF16)
            stats = mb.sbuf_scoped(f"b2fst{u}", [128, 16], F32)
            lrow = mb.sbuf_scoped(f"b2flr{u}", [128, 48], F32)
            lbf = mb.sbuf_scoped(f"b2flb{u}", [128, 32], BF16)
            ltr = mb.sbuf_scoped(f"b2flt{u}", [16, 128], BF16)
            trash = mb.sbuf_scoped(f"b2ftr{u}", [128, 640], F32)
            for pair in range(4):
                for tile, woff in ((wq, 0), (wk, 512), (wv, 1024)):
                    src = mb.wbf[f"w2f_qkv{l}"][:, woff + 128 * pair:
                                                woff + 128 * pair + 128]
                    src = src.rearrange("(kc p) c -> p kc c", p=128)
                    p.dma("scalar", tile[:], src,
                          reads=[("wbf", f"w2f_qkv{l}")], writes=[("w2f", id(tile), u)])
                for qr in range(4):
                    for kc in range(KC):
                        xb = xnb[kc % 2]
                        p.dma("sync", xb[:],
                              sxb_dram[kc, :, qr * 512:qr * 512 + 512],
                              reads=[("sxbd", kc)], writes=[("fxn", u, kc % 2)])
                        mb.mm(mb.pb[0][:, :], wq[:, kc, :], xb[:], ("pb", 0),
                              [("w2f", id(wq), u), ("fxn", u, kc % 2)],
                              first=(kc == 0), last=(kc == KC - 1))
                        mb.mm(mb.pb[1][:, :], wk[:, kc, :], xb[:], ("pb", 1),
                              [("w2f", id(wk), u), ("fxn", u, kc % 2)],
                              first=(kc == 0), last=(kc == KC - 1))
                        for tt_ in range(4):
                            bslc = mb.pb[2][:, tt_ * 128:tt_ * 128 + 128]
                            mb.mm(bslc, xb[:, tt_ * 128:tt_ * 128 + 128],
                                  wv[:, kc, :], ("pb", 2),
                                  [("fxn", u, kc % 2), ("w2f", id(wv), u)],
                                  first=(kc == 0 and tt_ == 0),
                                  last=(kc == KC - 1 and tt_ == 3))
                    lc = slice(qr * 512, qr * 512 + 512)
                    mb.act(QQ[:, lc], mb.pb[0][:, :], AF.Identity, ("fQr", u),
                           [("pb", 0)])
                    mb.act(KK[:, lc], mb.pb[1][:, :], AF.Identity, ("fKr", u),
                           [("pb", 1)])
                    mb.act(vt[:, 4 * qr:4 * qr + 4, :], mb.pb[2][:, :],
                           AF.Identity, ("fv2", u), [("pb", 2)])
                for which, SRC, sres, dres, xc, xs, xcn, xsn, sccol, qksc in (
                        ("q", QQ, ("fQr", u), ("fQ2", u), xq_t, xqs_t,
                         "xpq_C", "xpq_S", FC[("qs", l)], 8.0),
                        ("k", KK, ("fKr", u), ("fK2", u), xk_t, xks_t,
                         "xpk_C", "xpk_S", FC[("ks", l)], 1.0)):
                    mb.act(sqb[:, :], SRC[:, :], AF.Square, ("fsq", u), [sres])
                    for hh, ocol, ss_ in ((0, "top64colb", ssr[0]),
                                          (1, "bot64colb", ssr[1])):
                        for cch in range(4):
                            mb.mm(mb.pb[3][0:1, :], mb.cs(ocol)[:],
                                  sqb[:, cch * 512:(cch + 1) * 512], ("pb", 3),
                                  [("fsq", u), ("c", ocol)])
                            mb.copy(ss_[0:1, cch * 512:(cch + 1) * 512],
                                    mb.pb[3][0:1, :], ("fss", u, hh), [("pb", 3)])
                        mb.act(ss_[0:1, :], ss_[0:1, :], AF.Sqrt, ("fss", u, hh),
                               [("fss", u, hh)])
                        mb.ts(ss_[0:1, :], ss_[0:1, :], 1.0e-12, None, ALU.max,
                              None, ("fss", u, hh), [("fss", u, hh)])
                        mb.recip(ss_[0:1, :], ss_[0:1, :], ("fss", u, hh),
                                 [("fss", u, hh)])
                    for cch in range(4):
                        ccs = slice(cch * 512, cch * 512 + 512)
                        mb.mm(mb.pb[4][:, :], mb.cs("top64_row")[:],
                              ssr[0][0:1, ccs], ("pb", 4),
                              [("fss", u, 0), ("c", "top64_row")],
                              first=True, last=False)
                        mb.mm(mb.pb[4][:, :], mb.cs("bot64_row")[:],
                              ssr[1][0:1, ccs], ("pb", 4),
                              [("fss", u, 1), ("c", "bot64_row")],
                              first=False, last=True)
                        mb.ts(RB[:, :], mb.pb[4][:, :], sccol[:, 0:1], qksc,
                              ALU.mult, ALU.mult, ("fRB", u),
                              [("pb", 4), ("fc",)])
                        mb.tt(qn[:, :], SRC[:, ccs], RB[:, :], ALU.mult,
                              ("fqn", u), [sres, ("fRB", u)])
                        mb.mm(mb.pb[5][:, :], mb.cs("rope2_Rt")[:], qn[:, :],
                              ("pb", 5), [("fqn", u), ("c", "rope2_Rt")])
                        mb.tt(rtmp[:, :], qn[:, :], xc[:, ccs], ALU.mult,
                              ("frt", u), [("fqn", u), ("c", xcn)])
                        mb.tt(trash[:, 0:512], mb.pb[5][:, :], xs[:, ccs],
                              ALU.mult, ("ftr", u), [("pb", 5), ("c", xsn)])
                        mb.tt(SRC[:, ccs], rtmp[:, :], trash[:, 0:512],
                              ALU.add, dres, [("frt", u), ("ftr", u)])
                for hh in range(2):
                    rows = slice(64 * hh, 64 * hh + 64)
                    for qt in range(16):
                        kvs = max(0, qt * 128 - 512)
                        w_all = qt * 128 + 128 - kvs
                        nch = cdiv(w_all, 512)
                        for cch in range(nch):
                            off = kvs + cch * 512
                            w = min(512, qt * 128 + 128 - off)
                            pbs = mb.pb[cch % 2]
                            mb.mm(pbs[:, 0:w], QQ[rows, qt * 128:qt * 128 + 128],
                                  KK[rows, off:off + w], ("pb", cch % 2),
                                  [("fQ2", u), ("fK2", u)])
                            if off == kvs and qt >= 4:
                                mb.tt(pbs[:, 0:128], pbs[:, 0:128],
                                      mb.cs("mask_add_fl")[:, :], ALU.add,
                                      ("pb", cch % 2),
                                      [("pb", cch % 2), ("c", "mask_add_fl")])
                            if off + w == qt * 128 + 128:
                                mb.tt(pbs[:, w - 128:w], pbs[:, w - 128:w],
                                      mb.cs("mask_add_qj")[:, :], ALU.add,
                                      ("pb", cch % 2),
                                      [("pb", cch % 2), ("c", "mask_add_qj")])
                            mb.red(stats[:, cch:cch + 1], pbs[:, 0:w], ALU.max,
                                   ("st4", u, cch), [("pb", cch % 2)],
                                   negate=True)
                            mb.act(trash[:, 0:w], pbs[:, 0:w], AF.Exp,
                                   ("ftr", u), [("pb", cch % 2), ("st4", u, cch)],
                                   bias=stats[:, cch:cch + 1],
                                   accum=stats[:, 4 + cch:5 + cch],
                                   accum_res=("st4", u, 4 + cch))
                        _stats_combine(mb, u, stats, lrow, qt, nch, "st4")
                    _lrows_emit(mb, u, lrow, lbf, ltr, lr2[0:1, :], lr2[1:2, :],
                                ("flr2", u))
                    for jt in range(16):
                        ci0 = jt // 4
                        ci1 = min(3, (jt * 128 + 639) // 512)
                        qs_ = 512 * ci0
                        ptres = []
                        for ci4 in range(ci0, ci1 + 1):
                            coff = 512 * ci4
                            pbs = mb.pb[ci4 % 2]
                            mb.mm(pbs[:, :], KK[rows, jt * 128:jt * 128 + 128],
                                  QQ[rows, coff:coff + 512], ("pb", ci4 % 2),
                                  [("fQ2", u), ("fK2", u)], first=True, last=False)
                            mb.mm(pbs[:, :], mb.cs("ones2b")[:, :],
                                  lr2[:, coff:coff + 512], ("pb", ci4 % 2),
                                  [("flr2", u), ("c", "ones2b")],
                                  first=False, last=True)
                            for st in range(4):
                                o_t = coff + 128 * st - jt * 128
                                psl = pbs[:, 128 * st:128 * st + 128]
                                ptsl = pt[:, coff - qs_ + 128 * st:
                                          coff - qs_ + 128 * st + 128]
                                if o_t < 0 or o_t > 512:
                                    mb.memset(ptsl, 0.0, ("fpt", u, ci4, st))
                                    ptres.append(("fpt", u, ci4, st))
                                    continue
                                if o_t == 0:
                                    mb.tt(psl, psl, mb.cs("mask_add_fl")[:, :],
                                          ALU.add, ("pb", ci4 % 2),
                                          [("pb", ci4 % 2), ("c", "mask_add_fl")])
                                elif o_t == 512:
                                    mb.tt(psl, psl, mb.cs("mask_add_qj")[:, :],
                                          ALU.add, ("pb", ci4 % 2),
                                          [("pb", ci4 % 2), ("c", "mask_add_qj")])
                                mb.act(ptsl, psl, AF.Exp, ("fpt", u, ci4, st),
                                       [("pb", ci4 % 2)])
                                ptres.append(("fpt", u, ci4, st))
                        for ci4 in range(ci0, ci1 + 1):
                            bank = mb.pb[{0: 3, 1: 4, 2: 5, 3: 6}[ci4]]
                            mb.mm(bank[0:64, :],
                                  vt[:, jt, 64 * hh:64 * hh + 64],
                                  pt[:, 512 * ci4 - qs_:512 * ci4 - qs_ + 512],
                                  ("pb", {0: 3, 1: 4, 2: 5, 3: 6}[ci4]),
                                  [("fv2", u)] + ptres,
                                  first=(jt == max(0, 4 * ci4 - 4)),
                                  last=(jt == min(15, 4 * ci4 + 3)))
                    for ci4 in range(4):
                        mb.act(ystage[rows, ci4 * 512:(ci4 + 1) * 512],
                               mb.pb[{0: 3, 1: 4, 2: 5, 3: 6}[ci4]][0:64, :],
                               AF.Identity, ("fys2", u, hh),
                               [("pb", {0: 3, 1: 4, 2: 5, 3: 6}[ci4])])
                p.dma("sync", y2f_dram[pair, :, :], ystage[:, :],
                      reads=[("fys2", u, 0), ("fys2", u, 1)],
                      writes=[("y2fd", pair)])
            mb.end_stage()
            u = mb.uid()
            wok = mb.sbuf_scoped(f"b2fwo{u}", [128, 4, 512], BF16)
            ybuf = [mb.sbuf_scoped(f"b2fyb{u}_{i}", [128, 512], BF16)
                    for i in range(2)]
            for ocg in range(2):
                src = mb.wbf[f"w2f_wo{l}"][:, ocg * 512:(ocg + 1) * 512]
                src = src.rearrange("(hc p) c -> p hc c", p=128)
                p.dma("scalar", wok[:], src, reads=[("wbf", f"w2f_wo{l}")],
                      writes=[("fwo", u)])
                for tc in range(4):
                    for hc in range(4):
                        yb_ = ybuf[hc % 2]
                        p.dma("scalar", yb_[:],
                              y2f_dram[hc, :, tc * 512:(tc + 1) * 512],
                              reads=[("y2fd", hc)], writes=[("fyb", u, hc % 2)])
                        for o4 in range(4):
                            mb.mm(mb.pb[o4][:, :],
                                  wok[:, hc, o4 * 128:(o4 + 1) * 128], yb_[:],
                                  ("pb", o4), [("fwo", u), ("fyb", u, hc % 2)],
                                  first=(hc == 0), last=(hc == 3))
                    for o4 in range(4):
                        oc = ocg * 4 + o4
                        cols = slice(1 + tc * 512, 1 + tc * 512 + 512)
                        mb.stt(xf[oc][:, cols], mb.pb[o4][:, :], 1.0,
                               xf[oc][:, cols], ALU.mult, ALU.add, ("xf", oc),
                               [("pb", o4), ("xf", oc)])
            mb.end_stage()
            u = mb.uid()
            sxm = mb.sbuf_scoped(f"b2sm{u}", [128, KC, T], BF16)

            def get_piece(kc, tc, phase, u=u):
                base = 1 if kc < 4 else 0
                return (xf[kc][:, base + tc * 512:base + tc * 512 + 512],
                        ("xf", kc))

            def sink(kc, tc, u=u):
                return (sxm[:, kc, tc * 512:(tc + 1) * 512], ("sxm", u, kc),
                        lambda: None)
            ln_src(mb, get_piece, sink,
                   gain_col=lambda kc, l=l: FC[("lg", l)][:, kc:kc + 1],
                   bias_col=lambda kc, l=l: FC[("lb", l)][:, kc:kc + 1],
                   tag="b2m", affine_reads=[("fc",)])
            w2h = mb.sbuf_scoped(f"b2w2{u}", [128, 22, 512], BF16)
            w1s = [mb.sbuf_scoped(f"b2w1{u}_{i}", [128, KC, 128], BF16)
                   for i in range(2)]
            ag = mb.sbuf_scoped(f"b2ag{u}", [128, 22, 1024], BF16)
            gb = mb.sbuf_scoped(f"b2gb{u}", [128, 512], BF16)
            for halftc in range(2):
                tcs = [2 * halftc, 2 * halftc + 1]
                for hc in range(22):
                    for iw, wn in ((0, f"w2f_w1a{l}"), (1, f"w2f_w1g{l}")):
                        src = mb.wbf[wn][:, hc * 128:(hc + 1) * 128]
                        src = src.rearrange("(kc p) c -> p kc c", p=128)
                        p.dma("scalar", w1s[iw][:], src, reads=[("wbf", wn)],
                              writes=[("w1s2", u, iw)])
                    for itc, tc in enumerate(tcs):
                        for kc in range(KC):
                            mb.mm(mb.pb[0][:, :], w1s[0][:, kc, :],
                                  sxm[:, kc, tc * 512:(tc + 1) * 512], ("pb", 0),
                                  [("w1s2", u, 0), ("sxm", u, kc)],
                                  first=(kc == 0), last=(kc == KC - 1))
                            mb.mm(mb.pb[1][:, :], w1s[1][:, kc, :],
                                  sxm[:, kc, tc * 512:(tc + 1) * 512], ("pb", 1),
                                  [("w1s2", u, 1), ("sxm", u, kc)],
                                  first=(kc == 0), last=(kc == KC - 1))
                        mb.act(gb[:, :], mb.pb[1][:, :], AF.Gelu, ("gb2", u),
                               [("pb", 1), ("fc",)],
                               bias=FC[("b1g", l)][:, hc:hc + 1])
                        acol = slice(itc * 512, itc * 512 + 512)
                        mb.act(ag[:, hc, acol], mb.pb[0][:, :], AF.Identity,
                               ("ag2", u, hc), [("pb", 0), ("fc",)],
                               bias=FC[("b1a", l)][:, hc:hc + 1])
                        mb.tt(ag[:, hc, acol], ag[:, hc, acol], gb[:, :],
                              ALU.mult, ("ag2", u, hc),
                              [("ag2", u, hc), ("gb2", u)])
                for ocg in range(2):
                    src = mb.wbf[f"w2f_w2{l}"][:, ocg * 512:(ocg + 1) * 512]
                    src = src.rearrange("(hc p) c -> p hc c", p=128)
                    p.dma("scalar", w2h[:], src, reads=[("wbf", f"w2f_w2{l}")],
                          writes=[("w2w2", u)])
                    for itc, tc in enumerate(tcs):
                        acol = slice(itc * 512, itc * 512 + 512)
                        for o4 in range(4):
                            for hc in range(22):
                                mb.mm(mb.pb[3 + o4][:, :],
                                      w2h[:, hc, o4 * 128:o4 * 128 + 128],
                                      ag[:, hc, acol], ("pb", 3 + o4),
                                      [("w2w2", u)] + [("ag2", u, hc)],
                                      first=(hc == 0), last=(hc == 21))
                        for o4 in range(4):
                            oc = ocg * 4 + o4
                            cols = slice(1 + tc * 512, 1 + tc * 512 + 512)
                            mb.stt(xf[oc][:, cols], mb.pb[3 + o4][:, :],
                                   FC[("b2", l)][:, oc:oc + 1], xf[oc][:, cols],
                                   ALU.add, ALU.add, ("xf", oc),
                                   [("pb", 3 + o4), ("fc",), ("xf", oc)])
            mb.end_stage()
        uE = mb.uid()
        xeb = [mb.sbuf_scoped(f"b2xe{uE}_{i}", [128, 512], F32) for i in range(2)]
        for kc in range(KC):
            for tc in range(4):
                i = (kc + tc) % 2
                mb.copy(xeb[i][:, :], xf[kc][:, 1 + tc * 512:1 + tc * 512 + 512],
                        ("xeb", uE, i), [("xf", kc)])
                p.dma("sync", xf_dram[s, kc, :, tc * 512:(tc + 1) * 512],
                      xeb[i][:, :], reads=[("xeb", uE, i)], writes=[("xfd2", s, kc)])
        mb.end_stage()
        for cm in reversed(xf_cms):
            cm.__exit__(None, None, None)
    mb.realloc_x()
    for k in range(KC):
        p.dma("sync", mb.x[k][:], xspill[k, :, :], reads=[("xsp", k)],
              writes=[("x", k)])
    uR = mb.uid()
    xrb = [mb.sbuf_scoped(f"b2xr{uR}_{i}", [128, 512], F32) for i in range(2)]
    for s in range(NSEQ):
        for kc in range(KC):
            for tc in range(4):
                i = (kc + tc) % 2
                p.dma("sync", xrb[i][:, :],
                      xf_dram[s, kc, :, tc * 512:(tc + 1) * 512],
                      reads=[("xfd2", s, kc)], writes=[("xrb", uR, i)])
                cols = slice(s * T + tc * 512, s * T + tc * 512 + 512)
                mb.tt(mb.x[kc][:, cols], mb.x[kc][:, cols], xrb[i][:, :],
                      ALU.add, ("x", kc), [("x", kc), ("xrb", uR, i)])
    mb.end_stage()



def pre_block3(mb: MB):
    p = mb.p
    t = mb.sbuf("b3bo", [128, 8], F32)
    p.dma("sync", t[:], mb.dram["b3_bo"][:, :], writes=[("b3bo",)])
    t = mb.sbuf("b3g1", [128, 2], F32)
    p.dma("sync", t[:], mb.dram["b3_bg1"][:, :], writes=[("b3g1",)])
    t = mb.sbuf("b3bt", [1, 1], F32)
    p.dma("sync", t[:], mb.dram["b3_beta"][:, :], writes=[("b3bt",)])


def emit_block3(mb: MB):
    p = mb.p
    xn_dram = mb.dtmp("b3_xn", [NSEQ, KC, 128, T], BF16)
    qtmp = mb.dtmp("b3_qt", [16, 128, 1024], F32)
    ktmp = mb.dtmp("b3_kt", [16, 128, 1024], F32)
    eqd = mb.dtmp("b3_eqd", [KC, 128, T], BF16)
    ekd = mb.dtmp("b3_ekd", [KC, 128, T], BF16)
    att_dram = mb.dtmp("b3_att", [NSEQ, KC, 128, T], F32)
    ffn_dram = mb.dtmp("b3_ffn", [NSEQ, KC, 128, T], F32)
    xspill = mb.dtmp("b3_xs", [KC, 128, NSEQ * T], F32)
    for nm in (["w3_wqT", "w3_wkT", "w3_wvT", "w3_woT", "w3_wg1", "w3_wg2"]
               + [f"w3_win{e}" for e in range(8)]
               + [f"w3_wout{e}" for e in range(8)]):
        mb.stage_weight_bf16(nm)
    for s in range(NSEQ):
        uL = mb.uid()
        xst = [mb.sbuf_scoped(f"b3xst{uL}_{i}", [128, 512], BF16) for i in range(2)]

        def xn_sink(kc, tc, s=s, uL=uL):
            i = (kc + tc) % 2

            def post(kc=kc, tc=tc, i=i):
                p.dma("sync", xn_dram[s, kc, :, tc * 512:(tc + 1) * 512],
                      xst[i][:, :], reads=[("xst3", uL, i)],
                      writes=[("xnd3", s, kc)])
            return xst[i][:, :], ("xst3", uL, i), post
        mb.ln_fm(s, xn_sink, add_xn_into_x=False)
        mb.end_stage()
    for k in range(KC):
        p.dma("sync", xspill[k, :, :], mb.x[k][:], reads=[("x", k)],
              writes=[("xsp3", k)])
    mb.free_x()

    for s in range(NSEQ):
        u = mb.uid()
        ll_cms = [mb.nc.sbuf_tensor(f"b3vt{u}", [128, 16, 1024], BF16),
                  mb.nc.sbuf_tensor(f"b3bvb{u}", [128, 1024], F32),
                  mb.nc.sbuf_tensor(f"b3qs{u}", [128, 1024], F32),
                  mb.nc.sbuf_tensor(f"b3ks{u}", [128, 1024], F32),
                  mb.nc.sbuf_tensor(f"b3mx{u}", [128, 40], F32),
                  mb.nc.sbuf_tensor(f"b3be{u}", [128, 4], F32)]
        v_tm, bvb, qsb, ksb, mxc, becol = [cm.__enter__() for cm in ll_cms]
        xnf = mb.sbuf_scoped(f"b3xn{u}", [128, KC, T], BF16)
        for kc in range(KC):
            p.dma("sync", xnf[:, kc, :], xn_dram[s, kc, :, :],
                  reads=[("xnd3", s, kc)], writes=[("xnf", u)])
        wq3 = mb.sbuf_scoped(f"b3wq{u}", [128, KC, 1024], BF16)
        wk3 = mb.sbuf_scoped(f"b3wk{u}", [128, KC, 1024], BF16)
        wv3 = mb.sbuf_scoped(f"b3wv{u}", [128, KC, 1024], BF16)
        for tile, wn in ((wq3, "w3_wqT"), (wk3, "w3_wkT"), (wv3, "w3_wvT")):
            src = mb.wbf[wn][:, :].rearrange("(kc p) c -> p kc c", p=128)
            p.dma("scalar", tile[:], src, reads=[("wbf", wn)],
                  writes=[("w3", wn, u)])
        brow = mb.sbuf_scoped(f"b3br{u}", [1, 1024], F32)
        bqb = mb.sbuf_scoped(f"b3bqb{u}", [128, 1024], F32)
        bkb = mb.sbuf_scoped(f"b3bkb{u}", [128, 1024], F32)
        for bn, bt in (("b3_bq", bqb), ("b3_bk", bkb), ("b3_bv", bvb)):
            p.dma("sync", brow[:, :], mb.dram[bn][:, :], writes=[("brow", u)])
            for c2 in range(2):
                mb.mm(mb.pb[5][:, :], mb.cs("ones_row")[:],
                      brow[0:1, c2 * 512:(c2 + 1) * 512], ("pb", 5),
                      [("brow", u), ("c", "ones_row")])
                mb.copy(bt[:, c2 * 512:(c2 + 1) * 512], mb.pb[5][:, :],
                        ("bb3", bn, u), [("pb", 5)])
        for tt_ in range(16):
            tsl = slice(tt_ * 128, tt_ * 128 + 128)
            for oc2 in range(2):
                osl = slice(oc2 * 512, oc2 * 512 + 512)
                for kc in range(KC):
                    mb.mm(mb.pb[0][:, :], xnf[:, kc, tsl], wq3[:, kc, osl],
                          ("pb", 0), [("xnf", u), ("w3", "w3_wqT", u)],
                          first=(kc == 0), last=(kc == KC - 1))
                    mb.mm(mb.pb[1][:, :], xnf[:, kc, tsl], wk3[:, kc, osl],
                          ("pb", 1), [("xnf", u), ("w3", "w3_wkT", u)],
                          first=(kc == 0), last=(kc == KC - 1))
                    mb.mm(mb.pb[2][:, :], xnf[:, kc, tsl], wv3[:, kc, osl],
                          ("pb", 2), [("xnf", u), ("w3", "w3_wvT", u)],
                          first=(kc == 0), last=(kc == KC - 1))
                mb.stt(qsb[:, osl], mb.pb[0][:, :], 1.0, bqb[:, osl],
                       ALU.mult, ALU.add, ("qsb", u),
                       [("pb", 0), ("bb3", "b3_bq", u)])
                mb.stt(ksb[:, osl], mb.pb[1][:, :], 1.0, bkb[:, osl],
                       ALU.mult, ALU.add, ("ksb", u),
                       [("pb", 1), ("bb3", "b3_bk", u)])
                mb.act(v_tm[:, tt_, osl], mb.pb[2][:, :], AF.Identity,
                       ("vtm", u), [("pb", 2)])
            mb.red(mxc[:, tt_:tt_ + 1], qsb[:, :], ALU.max, ("mxq", u, tt_),
                   [("qsb", u)])
            mb.red(mxc[:, 16 + tt_:17 + tt_], ksb[:, :], ALU.max,
                   ("mxk", u, tt_), [("ksb", u)])
            p.dma("sync", qtmp[tt_, :, :], qsb[:, :], reads=[("qsb", u)],
                  writes=[("qtd", u, tt_)])
            p.dma("sync", ktmp[tt_, :, :], ksb[:, :], reads=[("ksb", u)],
                  writes=[("ktd", u, tt_)])
        MX = lambda j: mxc[:, 32 + j:33 + j]
        mb.red(MX(0), mxc[:, 0:16], ALU.max, ("mx2", u, 0),
               [("mxq", u, i) for i in range(16)])
        mb.red(MX(1), mxc[:, 16:32], ALU.max, ("mx2", u, 1),
               [("mxk", u, i) for i in range(16)])
        mb.p.do("gpsimd", lambda h, a=mxc[0:1, 35:36], b=MX(0): h.tensor_reduce(
            a, b, axis=AX.C, op=ALU.max),
            reads=[("mx2", u, 0)], writes=[("mx2", u, 3)])
        mb.p.do("gpsimd", lambda h, a=mxc[0:1, 36:37], b=MX(1): h.tensor_reduce(
            a, b, axis=AX.C, op=ALU.max),
            reads=[("mx2", u, 1)], writes=[("mx2", u, 4)])
        mb.act(mxc[0:1, 37:38], mb.sb["b3bt"][0:1, 0:1], AF.Exp, ("mx2", u, 5),
               [("b3bt",)])
        for j, dst in ((0, 38), (1, 39)):
            mb.mm(mb.pb[5][:, 0:1], mb.cs("ones_row")[:],
                  mxc[0:1, 35 + j:36 + j], ("pb", 5),
                  [("mx2", u, 3 + j), ("c", "ones_row")])
            mb.copy(mxc[:, dst:dst + 1], mb.pb[5][:, 0:1], ("mx2", u, 6 + j),
                    [("pb", 5)])
        mb.mm(mb.pb[5][:, 0:1], mb.cs("ones_row")[:], mxc[0:1, 37:38],
              ("pb", 5), [("mx2", u, 5), ("c", "ones_row")])
        mb.copy(becol[:, 0:1], mb.pb[5][:, 0:1], ("bec", u), [("pb", 5)])
        mb.tt(becol[:, 1:2], becol[:, 0:1], mxc[:, 38:39], ALU.mult,
              ("bec", u), [("bec", u), ("mx2", u, 6)])
        mb.ts(becol[:, 1:2], becol[:, 1:2], -1.0, None, ALU.mult, None,
              ("bec", u), [("bec", u)])
        mb.tt(becol[:, 2:3], becol[:, 0:1], mxc[:, 39:40], ALU.mult,
              ("bec", u), [("bec", u), ("mx2", u, 7)])
        mb.ts(becol[:, 2:3], becol[:, 2:3], -1.0, None, ALU.mult, None,
              ("bec", u), [("bec", u)])
        mb.end_stage()
        ektm = mb.sbuf_scoped(f"b3ektm{u}", [128, 16, 1024], BF16)
        etf = mb.sbuf_scoped(f"b3etf{u}", [128, 128], BF16)
        v2 = mb.sbuf_scoped(f"b3v2{u}", [128, 16, 1088], BF16)
        et = mb.sbuf_scoped(f"b3et{u}", [128, 1024], BF16)
        zw = mb.sbuf_scoped(f"b3zw{u}", [128, 128], F32)
        for tt_ in range(16):
            p.dma("sync", qsb[:, :], qtmp[tt_, :, :], reads=[("qtd", u, tt_)],
                  writes=[("qsb", u)])
            mb.act(et[:, :], qsb[:, :], AF.Exp, ("et", u),
                   [("qsb", u), ("bec", u)], scale=becol[:, 0:1],
                   bias=becol[:, 1:2])
            for fc in range(KC):
                mb.pe_transpose(mb.pb["t"][:, 0:128],
                                et[:, fc * 128:fc * 128 + 128], ("pbt",),
                                [("et", u)])
                mb.copy(etf[:, :], mb.pb["t"][:, 0:128], ("etf", u), [("pbt",)])
                p.dma("sync", eqd[fc, :, tt_ * 128:tt_ * 128 + 128], etf[:, :],
                      reads=[("etf", u)], writes=[("eqd", u)])
            p.dma("sync", ksb[:, :], ktmp[tt_, :, :], reads=[("ktd", u, tt_)],
                  writes=[("ksb", u)])
            mb.act(et[:, :], ksb[:, :], AF.Exp, ("et", u),
                   [("ksb", u), ("bec", u)], scale=becol[:, 0:1],
                   bias=becol[:, 2:3])
            mb.copy(ektm[:, tt_, :], et[:, :], ("ektm", u), [("et", u)])
            for fc in range(KC):
                mb.pe_transpose(mb.pb["t"][:, 0:128],
                                et[:, fc * 128:fc * 128 + 128], ("pbt",),
                                [("et", u)])
                mb.copy(etf[:, :], mb.pb["t"][:, 0:128], ("etf", u), [("pbt",)])
                p.dma("sync", ekd[fc, :, tt_ * 128:tt_ * 128 + 128], etf[:, :],
                      reads=[("etf", u)], writes=[("ekd", u)])
            mb.red(zw[:, 0:64],
                   et[:, :].rearrange("p (h d) -> p h d", d=16), ALU.add,
                   ("zw", u), [("et", u)])
            mb.recip(zw[:, 0:64], zw[:, 0:64], ("zw", u), [("zw", u)])
            mb.copy(v2[:, tt_, :].rearrange("p (h c) -> p h c", c=17)[:, :, 0:1],
                    zw[:, 0:64], ("v2", u), [("zw", u)])
            for h_ in range(64):
                mb.ts(v2[:, tt_, 17 * h_ + 1:17 * h_ + 17],
                      v_tm[:, tt_, 16 * h_:16 * h_ + 16],
                      zw[:, h_:h_ + 1], None, ALU.mult, None, ("v2", u),
                      [("vtm", u), ("zw", u)])
        eqh = mb.sbuf_scoped(f"b3eqh{u}", [16, T], BF16)
        ekh = mb.sbuf_scoped(f"b3ekh{u}", [16, T], BF16)
        kvs = mb.sbuf_scoped(f"b3kvs{u}", [16, 17], BF16)
        kvf = mb.sbuf_scoped(f"b3kvf{u}", [16, 17], F32)
        stm = mb.sbuf_scoped(f"b3stm{u}", [128, 256], BF16)
        rc = mb.sbuf_scoped(f"b3rc{u}", [128, 2], F32)
        otm = mb.sbuf_scoped(f"b3otm{u}", [128, 16, 1024], BF16)
        for h_ in range(64):
            fc, ro = h_ // 8, 16 * (h_ % 8)
            p.dma("sync", eqh[:, :], eqd[fc, ro:ro + 16, :], reads=[("eqd", u)],
                  writes=[("eqh", u)])
            p.dma("sync", ekh[:, :], ekd[fc, ro:ro + 16, :], reads=[("ekd", u)],
                  writes=[("ekh", u)])
            mb.memset(kvf[:, :], 0.0, ("kvf", u))
            for c in range(16):
                csl = slice(c * 128, c * 128 + 128)
                if c > 0:
                    mb.copy(kvs[:, :], kvf[:, :], ("kvs", u), [("kvf", u)])
                if c < 15:
                    mb.mm(mb.pb[6][0:16, 0:17],
                          ektm[:, c, 16 * h_:16 * h_ + 16],
                          v2[:, c, 17 * h_:17 * h_ + 17], ("pb", 6),
                          [("ektm", u), ("v2", u)])
                    mb.tt(kvf[:, :], kvf[:, :], mb.pb[6][0:16, 0:17], ALU.add,
                          ("kvf", u), [("kvf", u), ("pb", 6)])
                pbs = mb.pb[c % 2]
                mb.mm(pbs[:, 0:128], ekh[:, csl], eqh[:, csl], ("pb", c % 2),
                      [("ekh", u), ("eqh", u)])
                sts = stm[:, (c % 2) * 128:(c % 2) * 128 + 128]
                mb.tt(sts, pbs[:, 0:128], mb.cs("mask_keep_jq")[:, :],
                      ALU.mult, ("stm", u, c % 2),
                      [("pb", c % 2), ("c", "mask_keep_jq")])
                pby = mb.pb[2 + c % 2]
                mb.mm(pby[:, 0:17], sts, v2[:, c, 17 * h_:17 * h_ + 17],
                      ("pb", 2 + c % 2), [("stm", u, c % 2), ("v2", u)],
                      first=True, last=(c == 0))
                if c > 0:
                    mb.mm(pby[:, 0:17], eqh[:, csl], kvs[:, :],
                          ("pb", 2 + c % 2), [("eqh", u), ("kvs", u)],
                          first=False, last=True)
                mb.recip(rc[:, c % 2:c % 2 + 1], pby[:, 0:1], ("rc", u, c % 2),
                         [("pb", 2 + c % 2)])
                mb.stt(otm[:, c, 16 * h_:16 * h_ + 16],
                       pby[:, 1:17], rc[:, c % 2:c % 2 + 1],
                       bvb[:, 16 * h_:16 * h_ + 16],
                       ALU.mult, ALU.add, ("otm", u, h_),
                       [("pb", 2 + c % 2), ("rc", u, c % 2),
                        ("bb3", "b3_bv", u)])
        wo3 = mb.sbuf_scoped(f"b3wo{u}", [128, KC, 1024], BF16)
        src = mb.wbf["w3_woT"][:, :].rearrange("(kc p) c -> p kc c", p=128)
        p.dma("scalar", wo3[:], src, reads=[("wbf", "w3_woT")],
              writes=[("wo3", u)])
        ofm = mb.sbuf_scoped(f"b3ofm{u}", [128, KC, 128], BF16)
        attb = mb.sbuf_scoped(f"b3ab{u}", [128, KC, 128], F32)
        ores = [("otm", u, h_) for h_ in range(64)]
        for tt_ in range(16):
            for fc in range(KC):
                mb.pe_transpose(mb.pb["t"][:, 0:128],
                                otm[:, tt_, fc * 128:fc * 128 + 128], ("pbt",),
                                ores)
                mb.copy(ofm[:, fc, :], mb.pb["t"][:, 0:128], ("ofm", u, fc),
                        [("pbt",)])
            for oc in range(KC):
                for kc in range(KC):
                    mb.mm(mb.pb[oc % 2][:, 0:128], wo3[:, kc, oc * 128:oc * 128 + 128],
                          ofm[:, kc, :], ("pb", oc % 2),
                          [("wo3", u), ("ofm", u, kc)],
                          first=(kc == 0), last=(kc == KC - 1))
                mb.ts(attb[:, oc, :], mb.pb[oc % 2][:, 0:128],
                      mb.sb["b3bo"][:, oc:oc + 1], None, ALU.add, None,
                      ("attb", u), [("pb", oc % 2), ("b3bo",)])
            p.dma("sync",
                  att_dram[s, :, :, tt_ * 128:tt_ * 128 + 128].rearrange(
                      "kc p t -> p kc t"),
                  attb[:, :, :], reads=[("attb", u)], writes=[("attd", s)])
        mb.end_stage()
        for cm in reversed(ll_cms):
            cm.__exit__(None, None, None)

        u = mb.uid()
        wg1s = mb.sbuf_scoped(f"b3wg1{u}", [128, KC, 256], BF16)
        src = mb.wbf["w3_wg1"][:, :].rearrange("(kc p) c -> p kc c", p=128)
        p.dma("scalar", wg1s[:], src, reads=[("wbf", "w3_wg1")],
              writes=[("wg1", u)])
        wg2s = mb.sbuf_scoped(f"b3wg2{u}", [128, 2, 8], BF16)
        src = mb.wbf["w3_wg2"][:, :].rearrange("(kc p) c -> p kc c", p=128)
        p.dma("scalar", wg2s[:], src, reads=[("wbf", "w3_wg2")],
              writes=[("wg2", u)])
        xnp = [mb.sbuf_scoped(f"b3xp{u}_{i}", [128, 512], BF16) for i in range(2)]
        g1f = mb.sbuf_scoped(f"b3g1{u}", [128, 2, T], BF16)
        lgf = mb.sbuf_scoped(f"b3lg{u}", [8, T], F32)
        ltm = mb.sbuf_scoped(f"b3ltm{u}", [128, 32], F32)
        gfb = mb.sbuf_scoped(f"b3gfb{u}", [8, T], BF16)
        gbf16 = mb.sbuf_scoped(f"b3gb16{u}", [128, 8], BF16)
        for tc in range(4):
            for kc in range(KC):
                xb = xnp[kc % 2]
                p.dma("sync", xb[:], xn_dram[s, kc, :, tc * 512:(tc + 1) * 512],
                      reads=[("xnd3", s, kc)], writes=[("xp3", u, kc % 2)])
                for gc in range(2):
                    mb.mm(mb.pb[gc][:, :], wg1s[:, kc, gc * 128:gc * 128 + 128],
                          xb[:], ("pb", gc), [("wg1", u), ("xp3", u, kc % 2)],
                          first=(kc == 0), last=(kc == KC - 1))
            for gc in range(2):
                mb.act(g1f[:, gc, tc * 512:(tc + 1) * 512], mb.pb[gc][:, :],
                       AF.Gelu, ("g1f", u), [("pb", gc), ("b3g1",)],
                       bias=mb.sb["b3g1"][:, gc:gc + 1])
            for gc in range(2):
                mb.mm(mb.pb[2][0:8, :], wg2s[:, gc, :],
                      g1f[:, gc, tc * 512:(tc + 1) * 512], ("pb", 2),
                      [("wg2", u), ("g1f", u)], first=(gc == 0), last=(gc == 1))
            mb.copy(lgf[:, tc * 512:(tc + 1) * 512], mb.pb[2][0:8, :],
                    ("lgf", u), [("pb", 2)])
        LC = lambda j: ltm[:, 8 + j:9 + j]
        for tt_ in range(16):
            mb.p.do("tensor", lambda h, o=mb.pb[3][:, 0:8],
                    i=lgf[:, tt_ * 128:tt_ * 128 + 128],
                    idn=mb.cs("ident_f32")[0:8, 0:8]: h.transpose(o, i, idn),
                    reads=[("lgf", u), ("c", "ident_f32")], writes=[("pb", 3)])
            mb.copy(ltm[:, 0:8], mb.pb[3][:, 0:8], ("ltm", u), [("pb", 3)])
            mb.red(LC(0), ltm[:, 0:8], ALU.max, ("ltm", u), [("ltm", u)],
                   negate=True)
            mb.act(ltm[:, 16:24], ltm[:, 0:8], AF.Exp, ("ltm", u),
                   [("ltm", u)], bias=LC(0), accum=LC(1),
                   accum_res=("ltm", u))
            mb.ts(ltm[:, 24:32], ltm[:, 0:8], LC(0), None, ALU.add, None,
                  ("ltm", u), [("ltm", u)])
            mb.ts(ltm[:, 24:32], ltm[:, 24:32], 0.0, None, ALU.is_ge, None,
                  ("ltm", u), [("ltm", u)])
            mb.stt(ltm[:, 24:32], ltm[:, 24:32], -3.0e38, ltm[:, 0:8],
                   ALU.mult, ALU.add, ("ltm", u), [("ltm", u)])
            mb.red(LC(2), ltm[:, 24:32], ALU.max, ("ltm", u), [("ltm", u)])
            mb.ts(ltm[:, 24:32], ltm[:, 0:8], LC(2), None, ALU.is_ge, None,
                  ("ltm", u), [("ltm", u)])
            mb.recip(LC(1), LC(1), ("ltm", u), [("ltm", u)])
            mb.tt(ltm[:, 16:24], ltm[:, 16:24], ltm[:, 24:32], ALU.mult,
                  ("ltm", u), [("ltm", u)])
            mb.ts(ltm[:, 16:24], ltm[:, 16:24], LC(1), None, ALU.mult, None,
                  ("ltm", u), [("ltm", u)])
            mb.copy(gbf16[:, :], ltm[:, 16:24], ("gb16", u), [("ltm", u)])
            mb.pe_transpose(mb.pb["t"][0:8, 0:128], gbf16[:, :], ("pbt",),
                            [("gb16", u)])
            mb.copy(gfb[:, tt_ * 128:tt_ * 128 + 128], mb.pb["t"][0:8, 0:128],
                    ("gfb", u), [("pbt",)])
        win = mb.sbuf_scoped(f"b3wi{u}", [128, KC, 2048], BF16)
        xn8 = mb.sbuf_scoped(f"b3x8{u}", [128, KC, 512], BF16)
        woh = mb.sbuf_scoped(f"b3wh{u}", [128, 16, 512], BF16)
        hcur = mb.sbuf_scoped(f"b3hc{u}", [128, 16, 512], BF16)
        hb = mb.sbuf_scoped(f"b3hb{u}", [128, 512], BF16)
        Gb = mb.sbuf_scoped(f"b3Gb{u}", [128, 512], BF16)
        yacc = mb.sbuf_scoped(f"b3ya{u}", [128, KC, T], F32)
        for e in range(8):
            src = mb.wbf[f"w3_win{e}"][:, :].rearrange("(kc p) c -> p kc c", p=128)
            p.dma("scalar", win[:], src, reads=[("wbf", f"w3_win{e}")],
                  writes=[("wi3", u)])
            for tc in range(4):
                mb.mm(mb.pb[4][:, :], mb.cs("IND8")[:, 128 * e:128 * e + 128],
                      gfb[:, tc * 512:(tc + 1) * 512], ("pb", 4),
                      [("gfb", u), ("c", "IND8")])
                mb.act(Gb[:, :], mb.pb[4][:, :], AF.Identity, ("Gb3", u),
                       [("pb", 4)])
                for kc in range(KC):
                    p.dma("sync", xn8[:, kc, :],
                          xn_dram[s, kc, :, tc * 512:(tc + 1) * 512],
                          reads=[("xnd3", s, kc)], writes=[("xn8", u, kc)])
                for hc in range(16):
                    for kc in range(KC):
                        mb.mm(mb.pb[hc % 2][:, :],
                              win[:, kc, hc * 128:hc * 128 + 128],
                              xn8[:, kc, :], ("pb", hc % 2),
                              [("wi3", u), ("xn8", u, kc)],
                              first=(kc == 0), last=(kc == KC - 1))
                    mb.act(hb[:, :], mb.pb[hc % 2][:, :], AF.Gelu,
                           ("hb3", u), [("pb", hc % 2)])
                    mb.tt(hcur[:, hc, :], hb[:, :], Gb[:, :], ALU.mult,
                          ("hc3", u, hc), [("hb3", u), ("Gb3", u)])
                for ocg in range(2):
                    src = mb.wbf[f"w3_wout{e}"][:, ocg * 512:(ocg + 1) * 512]
                    src = src.rearrange("(hc p) c -> p hc c", p=128)
                    p.dma("scalar", woh[:], src,
                          reads=[("wbf", f"w3_wout{e}")], writes=[("wh3", u)])
                    for o4 in range(4):
                        for hc in range(16):
                            mb.mm(mb.pb[3 + o4][:, :],
                                  woh[:, hc, o4 * 128:o4 * 128 + 128],
                                  hcur[:, hc, :], ("pb", 3 + o4),
                                  [("wh3", u)] + [("hc3", u, hc)],
                                  first=(hc == 0), last=(hc == 15))
                        oc = ocg * 4 + o4
                        ysl = yacc[:, oc, tc * 512:(tc + 1) * 512]
                        if e == 0:
                            mb.act(ysl, mb.pb[3 + o4][:, :], AF.Identity,
                                   ("ya3", u, oc), [("pb", 3 + o4)])
                        else:
                            mb.stt(ysl, mb.pb[3 + o4][:, :], 1.0, ysl,
                                   ALU.mult, ALU.add, ("ya3", u, oc),
                                   [("pb", 3 + o4), ("ya3", u, oc)])
        for oc in range(KC):
            p.dma("sync", ffn_dram[s, oc, :, :], yacc[:, oc, :],
                  reads=[("ya3", u, oc)], writes=[("ffnd", s)])
        mb.end_stage()

    mb.realloc_x()
    for k in range(KC):
        p.dma("sync", mb.x[k][:], xspill[k, :, :], reads=[("xsp3", k)],
              writes=[("x", k)])
    uR = mb.uid()
    xrb = [mb.sbuf_scoped(f"b3xr{uR}_{i}", [128, 512], F32) for i in range(4)]
    for s in range(NSEQ):
        for kc in range(KC):
            for tc in range(4):
                i = (kc + tc) % 2
                p.dma("sync", xrb[i][:, :],
                      att_dram[s, kc, :, tc * 512:(tc + 1) * 512],
                      reads=[("attd", s)], writes=[("xrb3", uR, i)])
                p.dma("sync", xrb[2 + i][:, :],
                      ffn_dram[s, kc, :, tc * 512:(tc + 1) * 512],
                      reads=[("ffnd", s)], writes=[("xrb3", uR, 2 + i)])
                cols = slice(s * T + tc * 512, s * T + tc * 512 + 512)
                mb.tt(mb.x[kc][:, cols], mb.x[kc][:, cols], xrb[i][:, :],
                      ALU.add, ("x", kc), [("x", kc), ("xrb3", uR, i)])
                mb.tt(mb.x[kc][:, cols], mb.x[kc][:, cols], xrb[2 + i][:, :],
                      ALU.add, ("x", kc), [("x", kc), ("xrb3", uR, 2 + i)])
    mb.end_stage()



BLOCK_EMITTERS = {0: emit_block0, 1: emit_block1, 2: emit_block2, 3: emit_block3}
BLOCK_PRE = {0: pre_block0, 1: pre_block1, 2: pre_block2, 3: pre_block3}


def build_program(block_ids, consts, weights):
    used = []
    for b in block_ids:
        used += BLOCK_WEIGHTS[b]
    mb = MB(consts, weights)
    mb.setup(used, block_ids)
    mb.load_x()
    for b in block_ids:
        BLOCK_EMITTERS[b](mb)
    mb.store_x()
    mb.p.build(mb.stack)
    return mb


LAUNCH_PLAN = [[0], [1], [2], [3]]


def kernel(**inputs):
    consts = host_constants()
    weights = host_weights(inputs)
    x = np.asarray(inputs["L0_attn_x"])
    shards = []
    for c in range(N_CORES):
        xs = x[2 * c:2 * c + 2]
        xf = xs.transpose(2, 0, 1).reshape(D, NSEQ * T)
        shards.append(np.ascontiguousarray(
            xf.reshape(KC, 128, NSEQ * T), dtype=np.float32))
    cur = shards
    for block_ids in LAUNCH_PLAN:
        mb = build_program(block_ids, consts, weights)
        in_maps = []
        for c in range(N_CORES):
            m = {"x_in": cur[c]}
            if 1 in block_ids and not SIM_NO_CC:
                assert block_ids[0] == 1
                m["xoth"] = np.ascontiguousarray(np.stack(
                    [np.asarray(cur[c2]) for c2 in range(N_CORES) if c2 != c]))
            for name, arr in consts.items():
                m["c_" + name] = arr
            for b in block_ids:
                for wn in BLOCK_WEIGHTS[b]:
                    m[wn] = weights[wn]
            in_maps.append(m)
        res = run_bass_kernel_spmd(mb.nc, in_maps, core_ids=list(range(N_CORES)))
        cur = [r["x_out"] for r in res.results]
        for cm in reversed(mb._x_cms):
            cm.__exit__(None, None, None)
        mb._x_cms = []
        mb.stack.close()
    out = np.empty((16, T, D), np.float32)
    for c in range(N_CORES):
        xf = np.asarray(cur[c]).reshape(D, NSEQ, T)
        out[2 * c:2 * c + 2] = xf.transpose(1, 2, 0)
    return out


# version 02
# speedup vs baseline: 1.0345x; 1.0345x over previous
import math
import numpy as np
import ml_dtypes
import contextlib
import concourse.bass as bass
import concourse.mybir as mybir
from concourse.bass_utils import run_bass_kernel_spmd

F32 = mybir.dt.float32
BF16 = mybir.dt.bfloat16
U8 = mybir.dt.uint8
AF = mybir.ActivationFunctionType
ALU = mybir.AluOpType
AX = mybir.AxisListType

BF = ml_dtypes.bfloat16

N_CORES = 8
D = 1024
T = 2048
NSEQ = 2
KC = D // 128
NEG = -1.0e9

ENGINES = ("sync", "scalar", "vector", "tensor", "gpsimd")
SIM_NO_CC = False


def cdiv(a, b):
    return (a + b - 1) // b



NDMA = 12
NSW = 64
BIG_CONSTS = {"rope0_C", "rope0_S", "rope2_C", "rope2_S",
              "xpq_C", "xpq_S", "xpk_C", "xpk_S"}


class Prog:
    def __init__(self, nc):
        self.nc = nc
        self.stream = {e: [] for e in ENGINES}
        self.dmanames = [f"dma{i}" for i in range(NDMA)]
        self.swnames = [f"sw{i}" for i in range(NSW)]
        self.semnames = ["e_" + e for e in ENGINES] + self.dmanames + self.swnames
        self.counts = {s: 0 for s in self.semnames}
        self.hw = {e: {s: 0 for s in self.semnames} for e in ENGINES}
        self.sems = {}
        self.last_w = {}
        self.readers = {}
        self.pending = {e: {"r": [], "w": []} for e in ENGINES}
        self.n_ops = 0
        self._dma_rr = 0
        self._sw_next = 0
        self.phase = "init"
        self.phase_map = {}

    def _plan_waits(self, eng, events):
        planned = []
        for (s, c) in events:
            if c > self.hw[eng][s]:
                self.hw[eng][s] = c
                planned.append((s, c))
        return planned

    def _dep_events(self, reads, writes):
        ev = []
        for r in reads:
            e = self.last_w.get(r)
            if e is not None:
                ev.append(e)
        for w in writes:
            e = self.last_w.get(w)
            if e is not None:
                ev.append(e)
            ev.extend(self.readers.get(w, ()))
        return ev

    def _record(self, reads, writes, event):
        for w in writes:
            self.last_w[w] = event
            self.readers[w] = []
        for r in reads:
            self.readers.setdefault(r, []).append(event)

    def do(self, eng, emit, reads=(), writes=(), signal=True, dma=False):
        reads = list(reads)
        writes = list(writes)
        dep = self._dep_events(reads, writes)
        self.n_ops += 1
        if dma:
            if eng == "gpsimd":
                assert self._sw_next < NSW, "SWDGE sem pool exhausted (add barrier)"
                s = self.swnames[self._sw_next]
                self._sw_next += 1
            else:
                s = self.dmanames[self._dma_rr % NDMA]
                self._dma_rr += 1
                dep = dep + [(s, self.counts[s])]
        planned = self._plan_waits(eng, dep)
        sems = self.sems
        if signal:
            s = s if dma else ("e_" + eng)
            delta = 16 if dma else 1
            self.counts[s] += delta
            event = (s, self.counts[s])
            allr = self.pending[eng]["r"] + reads
            allw = self.pending[eng]["w"] + writes
            self.pending[eng] = {"r": [], "w": []}
            self._record(allr, allw, event)

            def closure(h, planned=planned, emit=emit, sn=s, delta=delta,
                        ph=self.phase):
                for (n, c) in planned:
                    h.wait_ge(sems[n], c)
                ins = emit(h)
                ins.then_inc(sems[sn], delta)
                self.phase_map[ins.ins.name] = (ph, eng)
        else:
            event = ("e_" + eng, self.counts["e_" + eng] + 1)
            self._record(reads, writes, event)

            def closure(h, planned=planned, emit=emit, ph=self.phase, eng=eng):
                for (n, c) in planned:
                    h.wait_ge(sems[n], c)
                ins = emit(h)
                self.phase_map[ins.ins.name] = (ph, eng)
        self.stream[eng].append(closure)
        return event

    def dma(self, eng, out, in_, reads=(), writes=(), **kw):
        return self.do(eng, lambda h: h.dma_start(out=out, in_=in_, **kw),
                       reads=reads, writes=writes, dma=True)

    def wait_event(self, eng, event):
        planned = self._plan_waits(eng, [event])
        sems = self.sems
        if planned:
            def closure(h, planned=planned):
                for (n, c) in planned:
                    h.wait_ge(sems[n], c)
            self.stream[eng].append(closure)

    def barrier(self):
        sems = self.sems
        dmaev = [(s, self.counts[s]) for s in self.dmanames if self.counts[s] > 0]
        dmaev += [(s, 16) for s in self.swnames[:self._sw_next]]
        for e in ENGINES:
            self.wait_event(e, ("e_" + e, self.counts["e_" + e]))
            for ev in dmaev:
                self.wait_event(e, ev)
        arr = [self.do(e, lambda h: h.nop()) for e in ENGINES]
        for e in ENGINES:
            for ev in arr:
                self.wait_event(e, ev)
        self.last_w = {}
        self.readers = {}

    def build(self, stack):
        nc = self.nc
        for s in self.semnames:
            self.sems[s] = stack.enter_context(nc.semaphore(s))
        block = stack.enter_context(nc.Block())
        for e in ENGINES:
            ops = self.stream[e]

            def body(h, ops=ops):
                for f in ops:
                    f(h)
            getattr(block, e)(body)



def _col128(v):
    n = v.shape[0]
    assert n % 128 == 0
    return np.ascontiguousarray(v.reshape(n // 128, 128).T.astype(np.float32))


def host_constants():
    c = {}
    c["ones_col"] = np.ones((128, 1), np.float32)
    c["ones_row"] = np.ones((1, 128), np.float32)
    c["ident_bf"] = np.eye(128, dtype=BF)
    c["top64_row"] = np.concatenate(
        [np.ones((1, 64), np.float32), np.zeros((1, 64), np.float32)], axis=1)
    c["bot64_row"] = np.concatenate(
        [np.zeros((1, 64), np.float32), np.ones((1, 64), np.float32)], axis=1)
    c["neg_big"] = np.full((1, 8), -3.0e38, np.float32)
    c["eps_col"] = np.full((128, 1), 1.0e-5, np.float32)
    c["zeros8"] = np.zeros((1, 8), np.float32)
    hs = 64
    pos = np.arange(T, dtype=np.float64)[:, None]
    i = np.arange(1, hs // 2 + 1, dtype=np.float64)[None]
    theta = 1.0 / 10000.0 ** (2.0 * (i - 1.0) / hs)
    ang = pos * theta
    cos = np.cos(ang).T
    sin = np.sin(ang).T
    c["rope0_C"] = np.tile(np.repeat(cos, 2, axis=0), (2, 1)).astype(BF)
    c["rope0_S"] = np.tile(np.repeat(sin, 2, axis=0), (2, 1)).astype(BF)
    M0 = np.zeros((64, 64), np.float64)
    for k in range(32):
        M0[2 * k, 2 * k + 1] = -1.0
        M0[2 * k + 1, 2 * k] = 1.0
    R0 = np.zeros((128, 128), np.float64)
    R0[:64, :64] = M0
    R0[64:, 64:] = M0
    c["rope0_Rt"] = np.ascontiguousarray(R0.T).astype(BF)
    ii = np.arange(128)[:, None]
    jj = np.arange(128)[None, :]
    c["mask_add_qj"] = np.where(jj > ii, np.float32(NEG), np.float32(0)).astype(np.float32)
    c["mask_add_fl"] = np.where(jj < ii, np.float32(NEG), np.float32(0)).astype(np.float32)
    c["mask_keep_jq"] = (jj >= ii).astype(BF)
    half = 32
    inv2 = 1.0 / 10000.0 ** (np.arange(0, half, dtype=np.float64) / half)
    fr2 = np.arange(T, dtype=np.float64)[:, None] * inv2[None, :]
    cos2, sin2 = np.cos(fr2).T, np.sin(fr2).T
    C2 = np.concatenate([cos2, cos2], axis=0)
    S2 = np.concatenate([sin2, sin2], axis=0)
    c["rope2_C"] = np.tile(C2, (2, 1)).astype(BF)
    c["rope2_S"] = np.tile(S2, (2, 1)).astype(BF)
    M2 = np.zeros((64, 64), np.float64)
    for k in range(32):
        M2[k, 32 + k] = -1.0
        M2[32 + k, k] = 1.0
    R2 = np.zeros((128, 128), np.float64)
    R2[:64, :64] = M2
    R2[64:, 64:] = M2
    c["rope2_Rt"] = np.ascontiguousarray(R2.T).astype(BF)
    DH, WIN = 64, 512
    hi = np.arange(0, DH, 2, dtype=np.float64)
    invf = 1.0 / 10000.0 ** (hi / DH)
    t_ = np.arange(T, dtype=np.float64)
    fr = t_[:, None] * invf[None, :]
    fr = np.concatenate([fr, fr], axis=-1)
    base = (hi + 0.4 * DH) / (1.4 * DH)
    power = (t_ - T // 2) / float(WIN // 2)
    scl = base[None, :] ** power[:, None]
    scl = np.concatenate([scl, scl], axis=-1)
    cosx, sinx = np.cos(fr), np.sin(fr)
    c["xpq_C"] = np.tile((cosx * scl).T, (2, 1)).astype(BF)
    c["xpq_S"] = np.tile((sinx * scl).T, (2, 1)).astype(BF)
    c["xpk_C"] = np.tile((cosx / scl).T, (2, 1)).astype(BF)
    c["xpk_S"] = np.tile((sinx / scl).T, (2, 1)).astype(BF)
    c["top64colb"] = np.concatenate([np.ones((64, 1), BF), np.zeros((64, 1), BF)])
    c["bot64colb"] = np.concatenate([np.zeros((64, 1), BF), np.ones((64, 1), BF)])
    c["ones1b"] = np.ones((1, 128), BF)
    c["ones2b"] = np.ones((2, 128), BF)
    c["ones_col_bf"] = np.ones((128, 1), BF)
    c["ident_f32"] = np.eye(128, dtype=np.float32)
    IND = np.zeros((8, 1024), BF)
    for e in range(8):
        IND[e, 128 * e:128 * e + 128] = 1
    c["IND8"] = IND
    return c


def host_weights(inp):
    w = {}
    perm0 = np.concatenate([np.arange(64) + 192 * h + 64 * t
                            for t in range(3) for h in range(16)])
    w["w0_qkv"] = np.ascontiguousarray(inp["L0_attn_W_qkv"][:, perm0])
    w["b0_qkv"] = _col128(inp["L0_attn_b_qkv"][perm0])
    w["w0_proj"] = np.ascontiguousarray(inp["L0_attn_W_proj"])
    w["b0_proj"] = _col128(inp["L0_attn_b_proj"])
    w["w0_WrT"] = np.ascontiguousarray(inp["L0_ffn_Wr"].T.astype(np.float32))
    w["w0_AT"] = np.ascontiguousarray(inp["L0_ffn_A"].reshape(512, 1024).T)
    w["w0_BwR"] = np.ascontiguousarray(
        inp["L0_ffn_Bw"].transpose(0, 2, 1).reshape(512, 1024))
    for nm in ("q1", "q2", "k1", "k2"):
        w[f"w1_{nm}T"] = np.ascontiguousarray(inp[f"L1_attn_{nm}_w"].T)
    w["w1_vT"] = np.ascontiguousarray(inp["L1_attn_v_w"].T)
    w["w1_cT"] = np.ascontiguousarray(inp["L1_attn_c_w"].T)
    w["b1_g"] = _col128(inp["L1_attn_bn_gamma"])
    w["b1_b"] = _col128(inp["L1_attn_bn_beta"])
    w["b1_subln"] = np.ascontiguousarray(
        inp["L1_attn_subln_w"].reshape(128, 1).astype(np.float32))
    for nm in ("lq1", "lk1", "lq2", "lk2"):
        w[f"b1_{nm}"] = np.ascontiguousarray(
            inp[f"L1_attn_{nm}"].reshape(1, 64).astype(np.float32))
    w["w1_Wg"] = _col128(inp["L1_ffn_Wg"][:, 0])
    for nm in ("Wq", "Wk", "Wv", "Wo"):
        w[f"w1_{nm}"] = np.ascontiguousarray(inp[f"L1_ffn_{nm}"])
    w["w2a_qkvT"] = np.ascontiguousarray(inp["L2_attn_w_qkv"].T)
    w["w2a_outT"] = np.ascontiguousarray(inp["L2_attn_w_out"].T)
    w["b2a_out"] = _col128(inp["L2_attn_b_out"])
    w["ln2a_g"] = _col128(inp["L2_attn_ln_g"])
    w["ln2a_b"] = _col128(inp["L2_attn_ln_b"])
    HID = 2730
    HPAD = 2816
    for l in range(2):
        w[f"w2f_qkv{l}"] = np.ascontiguousarray(inp["L2_ffn_wqkv"][l])
        w[f"w2f_wo{l}"] = np.ascontiguousarray(inp["L2_ffn_wo"][l])
        w1 = inp["L2_ffn_w1"][l]
        w1a = np.zeros((1024, HPAD), np.float32); w1a[:, :HID] = w1[:, :HID]
        w1g = np.zeros((1024, HPAD), np.float32); w1g[:, :HID] = w1[:, HID:]
        w[f"w2f_w1a{l}"] = w1a
        w[f"w2f_w1g{l}"] = w1g
        w2p = np.zeros((HPAD, 1024), np.float32); w2p[:HID] = inp["L2_ffn_w2"][l]
        w[f"w2f_w2{l}"] = w2p
        b1 = inp["L2_ffn_b1"][l]
        b1a = np.zeros(HPAD, np.float32); b1a[:HID] = b1[:HID]
        b1g = np.zeros(HPAD, np.float32); b1g[:HID] = b1[HID:]
        w[f"b2f_b1a{l}"] = _col128(b1a)
        w[f"b2f_b1g{l}"] = _col128(b1g)
        w[f"b2f_b2{l}"] = _col128(inp["L2_ffn_b2"][l])
        w[f"b2f_lg{l}"] = _col128(inp["L2_ffn_ln_g"][l])
        w[f"b2f_lb{l}"] = _col128(inp["L2_ffn_ln_b"][l])
        w[f"b2f_qs{l}"] = np.ascontiguousarray(
            np.tile(inp["L2_ffn_q_scale"][l], 2).reshape(128, 1).astype(np.float32))
        w[f"b2f_ks{l}"] = np.ascontiguousarray(
            np.tile(inp["L2_ffn_k_scale"][l], 2).reshape(128, 1).astype(np.float32))
    for nm in ("Wq", "Wk", "Wv", "Wo"):
        w[f"w3_{nm.lower()}T"] = np.ascontiguousarray(inp[f"L3_attn_{nm}"].T)
    for nm in ("bq", "bk", "bv"):
        w[f"b3_{nm}"] = np.ascontiguousarray(
            inp[f"L3_attn_{nm}"].reshape(1, 1024).astype(np.float32))
    w["b3_bo"] = _col128(inp["L3_attn_bo"])
    w["b3_beta"] = np.ascontiguousarray(
        inp["L3_attn_beta"].reshape(1, 1).astype(np.float32))
    w["w3_wg1"] = np.ascontiguousarray(inp["L3_ffn_wg1"])
    w["b3_bg1"] = _col128(inp["L3_ffn_bg1"])
    w["w3_wg2"] = np.ascontiguousarray(inp["L3_ffn_wg2"].astype(np.float32))
    for e in range(8):
        w[f"w3_win{e}"] = np.ascontiguousarray(inp["L3_ffn_w_in"][e])
        w[f"w3_wout{e}"] = np.ascontiguousarray(inp["L3_ffn_w_out"][e])
    return w


BLOCK_WEIGHTS = {
    0: ["w0_qkv", "b0_qkv", "w0_proj", "b0_proj", "w0_WrT", "w0_AT", "w0_BwR"],
    1: ["w1_q1T", "w1_q2T", "w1_k1T", "w1_k2T", "w1_vT", "w1_cT", "b1_g", "b1_b",
        "b1_subln", "b1_lq1", "b1_lk1", "b1_lq2", "b1_lk2", "w1_Wg", "w1_Wq",
        "w1_Wk", "w1_Wv", "w1_Wo"],
    2: ["w2a_qkvT", "w2a_outT", "b2a_out", "ln2a_g", "ln2a_b"]
       + [f"{nm}{l}" for l in range(2) for nm in
          ("w2f_qkv", "w2f_wo", "w2f_w1a", "w2f_w1g", "w2f_w2", "b2f_b1a",
           "b2f_b1g", "b2f_b2", "b2f_lg", "b2f_lb", "b2f_qs", "b2f_ks")],
    3: (["w3_wqT", "w3_wkT", "w3_wvT", "w3_woT", "b3_bq", "b3_bk", "b3_bv",
         "b3_bo", "b3_beta", "w3_wg1", "b3_bg1", "w3_wg2"]
        + [f"w3_win{e}" for e in range(8)] + [f"w3_wout{e}" for e in range(8)]),
}



class MB:
    def __init__(self, consts, weights):
        self.nc = bass.Bass("TRN2", target_bir_lowering=False)
        self.p = Prog(self.nc)
        self.stack = contextlib.ExitStack()
        self.dram = {}
        self.sb = {}
        self.pb = {}
        self.consts = consts
        self.weights = weights
        self._scoped = []
        self._uid = 0

    def uid(self):
        self._uid += 1
        return self._uid

    def ph(self, label):
        self.p.phase = label

    def din(self, name, shape, dtype=F32):
        ap = self.nc.dram_tensor(name, list(shape), dtype, kind="ExternalInput").ap()
        self.dram[name] = ap
        return ap

    def dout(self, name, shape, dtype=F32):
        ap = self.nc.dram_tensor(name, list(shape), dtype, kind="ExternalOutput").ap()
        self.dram[name] = ap
        return ap

    def dtmp(self, name, shape, dtype=BF16):
        ap = self.nc.dram_tensor(name, list(shape), dtype, kind="Internal").ap()
        self.dram[name] = ap
        return ap

    def sbuf(self, name, shape, dtype):
        h = self.stack.enter_context(self.nc.sbuf_tensor(name, list(shape), dtype))
        self.sb[name] = h
        return h

    def sbuf_scoped(self, name, shape, dtype):
        cm = self.nc.sbuf_tensor(name, list(shape), dtype)
        h = cm.__enter__()
        self._scoped.append(cm)
        self.sb[name] = h
        return h

    def end_stage(self):
        if self._scoped:
            self.p.barrier()
            for cm in reversed(self._scoped):
                cm.__exit__(None, None, None)
            self._scoped = []

    def setup(self, used_weights, block_ids=()):
        nc = self.nc
        for i in range(7):
            self.pb[i] = self.stack.enter_context(
                nc.psum_tensor(f"pb{i}", [128, 512], F32))
        self.pb["t"] = self.stack.enter_context(
            nc.psum_tensor("pbt", [128, 1024], BF16))
        for name, arr in self.consts.items():
            dt = F32 if arr.dtype == np.float32 else BF16
            ap = self.din("c_" + name, arr.shape, dt)
            if name in BIG_CONSTS:
                continue
            t = self.sbuf("k_" + name, arr.shape, dt)
            self.p.dma("sync", t[:], ap[:, :], writes=[("c", name)])
        self.wbf = {}
        for name in used_weights:
            arr = self.weights[name]
            self.din(name, arr.shape, F32)
        for b in block_ids:
            BLOCK_PRE.get(b, lambda mb: None)(self)
        self._x_cms = []
        self._x_gen = 0
        self.x = []
        self._alloc_x()

    def _alloc_x(self):
        self._x_gen += 1
        self._x_cms = [self.nc.sbuf_tensor(f"x{k}g{self._x_gen}",
                                           [128, NSEQ * T], F32)
                       for k in range(KC)]
        self.x = [cm.__enter__() for cm in self._x_cms]

    def free_x(self):
        assert not self._scoped
        self.p.barrier()
        for cm in reversed(self._x_cms):
            cm.__exit__(None, None, None)
        self._x_cms = []

    def realloc_x(self):
        self._alloc_x()

    def stage_weight_bf16(self, name):
        if name in self.wbf:
            return self.wbf[name]
        src = self.dram[name]
        ap = self.dtmp(name + "_bf", src.tensor.shape, BF16)
        self.p.dma("gpsimd", ap[tuple(slice(None) for _ in src.tensor.shape)],
                   src[tuple(slice(None) for _ in src.tensor.shape)],
                   writes=[("wbf", name)])
        self.wbf[name] = ap
        return ap

    def cs(self, name):
        return self.sb["k_" + name]

    def load_const_scoped(self, name):
        arr = self.consts[name]
        dt = F32 if arr.dtype == np.float32 else BF16
        t = self.sbuf_scoped("k_" + name + str(self.uid()), list(arr.shape), dt)
        self.p.dma("sync", t[:], self.dram["c_" + name][:, :],
                   writes=[("c", name)])
        return t

    def load_x(self):
        ap = self.din("x_in", [KC, 128, NSEQ * T], F32)
        for k in range(KC):
            self.p.dma("sync", self.x[k][:], ap[k, :, :], writes=[("x", k)])

    def store_x(self):
        ap = self.dout("x_out", [KC, 128, NSEQ * T], F32)
        evs = [self.p.dma("sync", ap[k, :, :], self.x[k][:], reads=[("x", k)])
               for k in range(KC)]
        for e in evs:
            self.p.wait_event("sync", e)

    def mm(self, out_ap, lhsT, rhs, res_w, reads, first=True, last=True,
           tile_position=None):
        def em(h):
            return h.matmul(out_ap, lhsT, rhs, start=first, stop=last,
                            tile_position=tile_position)
        return self.p.do("tensor", em, reads=list(reads),
                         writes=[res_w], signal=True)

    def tt(self, out, a, bb, op, res_w, reads, eng="vector"):
        self.p.do(eng, lambda h: h.tensor_tensor(out, a, bb, op=op),
                  reads=list(reads), writes=[res_w])

    def ts(self, out, in0, s1, s2, op0, op1, res_w, reads, eng="vector"):
        if s2 is None:
            self.p.do(eng, lambda h: h.tensor_scalar(out, in0, s1, None, op0=op0),
                      reads=list(reads), writes=[res_w])
        else:
            self.p.do(eng, lambda h: h.tensor_scalar(out, in0, s1, s2, op0=op0,
                                                     op1=op1),
                      reads=list(reads), writes=[res_w])

    def stt(self, out, in0, scalar, in1, op0, op1, res_w, reads):
        self.p.do("vector",
                  lambda h: h.scalar_tensor_tensor(out, in0, scalar, in1,
                                                   op0=op0, op1=op1),
                  reads=list(reads), writes=[res_w])

    def act(self, out, in_, func, res_w, reads, bias=None, scale=None, accum=None,
            accum_res=None):
        def em(h):
            kw = {}
            if bias is not None:
                kw["bias"] = bias
            if scale is not None:
                kw["scale"] = scale
            if accum is not None:
                kw["accum_out"] = accum
            return h.activation(out=out, in_=in_, func=func, **kw)
        w = [res_w] + ([accum_res] if accum_res is not None else [])
        self.p.do("scalar", em, reads=list(reads), writes=w)

    def red(self, out, in_, op, res_w, reads, negate=False):
        self.p.do("vector",
                  lambda h: h.tensor_reduce(out, in_, axis=AX.X, op=op,
                                            negate=negate),
                  reads=list(reads), writes=[res_w])

    def copy(self, out, in_, res_w, reads, eng="vector"):
        self.p.do(eng, lambda h: h.tensor_copy(out, in_),
                  reads=list(reads), writes=[res_w])

    def memset(self, ap, val, res_w, eng="vector"):
        self.p.do(eng, lambda h: h.memset(ap, val), writes=[res_w])

    def recip(self, out, in_, res_w, reads):
        self.p.do("vector", lambda h: h.reciprocal(out, in_),
                  reads=list(reads), writes=[res_w])

    def pe_transpose(self, psum_ap, in_ap, res_w, reads):
        ident = self.cs("ident_bf")
        self.p.do("tensor", lambda h: h.transpose(psum_ap, in_ap, ident[:]),
                  reads=list(reads) + [("c", "ident_bf")], writes=[res_w])

    def ln_fm(self, seq, xn_sink, add_xn_into_x=False):
        c0 = seq * T
        ones = self.cs("ones_col")
        onesr = self.cs("ones_row")
        u = self.uid()
        xsq = [self.sbuf_scoped(f"lnsq{u}_{i}", [128, 512], F32) for i in range(2)]
        rows = self.sbuf_scoped(f"lnrow{u}", [1, 512 * 3], F32)
        ab = self.sbuf_scoped(f"lnab{u}", [128, 512], F32)
        bb = self.sbuf_scoped(f"lnbb{u}", [128, 512], F32)
        t1 = self.sbuf_scoped(f"lnt1{u}", [128, 512], F32)
        R = lambda j: rows[0:1, 512 * j:512 * (j + 1)]
        for tc in range(4):
            cols = slice(c0 + tc * 512, c0 + tc * 512 + 512)
            xres = [("x", k) for k in range(KC)]
            for k in range(KC):
                self.mm(self.pb[0][0:1, :], ones[:], self.x[k][:, cols],
                        ("pb", 0), [("x", k), ("c", "ones_col")],
                        first=(k == 0), last=(k == KC - 1))
            for k in range(KC):
                b = xsq[k % 2]
                self.act(b[:], self.x[k][:, cols], AF.Square,
                         ("lnsq", u, k % 2), [("x", k)])
                self.mm(self.pb[1][0:1, :], ones[:], b[:],
                        ("pb", 1), [("lnsq", u, k % 2), ("c", "ones_col")],
                        first=(k == 0), last=(k == KC - 1))
            self.ts(R(0), self.pb[0][0:1, :], 1.0 / D, None, ALU.mult, None,
                    ("lnr", u, 0), [("pb", 0)])
            self.ts(R(1), self.pb[1][0:1, :], 1.0 / D, None, ALU.mult, None,
                    ("lnr", u, 1), [("pb", 1)])
            self.act(R(2), R(0), AF.Square, ("lnr", u, 2), [("lnr", u, 0)])
            self.tt(R(1), R(1), R(2), ALU.subtract, ("lnr", u, 1),
                    [("lnr", u, 1), ("lnr", u, 2)])
            self.act(R(1), R(1), AF.Sqrt, ("lnr", u, 1),
                     [("lnr", u, 1), ("c", "eps_col")],
                     bias=self.cs("eps_col")[0:1, 0:1], scale=1.0)
            self.recip(R(1), R(1), ("lnr", u, 1), [("lnr", u, 1)])
            self.stt(R(2), R(0), -1.0, R(1), ALU.mult, ALU.mult,
                     ("lnr", u, 2), [("lnr", u, 0), ("lnr", u, 1)])
            self.mm(self.pb[2][:, :], onesr[:], R(1), ("pb", 2),
                    [("lnr", u, 1), ("c", "ones_row")])
            self.mm(self.pb[3][:, :], onesr[:], R(2), ("pb", 3),
                    [("lnr", u, 2), ("c", "ones_row")])
            self.copy(ab[:], self.pb[2][:, :], ("lnab", u), [("pb", 2)])
            self.copy(bb[:], self.pb[3][:, :], ("lnbb", u), [("pb", 3)])
            for k in range(KC):
                self.tt(t1[:], self.x[k][:, cols], ab[:], ALU.mult,
                        ("lnt1", u), [("x", k), ("lnab", u)])
                sink_ap, sink_res, post = xn_sink(k, tc)
                self.tt(sink_ap, t1[:], bb[:], ALU.add, sink_res,
                        [("lnt1", u), ("lnbb", u)])
                post()
                if add_xn_into_x:
                    self.tt(self.x[k][:, cols], self.x[k][:, cols], t1[:],
                            ALU.add, ("x", k), [("x", k), ("lnt1", u)])
                    self.tt(self.x[k][:, cols], self.x[k][:, cols], bb[:],
                            ALU.add, ("x", k), [("x", k), ("lnbb", u)])



CUT = None


def pre_block0(mb: MB):
    p = mb.p
    t = mb.sbuf("b0c", [128, 24], F32)
    p.dma("sync", t[:], mb.dram["b0_qkv"][:, :], writes=[("b0c",)])
    t2 = mb.sbuf("b0pc", [128, 8], F32)
    p.dma("sync", t2[:], mb.dram["b0_proj"][:, :], writes=[("b0pc",)])
    for s in range(NSEQ):
        mb.sbuf(f"b0cls{s}", [128, KC], BF16)


def emit_block0(mb: MB):
    p = mb.p
    y_dram = mb.dtmp("b0_y", [NSEQ, KC, 128, T], BF16)
    xn_dram = mb.dtmp("b0_xn", [NSEQ, KC, 128, T], BF16)
    wqkv_bf = mb.stage_weight_bf16("w0_qkv")
    wproj_bf = mb.stage_weight_bf16("w0_proj")
    wrt_bf = mb.stage_weight_bf16("w0_WrT")
    wat_bf = mb.stage_weight_bf16("w0_AT")
    wbw_bf = mb.stage_weight_bf16("w0_BwR")
    clss = [mb.sb[f"b0cls{s}"] for s in range(NSEQ)]
    for s in range(NSEQ):
        c0 = s * T
        mb.ph("b0.ln")
        uL = mb.uid()
        xst = [mb.sbuf_scoped(f"b0xst{uL}_{i}", [128, 512], BF16) for i in range(2)]
        cls = clss[s]

        def xn_sink(kc, tc, s=s, uL=uL):
            i = (kc + tc) % 2
            ap = xst[i][:, :]
            res = ("xst", uL, i)

            def post():
                p.dma("sync", xn_dram[s, kc, :, tc * 512:(tc + 1) * 512],
                      xst[i][:, :], reads=[res], writes=[("xnd", s, kc)])
                if tc == 0:
                    mb.copy(cls[:, kc:kc + 1], xst[i][:, 0:1], ("cls", uL),
                            [res])
            return ap, res, post
        mb.ln_fm(s, xn_sink, add_xn_into_x=True)
        mb.end_stage()
        if CUT == "ln":
            continue

        mb.ph("b0.attn")
        wqkv = wqkv_bf
        u = mb.uid()
        wq = [mb.sbuf_scoped(f"b0wq{u}_0", [128, KC, 128], BF16)]
        wk = [mb.sbuf_scoped(f"b0wk{u}_0", [128, KC, 128], BF16)]
        wv = [mb.sbuf_scoped(f"b0wv{u}_0", [128, KC, 128], BF16)]
        xnb = [mb.sbuf_scoped(f"b0xnb{u}_{i}", [128, 1024], BF16) for i in range(2)]
        qp = mb.sbuf_scoped(f"b0qp{u}", [128, 1024], BF16)
        kp = mb.sbuf_scoped(f"b0kp{u}", [128, 1024], BF16)
        vp = mb.sbuf_scoped(f"b0vp{u}", [128, 16, 128], BF16)
        qa = [mb.sbuf_scoped(f"b0qa{u}_{i}", [67, T], BF16) for i in range(2)]
        ka = [mb.sbuf_scoped(f"b0ka{u}_{i}", [67, T], BF16) for i in range(2)]
        pt = mb.sbuf_scoped(f"b0pt{u}", [128, T], BF16)
        rtmp = mb.sbuf_scoped(f"b0rt{u}", [128, 512], BF16)
        ystage = mb.sbuf_scoped(f"b0ys{u}", [128, T], BF16)
        stats = mb.sbuf_scoped(f"b0st{u}", [128, 16], F32)
        lrow = mb.sbuf_scoped(f"b0lr{u}", [128, 16 * 4], F32)
        lbf = mb.sbuf_scoped(f"b0lbf{u}", [128, 32], BF16)
        ltr = mb.sbuf_scoped(f"b0ltr{u}", [16, 128], BF16)
        sdiag = mb.sbuf_scoped(f"b0sd{u}", [128, 128], F32)
        trash = mb.sbuf_scoped(f"b0tr{u}", [128, 512], BF16)
        biasq = mb.sbuf_scoped(f"b0bq{u}", [128, 2], F32)
        ropeC_t = mb.load_const_scoped("rope0_C")
        ropeS_t = mb.load_const_scoped("rope0_S")
        bcol = mb.sb["b0c"]

        def load_pair_w(pair, buf):
            for kind, tile, off in (("q", wq[buf], 0), ("k", wk[buf], 1024),
                                    ("v", wv[buf], 2048)):
                src = wqkv[:, off + 128 * pair: off + 128 * pair + 128]
                src = src.rearrange("(kc p) c -> p kc c", p=128)
                p.dma("scalar", tile[:], src, reads=[("wbf", "w0_qkv")],
                      writes=[("w0s", kind, buf)])

        for pair in range(8):
            buf = 0
            load_pair_w(pair, buf)
            mb.ts(biasq[:, 0:1], bcol[:, pair:pair + 1], 0.125, None, ALU.mult,
                  None, ("bq", u), [("b0c",)])
            ropeC = ropeC_t
            ropeS = ropeS_t
            ropeR = mb.cs("rope0_Rt")
            for half in range(2):
                hcols = slice(c0 + half * 1024, c0 + half * 1024 + 1024)
                for k in range(KC):
                    xb = xnb[k % 2]
                    p.dma("sync", xb[:],
                          xn_dram[s, k, :, half * 1024:half * 1024 + 1024],
                          reads=[("xnd", s, k)], writes=[("xnb", u, k % 2)])
                    for t2_ in range(2):
                        lc = slice(t2_ * 512, t2_ * 512 + 512)
                        mb.mm(mb.pb[t2_][:, :], wq[buf][:, k, :], xb[:, lc],
                              ("pb", t2_), [("w0s", "q", buf), ("xnb", u, k % 2)],
                              first=(k == 0), last=(k == KC - 1))
                        mb.mm(mb.pb[2 + t2_][:, :], wk[buf][:, k, :], xb[:, lc],
                              ("pb", 2 + t2_), [("w0s", "k", buf), ("xnb", u, k % 2)],
                              first=(k == 0), last=(k == KC - 1))
                    for tt_ in range(8 if CUT != "projqk" else 0):
                        trows = slice(tt_ * 128, tt_ * 128 + 128)
                        bank = mb.pb[4 + tt_ // 4]
                        bslc = bank[:, (tt_ % 4) * 128:(tt_ % 4) * 128 + 128]
                        mb.mm(bslc, xb[:, trows], wv[buf][:, k, :],
                              ("pb", 4 + tt_ // 4),
                              [("xnb", u, k % 2), ("w0s", "v", buf)],
                              first=(k == 0 and tt_ % 4 == 0),
                              last=(k == KC - 1 and tt_ % 4 == 3))
                for t2_ in range(2):
                    lc = slice(t2_ * 512, t2_ * 512 + 512)
                    mb.act(qp[:, lc], mb.pb[t2_][:, :], AF.Identity, ("qp", u),
                           [("pb", t2_), ("bq", u)], bias=biasq[:, 0:1],
                           scale=0.125)
                    mb.act(kp[:, lc], mb.pb[2 + t2_][:, :], AF.Identity, ("kp", u),
                           [("pb", 2 + t2_), ("b0c",)],
                           bias=bcol[:, 8 + pair:9 + pair], scale=1.0)
                for tt_ in range(8 if CUT != "projqk" else 0):
                    bank = mb.pb[4 + tt_ // 4]
                    bslc = bank[:, (tt_ % 4) * 128:(tt_ % 4) * 128 + 128]
                    mb.act(vp[:, half * 8 + tt_, :], bslc, AF.Identity, ("vp", u),
                           [("pb", 4 + tt_ // 4)])
                for which, src, srcres, augs in (
                        () if CUT in ("projqk", "projv") else
                        (("q", qp, ("qp", u), qa), ("k", kp, ("kp", u), ka))):
                    for t2_ in range(2):
                        lc = slice(t2_ * 512, t2_ * 512 + 512)
                        gcols = slice(half * 1024 + t2_ * 512,
                                      half * 1024 + t2_ * 512 + 512)
                        mb.mm(mb.pb[6][:, :], ropeR[:], src[:, lc], ("pb", 6),
                              [srcres, ("c", "rope0_Rt")])
                        mb.tt(rtmp[:, :], src[:, lc],
                              ropeC[:, gcols], ALU.mult,
                              ("rtmp", u), [srcres, ("c", "rope0_C")])
                        mb.tt(trash[:, :], mb.pb[6][:, :],
                              ropeS[:, gcols], ALU.mult,
                              ("trash", u), [("pb", 6), ("c", "rope0_S")])
                        for hh in range(2):
                            rr = slice(64 * hh, 64 * hh + 64)
                            mb.tt(augs[hh][0:64, gcols], rtmp[rr, :],
                                  trash[rr, :], ALU.add, ("aug", which, u, hh),
                                  [("rtmp", u), ("trash", u)])
            if (CUT is None) and s == 0 and pair == 0 and "b0_dbg_q" not in mb.dram:
                dq = mb.dtmp("b0_dbg_q", [64, T], BF16)
                dk = mb.dtmp("b0_dbg_k", [64, T], BF16)
                dv = mb.dtmp("b0_dbg_v", [128, 16, 128], BF16)
                mb.dtmp("b0_dbg_L", [2, T], BF16)
                p.dma("sync", dq[:, :], qa[0][0:64, :], reads=[("aug", "q", u, 0)])
                p.dma("sync", dk[:, :], ka[0][0:64, :], reads=[("aug", "k", u, 0)])
                p.dma("sync", dv[:, :, :], vp[:, :, :], reads=[("vp", u)])
            for hh in range(2 if CUT not in ("proj", "pass1") else 0):
                h_aug_q, h_aug_k = qa[hh], ka[hh]
                qres, kres = ("aug", "q", u, hh), ("aug", "k", u, hh)
                mb.memset(h_aug_k[64:67, :], 1.0, kres)
                for qt in range(16):
                    wall = (qt + 1) * 128
                    nch = cdiv(wall, 512)
                    for cch in range(nch):
                        off = cch * 512
                        w = min(512, wall - off)
                        pbs = mb.pb[cch % 3]
                        mb.mm(pbs[:, 0:w], h_aug_q[0:64, qt * 128:qt * 128 + 128],
                              h_aug_k[0:64, off:off + w], ("pb", cch % 3),
                              [qres, kres])
                        if off + w == wall:
                            mb.tt(pbs[:, w - 128:w], pbs[:, w - 128:w],
                                  mb.cs("mask_add_qj")[:, :], ALU.add,
                                  ("pb", cch % 3),
                                  [("pb", cch % 3), ("c", "mask_add_qj")])
                        mb.red(stats[:, cch:cch + 1], pbs[:, 0:w], ALU.max,
                               ("st", u, cch), [("pb", cch % 3)], negate=True)
                        mb.act(trash[:, 0:w], pbs[:, 0:w], AF.Exp,
                               ("trash", u), [("pb", cch % 3), ("st", u, cch)],
                               bias=stats[:, cch:cch + 1],
                               accum=stats[:, 4 + cch:5 + cch],
                               accum_res=("st", u, 4 + cch))
                    strs = [("st", u, i) for i in (0, 1, 2, 3, 4, 5, 6, 7, 8, 9)]
                    if nch == 1:
                        mb.act(stats[:, 12:13], stats[:, 4:5], AF.Ln,
                               ("st", u, 12), strs)
                        mb.tt(lrow[:, qt:qt + 1], stats[:, 0:1], stats[:, 12:13],
                              ALU.subtract, ("lrow", u, qt),
                              [("st", u, 0), ("st", u, 12)])
                    else:
                        mb.red(stats[:, 8:9], stats[:, 0:nch], ALU.min,
                               ("st", u, 8), strs)
                        mb.act(stats[:, 0:nch], stats[:, 0:nch], AF.Exp,
                               ("st", u, 0), strs + [("st", u, 8)],
                               scale=-1.0, bias=stats[:, 8:9])
                        mb.tt(stats[:, 4:4 + nch], stats[:, 4:4 + nch],
                              stats[:, 0:nch], ALU.mult, ("st", u, 4), strs)
                        mb.red(stats[:, 9:10], stats[:, 4:4 + nch], ALU.add,
                               ("st", u, 9), strs)
                        mb.act(stats[:, 12:13], stats[:, 9:10], AF.Ln,
                               ("st", u, 12), strs)
                        mb.tt(lrow[:, qt:qt + 1], stats[:, 8:9], stats[:, 12:13],
                              ALU.subtract, ("lrow", u, qt),
                              [("st", u, 8), ("st", u, 12)])
                lres = [("lrow", u, qt) for qt in range(16)]
                mb.copy(lbf[:, 0:16], lrow[:, 0:16], ("lbf", u), lres)
                mb.copy(lrow[:, 16:32], lbf[:, 0:16], ("lrow2", u), [("lbf", u)])
                mb.tt(lrow[:, 32:48], lrow[:, 0:16], lrow[:, 16:32],
                      ALU.subtract, ("lrow3", u), [("lrow2", u)] + lres)
                mb.copy(lbf[:, 16:32], lrow[:, 32:48], ("lbf2", u), [("lrow3", u)])
                mb.pe_transpose(mb.pb["t"][0:16, 0:128], lbf[:, 0:16],
                                ("pbt",), [("lbf", u)])
                mb.copy(ltr[:, :], mb.pb["t"][0:16, 0:128], ("ltr", u), [("pbt",)])
                p.dma("sync", h_aug_q[64:65, :], ltr[:, :], reads=[("ltr", u)],
                      writes=[qres])
                mb.pe_transpose(mb.pb["t"][0:16, 0:128], lbf[:, 16:32],
                                ("pbt",), [("lbf2", u)])
                mb.copy(ltr[:, :], mb.pb["t"][0:16, 0:128], ("ltr", u), [("pbt",)])
                p.dma("sync", h_aug_q[65:66, :], ltr[:, :], reads=[("ltr", u)],
                      writes=[qres])
                if CUT is None and s == 0 and pair == 0 and hh == 0:
                    p.dma("sync", mb.dram["b0_dbg_L"][:, :], h_aug_q[64:66, :],
                          reads=[qres])
                yb = {0: 3, 1: 4, 2: 5, 3: 6}
                for jt in range(16):
                    chunks = [(jt * 128, 128)]
                    off = jt * 128 + 128
                    while off < T:
                        w = min(512, T - off)
                        chunks.append((off, w))
                        off += w
                    for ci, (off, w) in enumerate(chunks):
                        pbs = mb.pb[ci % 3]
                        mb.mm(pbs[:, 0:w], h_aug_k[0:66, jt * 128:jt * 128 + 128],
                              h_aug_q[0:66, off:off + w], ("pb", ci % 3),
                              [kres, qres])
                        loc = off - jt * 128
                        if ci == 0:
                            mb.ts(sdiag[:, :], pbs[:, 0:128], 80.0, None,
                                  ALU.min, None, ("sdiag", u), [("pb", ci % 3)])
                            mb.act(pt[:, 0:128], sdiag[:, :], AF.Exp,
                                   ("pt", u, 0), [("sdiag", u)])
                            mb.tt(pt[:, 0:128], pt[:, 0:128],
                                  mb.cs("mask_keep_jq")[:, :], ALU.mult,
                                  ("pt", u, 0), [("pt", u, 0), ("c", "mask_keep_jq")])
                        else:
                            mb.act(pt[:, loc:loc + w], pbs[:, 0:w], AF.Exp,
                                   ("pt", u, ci), [("pb", ci % 3)])
                    ptres = [("pt", u, i) for i in range(len(chunks))]
                    for ci4 in range(jt // 4, 4):
                        qs = max(512 * ci4, jt * 128)
                        qe = 512 * ci4 + 512
                        loc = qs - jt * 128
                        bank = mb.pb[yb[ci4]]
                        boff = qs - 512 * ci4
                        mb.mm(bank[0:64, boff:boff + (qe - qs)],
                              vp[:, jt, 64 * hh:64 * hh + 64],
                              pt[:, loc:loc + (qe - qs)],
                              ("pb", yb[ci4]),
                              [("vp", u)] + ptres,
                              first=(jt == 0), last=(jt == min(15, 4 * ci4 + 3)))
                for ci4 in range(4):
                    mb.act(ystage[64 * hh:64 * hh + 64, ci4 * 512:(ci4 + 1) * 512],
                           mb.pb[yb[ci4]][0:64, :], AF.Identity, ("ys", u, hh),
                           [("pb", yb[ci4]), ("b0c",)],
                           bias=bcol[64 * hh:64 * hh + 64, 16 + pair:17 + pair],
                           scale=1.0)
            p.dma("sync", y_dram[s, pair, :, :], ystage[:, :],
                  reads=[("ys", u, 0), ("ys", u, 1)], writes=[("yd", s, pair)])
        mb.end_stage()
        if CUT in ("proj", "pass1", "attn"):
            continue

        mb.ph("b0.outproj")
        u = mb.uid()
        wpk = mb.sbuf_scoped(f"b0wp{u}", [128, KC, 512], BF16)
        ybuf = [mb.sbuf_scoped(f"b0yb{u}_{i}", [128, 512], BF16) for i in range(2)]
        wproj = wproj_bf
        bpc = mb.sb["b0pc"]
        for ocg in range(2):
            src = wproj[:, ocg * 512:(ocg + 1) * 512]
            src = src.rearrange("(kc p) c -> p kc c", p=128)
            p.dma("scalar", wpk[:], src, reads=[("wbf", "w0_proj")],
                  writes=[("wp", u)])
            for tc in range(4):
                for k in range(KC):
                    yb_ = ybuf[k % 2]
                    p.dma("scalar", yb_[:], y_dram[s, k, :, tc * 512:(tc + 1) * 512],
                          reads=[("yd", s, k)], writes=[("ybuf", u, k % 2)])
                    for o4 in range(4):
                        mb.mm(mb.pb[o4][:, :], wpk[:, k, o4 * 128:(o4 + 1) * 128],
                              yb_[:], ("pb", o4), [("wp", u), ("ybuf", u, k % 2)],
                              first=(k == 0), last=(k == KC - 1))
                for o4 in range(4):
                    oc = ocg * 4 + o4
                    cols = slice(c0 + tc * 512, c0 + tc * 512 + 512)
                    mb.stt(mb.x[oc][:, cols], mb.pb[o4][:, :],
                           bpc[:, oc:oc + 1], mb.x[oc][:, cols],
                           ALU.add, ALU.add, ("x", oc),
                           [("pb", o4), ("b0pc",), ("x", oc)])
        mb.end_stage()
        if CUT == "outproj":
            continue

        mb.ph("b0.ffn")
        u = mb.uid()
        wrt = mb.sbuf_scoped(f"b0wr{u}", [128, KC, 8], BF16)
        src = wrt_bf[:, :].rearrange("(kc p) c -> p kc c", p=128)
        p.dma("scalar", wrt[:], src, reads=[("wbf", "w0_WrT")],
              writes=[("wrt", u)])
        grow = mb.sbuf_scoped(f"b0gr{u}", [1, 64], F32)
        gcols = mb.sbuf_scoped(f"b0gc{u}", [128, 4], F32)
        G = lambda j: grow[0:1, 8 * j:8 * (j + 1)]
        for k in range(KC):
            mb.mm(mb.pb[0][0:1, 0:8], cls[:, k:k + 1], wrt[:, k, :], ("pb", 0),
                  [("cls", uL), ("wrt", u)], first=(k == 0), last=(k == KC - 1))
        mb.copy(G(0), mb.pb[0][0:1, 0:8], ("g", u, 0), [("pb", 0)])
        mb.red(G(1)[0:1, 0:1], G(0), ALU.max, ("g", u, 1), [("g", u, 0)],
               negate=True)
        mb.ts(G(2), G(0), G(1)[0:1, 0:1], None, ALU.add, None, ("g", u, 2),
              [("g", u, 0), ("g", u, 1)])
        mb.ts(G(3), G(2), 0.0, None, ALU.is_ge, None, ("g", u, 3), [("g", u, 2)])
        mb.stt(G(4), G(3), -3.0e38, G(0), ALU.mult, ALU.add, ("g", u, 4),
               [("g", u, 3), ("g", u, 0)])
        mb.red(G(5)[0:1, 0:1], G(4), ALU.max, ("g", u, 5), [("g", u, 4)])
        mb.act(G(6), G(0), AF.Exp, ("g", u, 6), [("g", u, 0), ("g", u, 1)],
               bias=G(1)[0:1, 0:1], scale=1.0)
        mb.ts(G(7), G(0), G(5)[0:1, 0:1], None, ALU.is_ge, None, ("g", u, 7),
              [("g", u, 0), ("g", u, 5)])
        mb.tt(G(6), G(6), G(7), ALU.mult, ("g", u, 6), [("g", u, 6), ("g", u, 7)])
        mb.red(G(5)[0:1, 1:2], G(6), ALU.add, ("g", u, 5), [("g", u, 6)])
        mb.recip(G(5)[0:1, 1:2], G(5)[0:1, 1:2], ("g", u, 5), [("g", u, 5)])
        mb.ts(G(6), G(6), G(5)[0:1, 1:2], None, ALU.mult, None, ("g", u, 6),
              [("g", u, 6), ("g", u, 5)])
        for cch in range(4):
            mb.mm(mb.pb[1][:, cch:cch + 1], mb.cs("top64_row")[:],
                  G(6)[0:1, 2 * cch:2 * cch + 1], ("pb", 1),
                  [("g", u, 6), ("c", "top64_row")], first=True, last=False)
            mb.mm(mb.pb[1][:, cch:cch + 1], mb.cs("bot64_row")[:],
                  G(6)[0:1, 2 * cch + 1:2 * cch + 2], ("pb", 1),
                  [("g", u, 6), ("c", "bot64_row")], first=False, last=True)
        mb.copy(gcols[:, :], mb.pb[1][:, 0:4], ("gc", u), [("pb", 1)])
        wat = mb.sbuf_scoped(f"b0wa{u}", [128, KC, 512], BF16)
        src = wat_bf[:, :].rearrange("(kc p) c -> p kc c", p=128)
        p.dma("scalar", wat[:], src, reads=[("wbf", "w0_AT")],
              writes=[("wat", u)])
        mid = mb.sbuf_scoped(f"b0mid{u}", [128, 4, T], BF16)
        xnp = [mb.sbuf_scoped(f"b0xp{u}_{i}", [128, 512], BF16) for i in range(2)]
        for tc in range(4):
            for k in range(KC):
                xb = xnp[k % 2]
                p.dma("sync", xb[:],
                      xn_dram[s, k, :, tc * 512:(tc + 1) * 512],
                      reads=[("xnd", s, k)], writes=[("xnp", u, k % 2)])
                for erc in range(4):
                    mb.mm(mb.pb[erc][:, :], wat[:, k, erc * 128:(erc + 1) * 128],
                          xb[:], ("pb", erc),
                          [("wat", u), ("xnp", u, k % 2)],
                          first=(k == 0), last=(k == KC - 1))
            for erc in range(4):
                mb.ts(mid[:, erc, tc * 512:(tc + 1) * 512], mb.pb[erc][:, :],
                      gcols[:, erc:erc + 1], None, ALU.mult, None,
                      ("mid", u, erc), [("pb", erc), ("gc", u)])
        wbw = mb.sbuf_scoped(f"b0wb{u}", [128, 4, 1024], BF16)
        src = wbw_bf[:, :].rearrange("(ec p) c -> p ec c", p=128)
        p.dma("scalar", wbw[:], src, reads=[("wbf", "w0_BwR")],
              writes=[("wbw", u)])
        for oc in range(KC):
            for tc in range(4):
                cols = slice(c0 + tc * 512, c0 + tc * 512 + 512)
                for erc in range(4):
                    mb.mm(mb.pb[4 + (oc % 2)][:, :],
                          wbw[:, erc, oc * 128:(oc + 1) * 128],
                          mid[:, erc, tc * 512:(tc + 1) * 512],
                          ("pb", 4 + (oc % 2)),
                          [("wbw", u), ("mid", u, erc)],
                          first=(erc == 0), last=(erc == 3))
                mb.stt(mb.x[oc][:, cols], mb.pb[4 + (oc % 2)][:, :], 1.0,
                       mb.x[oc][:, cols], ALU.mult, ALU.add, ("x", oc),
                       [("pb", 4 + (oc % 2)), ("x", oc)])
        mb.end_stage()



def _stats_combine(mb, u, stats, lrow, qt, nch, tag):
    strs = [(tag, u, i) for i in (0, 1, 2, 3, 4, 5, 6, 7, 8, 9)]
    if nch == 1:
        mb.act(stats[:, 12:13], stats[:, 4:5], AF.Ln, (tag, u, 12), strs)
        mb.tt(lrow[:, qt:qt + 1], stats[:, 0:1], stats[:, 12:13],
              ALU.subtract, ("lrow", u, qt), [(tag, u, 0), (tag, u, 12)])
    else:
        mb.red(stats[:, 8:9], stats[:, 0:nch], ALU.min, (tag, u, 8), strs)
        mb.act(stats[:, 0:nch], stats[:, 0:nch], AF.Exp, (tag, u, 0),
               strs + [(tag, u, 8)], scale=-1.0, bias=stats[:, 8:9])
        mb.tt(stats[:, 4:4 + nch], stats[:, 4:4 + nch], stats[:, 0:nch],
              ALU.mult, (tag, u, 4), strs)
        mb.red(stats[:, 9:10], stats[:, 4:4 + nch], ALU.add, (tag, u, 9), strs)
        mb.act(stats[:, 12:13], stats[:, 9:10], AF.Ln, (tag, u, 12),
               [(tag, u, 9)])
        mb.tt(lrow[:, qt:qt + 1], stats[:, 8:9], stats[:, 12:13],
              ALU.subtract, ("lrow", u, qt), [(tag, u, 8), (tag, u, 12)])


def _lrows_emit(mb, u, lrow, lbf, ltr, dst_hi, dst_lo, dst_res, extra=None,
                extra_res=()):
    p = mb.p
    lres = [("lrow", u, qt) for qt in range(16)]
    if extra is not None:
        mb.tt(lrow[:, 0:16], lrow[:, 0:16], extra, ALU.add, ("lrowx", u),
              lres + list(extra_res))
        lres = lres + [("lrowx", u)]
    mb.copy(lbf[:, 0:16], lrow[:, 0:16], ("lbf", u), lres)
    mb.copy(lrow[:, 16:32], lbf[:, 0:16], ("lrow2", u), [("lbf", u)])
    mb.tt(lrow[:, 32:48], lrow[:, 0:16], lrow[:, 16:32], ALU.subtract,
          ("lrow3", u), [("lrow2", u)] + lres)
    mb.copy(lbf[:, 16:32], lrow[:, 32:48], ("lbf2", u), [("lrow3", u)])
    mb.pe_transpose(mb.pb["t"][0:16, 0:128], lbf[:, 0:16], ("pbt",), [("lbf", u)])
    mb.copy(ltr[:, :], mb.pb["t"][0:16, 0:128], ("ltr", u), [("pbt",)])
    p.dma("sync", dst_hi, ltr[:, :], reads=[("ltr", u)], writes=[dst_res])
    mb.pe_transpose(mb.pb["t"][0:16, 0:128], lbf[:, 16:32], ("pbt",), [("lbf2", u)])
    mb.copy(ltr[:, :], mb.pb["t"][0:16, 0:128], ("ltr", u), [("pbt",)])
    p.dma("sync", dst_lo, ltr[:, :], reads=[("ltr", u)], writes=[dst_res])


def pre_block1(mb: MB):
    p = mb.p
    bn_g = mb.sbuf("b1g", [128, KC], F32)
    p.dma("sync", bn_g[:], mb.dram["b1_g"][:, :], writes=[("b1g",)])
    bn_b = mb.sbuf("b1bta", [128, KC], F32)
    p.dma("sync", bn_b[:], mb.dram["b1_b"][:, :], writes=[("b1bta",)])
    sub_c = mb.sbuf("b1sub", [128, 1], F32)
    p.dma("sync", sub_c[:], mb.dram["b1_subln"][:, :], writes=[("b1sub",)])
    lqk = mb.sbuf("b1lqk", [1, 256], F32)
    for i, nm in enumerate(("b1_lq1", "b1_lk1", "b1_lq2", "b1_lk2")):
        p.dma("sync", lqk[0:1, 64 * i:64 * i + 64], mb.dram[nm][:, :],
              writes=[("lqk", i)])
    mb.sbuf("b1bnacc", [128, 16], F32)
    mb.sbuf("b1bncoef", [128, 16], F32)
    mb.sbuf("b1lam", [1, 8], F32)
    mb.sbuf("b1nlam", [128, 1], F32)
    mb.sbuf("b1sc", [128, 1], F32)


def emit_block1(mb: MB):
    p = mb.p
    xn_dram = mb.dtmp("b1_xn", [NSEQ, KC, 128, T], BF16)
    xnbn_dram = mb.dtmp("b1_xnbn", [NSEQ, KC, 128, T], BF16)
    y_dram = mb.dtmp("b1_y", [NSEQ, 16, 128, T], BF16)
    yf_dram = mb.dtmp("b1_yf", [NSEQ, KC, 128, T], BF16)
    xoth = (None if SIM_NO_CC
            else mb.din("xoth", [N_CORES - 1, KC, 128, NSEQ * T], F32))
    for nm in ("w1_q1T", "w1_q2T", "w1_k1T", "w1_k2T", "w1_vT", "w1_cT",
               "w1_Wg", "w1_Wq", "w1_Wk", "w1_Wv", "w1_Wo"):
        mb.stage_weight_bf16(nm)
    bn_g, bn_b, sub_c = mb.sb["b1g"], mb.sb["b1bta"], mb.sb["b1sub"]
    lqk, bnacc, bncoef = mb.sb["b1lqk"], mb.sb["b1bnacc"], mb.sb["b1bncoef"]
    lamt, nlam_c, sc_c = mb.sb["b1lam"], mb.sb["b1nlam"], mb.sb["b1sc"]
    LAMBDA_INIT = 0.8 - 0.6 * math.exp(-0.3 * 1)

    mb.memset(bnacc[:, :], 0.0, ("bnacc",))
    mb.ph("b1.ln")
    for s in range(NSEQ):
        uL = mb.uid()
        xst = [mb.sbuf_scoped(f"b1xst{uL}_{i}", [128, 512], BF16) for i in range(2)]
        sqb = mb.sbuf_scoped(f"b1sqb{uL}", [128, 512], F32)
        colt = mb.sbuf_scoped(f"b1colt{uL}", [128, 2], F32)

        def xn_sink(kc, tc, s=s, uL=uL):
            i = (kc + tc) % 2
            ap = xst[i][:, :]
            res = ("xst", uL, i)

            def post(kc=kc, tc=tc, i=i, res=res):
                p.dma("sync", xn_dram[s, kc, :, tc * 512:(tc + 1) * 512],
                      xst[i][:, :], reads=[res], writes=[("xnd1", s, kc)])
                mb.red(colt[:, 0:1], xst[i][:, :], ALU.add, ("colt", uL, 0), [res])
                mb.tt(bnacc[:, kc:kc + 1], bnacc[:, kc:kc + 1], colt[:, 0:1],
                      ALU.add, ("bnacc",), [("bnacc",), ("colt", uL, 0)])
                mb.act(sqb[:, :], xst[i][:, :], AF.Square, ("sqb", uL), [res])
                mb.red(colt[:, 1:2], sqb[:, :], ALU.add, ("colt", uL, 1),
                       [("sqb", uL)])
                mb.tt(bnacc[:, 8 + kc:9 + kc], bnacc[:, 8 + kc:9 + kc],
                      colt[:, 1:2], ALU.add, ("bnacc",),
                      [("bnacc",), ("colt", uL, 1)])
            return ap, res, post
        mb.ln_fm(s, xn_sink, add_xn_into_x=False)
        mb.end_stage()

    mb.ph("b1.bnstats")
    if not SIM_NO_CC:
        for oc in range(N_CORES - 1):
            for sl in range(NSEQ):
                u = mb.uid()
                xoc = [mb.sbuf_scoped(f"b1fo{u}_{i}", [128, 512], F32)
                       for i in range(2)]
                snk = mb.sbuf_scoped(f"b1fs{u}", [128, 512], F32)
                fsq = mb.sbuf_scoped(f"b1fq{u}", [128, 512], F32)
                fct = mb.sbuf_scoped(f"b1fc{u}", [128, 2], F32)

                def get_piece(kc, tc, phase, oc=oc, sl=sl, u=u, xoc=xoc):
                    i = (kc + tc + phase) % 2
                    res = ("b1fo", u, i)
                    p.dma("sync", xoc[i][:, :],
                          xoth[oc, kc, :, sl * T + tc * 512:
                               sl * T + (tc + 1) * 512],
                          writes=[res])
                    return xoc[i][:, :], res

                def sink(kc, tc, u=u, snk=snk, fsq=fsq, fct=fct):
                    res = ("b1fs", u)

                    def post(kc=kc, res=res):
                        mb.red(fct[:, 0:1], snk[:, :], ALU.add,
                               ("b1fc", u, 0), [res])
                        mb.tt(bnacc[:, kc:kc + 1], bnacc[:, kc:kc + 1],
                              fct[:, 0:1], ALU.add, ("bnacc",),
                              [("bnacc",), ("b1fc", u, 0)])
                        mb.act(fsq[:, :], snk[:, :], AF.Square,
                               ("b1fq", u), [res])
                        mb.red(fct[:, 1:2], fsq[:, :], ALU.add,
                               ("b1fc", u, 1), [("b1fq", u)])
                        mb.tt(bnacc[:, 8 + kc:9 + kc], bnacc[:, 8 + kc:9 + kc],
                              fct[:, 1:2], ALU.add, ("bnacc",),
                              [("bnacc",), ("b1fc", u, 1)])
                    return snk[:, :], res, post
                ln_src(mb, get_piece, sink, tag="b1f")
                mb.end_stage()
    NTOK = (NSEQ * T) if SIM_NO_CC else (16 * T)
    mb.ts(bncoef[:, 0:8], bnacc[:, 0:8], 1.0 / NTOK, None, ALU.mult, None,
          ("bnc", 0), [("bnacc",)])
    mb.ts(bncoef[:, 8:16], bnacc[:, 8:16], 1.0 / NTOK, None, ALU.mult, None,
          ("bnc", 1), [("bnacc",)])
    mb.act(bnacc[:, 0:8], bncoef[:, 0:8], AF.Square, ("bnacc",), [("bnc", 0)])
    mb.tt(bncoef[:, 8:16], bncoef[:, 8:16], bnacc[:, 0:8], ALU.subtract,
          ("bnc", 1), [("bnc", 1), ("bnacc",)])
    mb.act(bncoef[:, 8:16], bncoef[:, 8:16], AF.Sqrt, ("bnc", 1),
           [("bnc", 1), ("c", "eps_col")], bias=mb.cs("eps_col")[:, 0:1],
           scale=1.0)
    mb.recip(bncoef[:, 8:16], bncoef[:, 8:16], ("bnc", 1), [("bnc", 1)])
    mb.tt(bncoef[:, 8:16], bncoef[:, 8:16], bn_g[:, :], ALU.mult, ("bnc", 1),
          [("bnc", 1), ("b1g",)])
    mb.tt(bnacc[:, 0:8], bncoef[:, 0:8], bncoef[:, 8:16], ALU.mult, ("bnacc",),
          [("bnc", 0), ("bnc", 1)])
    mb.stt(bncoef[:, 0:8], bnacc[:, 0:8], -1.0, bn_b[:, :], ALU.mult, ALU.add,
           ("bnc", 0), [("bnacc",), ("b1bta",)])
    L8 = lambda j: lamt[0:1, j:j + 1]
    mb.tt(lqk[0:1, 0:64], lqk[0:1, 0:64], lqk[0:1, 64:128], ALU.mult,
          ("lqk", 0), [("lqk", 0), ("lqk", 1)])
    mb.red(L8(0), lqk[0:1, 0:64], ALU.add, ("lam", 0), [("lqk", 0)])
    mb.tt(lqk[0:1, 128:192], lqk[0:1, 128:192], lqk[0:1, 192:256], ALU.mult,
          ("lqk", 2), [("lqk", 2), ("lqk", 3)])
    mb.red(L8(1), lqk[0:1, 128:192], ALU.add, ("lam", 1), [("lqk", 2)])
    mb.act(L8(2), L8(0), AF.Exp, ("lam", 2), [("lam", 0)])
    mb.act(L8(3), L8(1), AF.Exp, ("lam", 3), [("lam", 1)])
    mb.tt(L8(4), L8(2), L8(3), ALU.subtract, ("lam", 4), [("lam", 2), ("lam", 3)])
    mb.ts(L8(5), L8(4), LAMBDA_INIT, None, ALU.add, None, ("lam", 5), [("lam", 4)])
    mb.mm(mb.pb[0][:, 0:1], mb.cs("ones_row")[:], L8(5), ("pb", 0),
          [("lam", 5), ("c", "ones_row")])
    mb.ts(nlam_c[:, :], mb.pb[0][:, 0:1], -1.0, None, ALU.mult, None,
          ("nlam",), [("pb", 0)])
    mb.ts(sc_c[:, :], sub_c[:, :], 1.0 - LAMBDA_INIT, None, ALU.mult, None,
          ("scc",), [("b1sub",)])

    mb.ph("b1.attn")
    for s in range(NSEQ):
        u = mb.uid()
        xch = [mb.sbuf_scoped(f"b1xch{u}_{i}", [128, T], BF16) for i in range(2)]
        for kc in range(KC):
            xb = xch[kc % 2]
            p.dma("sync", xb[:], xn_dram[s, kc, :, :], reads=[("xnd1", s, kc)],
                  writes=[("xch", u, kc % 2)])
            mb.ts(xb[:], xb[:], bncoef[:, 8 + kc:9 + kc],
                  bncoef[:, kc:kc + 1], ALU.mult, ALU.add, ("xch", u, kc % 2),
                  [("xch", u, kc % 2), ("bnc", 0), ("bnc", 1)])
            p.dma("sync", xnbn_dram[s, kc, :, :], xb[:],
                  reads=[("xch", u, kc % 2)], writes=[("xnbnd", s, kc)])
        mb.end_stage()

        u = mb.uid()
        qs_ = {nm: mb.sbuf_scoped(f"b1{nm}{u}", [128, KC, 64], BF16)
               for nm in ("q1s", "q2s", "k1s", "k2s")}
        vs = mb.sbuf_scoped(f"b1vs{u}", [128, KC, 128], BF16)
        xnb = [mb.sbuf_scoped(f"b1xnb{u}_{i}", [128, 512], BF16) for i in range(2)]
        QQ = mb.sbuf_scoped(f"b1QQ{u}", [128, T], BF16)
        KK = mb.sbuf_scoped(f"b1KK{u}", [128, T], BF16)
        vt = mb.sbuf_scoped(f"b1vt{u}", [128, 16, 128], BF16)
        lr2 = mb.sbuf_scoped(f"b1lr2{u}", [2, T], BF16)
        pt = mb.sbuf_scoped(f"b1pt{u}", [128, T], BF16)
        ysb = mb.sbuf_scoped(f"b1ysb{u}", [128, T], F32)
        sqb = mb.sbuf_scoped(f"b1sq{u}", [128, T], BF16)
        ssrow = mb.sbuf_scoped(f"b1ss{u}", [1, T], F32)
        ystage = mb.sbuf_scoped(f"b1ys{u}", [128, T], BF16)
        stats = mb.sbuf_scoped(f"b1st{u}", [128, 16], F32)
        lrow = mb.sbuf_scoped(f"b1lr{u}", [128, 48], F32)
        lbf = mb.sbuf_scoped(f"b1lbf{u}", [128, 32], BF16)
        ltr = mb.sbuf_scoped(f"b1ltr{u}", [16, 128], BF16)
        trash = mb.sbuf_scoped(f"b1tr{u}", [128, 512], BF16)
        for h in range(16):
            for nm, wname in (("q1s", "w1_q1T"), ("q2s", "w1_q2T"),
                              ("k1s", "w1_k1T"), ("k2s", "w1_k2T")):
                src = mb.wbf[wname][:, 64 * h:64 * h + 64]
                src = src.rearrange("(kc p) c -> p kc c", p=128)
                p.dma("scalar", qs_[nm][:], src, reads=[("wbf", wname)],
                      writes=[("w1s", nm, u)])
            src = mb.wbf["w1_vT"][:, 128 * h:128 * h + 128]
            src = src.rearrange("(kc p) c -> p kc c", p=128)
            p.dma("scalar", vs[:], src, reads=[("wbf", "w1_vT")],
                  writes=[("w1s", "vs", u)])
            for qr in range(4):
                for kc in range(KC):
                    xb = xnb[kc % 2]
                    p.dma("sync", xb[:],
                          xnbn_dram[s, kc, :, qr * 512:qr * 512 + 512],
                          reads=[("xnbnd", s, kc)], writes=[("xnb", u, kc % 2)])
                    for bi, nm in ((0, "q1s"), (1, "q2s"), (2, "k1s"),
                                   (3, "k2s")):
                        mb.mm(mb.pb[bi][0:64, :], qs_[nm][:, kc, :], xb[:],
                              ("pb", bi), [("w1s", nm, u), ("xnb", u, kc % 2)],
                              first=(kc == 0), last=(kc == KC - 1))
                    for tt_ in range(4):
                        bslc = mb.pb[4][:, tt_ * 128:tt_ * 128 + 128]
                        mb.mm(bslc, xb[:, tt_ * 128:tt_ * 128 + 128],
                              vs[:, kc, :], ("pb", 4),
                              [("xnb", u, kc % 2), ("w1s", "vs", u)],
                              first=(kc == 0 and tt_ == 0),
                              last=(kc == KC - 1 and tt_ == 3))
                lc = slice(qr * 512, qr * 512 + 512)
                mb.act(QQ[0:64, lc], mb.pb[0][0:64, :], AF.Identity, ("QQ", u),
                       [("pb", 0)], scale=0.125)
                mb.act(QQ[64:128, lc], mb.pb[1][0:64, :], AF.Identity, ("QQ", u),
                       [("pb", 1)], scale=0.125)
                mb.act(KK[0:64, lc], mb.pb[2][0:64, :], AF.Identity, ("KK", u),
                       [("pb", 2)])
                mb.act(KK[64:128, lc], mb.pb[3][0:64, :], AF.Identity, ("KK", u),
                       [("pb", 3)])
                mb.act(vt[:, 4 * qr:4 * qr + 4, :], mb.pb[4][:, :], AF.Identity,
                       ("vt", u), [("pb", 4)])
            for m_ in range(2):
                rows = slice(64 * m_, 64 * m_ + 64)
                qres, kres = ("QQ", u), ("KK", u)
                for qt in range(16):
                    for cch in range(4):
                        pbs = mb.pb[cch]
                        mb.mm(pbs[:, :], QQ[rows, qt * 128:qt * 128 + 128],
                              KK[rows, cch * 512:cch * 512 + 512],
                              ("pb", cch), [qres, kres])
                        mb.red(stats[:, cch:cch + 1], pbs[:, :], ALU.max,
                               ("st1", u, cch), [("pb", cch)], negate=True)
                        mb.act(trash[:, :], pbs[:, :], AF.Exp,
                               ("trash", u), [("pb", cch), ("st1", u, cch)],
                               bias=stats[:, cch:cch + 1],
                               accum=stats[:, 4 + cch:5 + cch],
                               accum_res=("st1", u, 4 + cch))
                    _stats_combine(mb, u, stats, lrow, qt, 4, "st1")
                _lrows_emit(mb, u, lrow, lbf, ltr, lr2[0:1, :], lr2[1:2, :],
                            ("lr2", u))
                yb = {0: 3, 1: 4, 2: 5, 3: 6}
                for jt in range(16):
                    for qc in range(4):
                        pbs = mb.pb[qc % 3]
                        qcs = slice(qc * 512, qc * 512 + 512)
                        mb.mm(pbs[:, :], KK[rows, jt * 128:jt * 128 + 128],
                              QQ[rows, qcs], ("pb", qc % 3), [qres, kres],
                              first=True, last=False)
                        mb.mm(pbs[:, :], mb.cs("ones2b")[:, :], lr2[:, qcs],
                              ("pb", qc % 3), [("lr2", u), ("c", "ones2b")],
                              first=False, last=True)
                        mb.act(pt[:, qcs], pbs[:, :], AF.Exp, ("pt", u, qc),
                               [("pb", qc % 3)])
                        mb.mm(mb.pb[yb[qc]][:, :], vt[:, jt, :], pt[:, qcs],
                              ("pb", yb[qc]), [("vt", u), ("pt", u, qc)],
                              first=(jt == 0), last=(jt == 15))
                if m_ == 0:
                    for qc in range(4):
                        mb.act(ysb[:, qc * 512:(qc + 1) * 512],
                               mb.pb[yb[qc]][:, :], AF.Identity, ("ysb", u),
                               [("pb", yb[qc])])
                else:
                    for qc in range(4):
                        mb.stt(ysb[:, qc * 512:(qc + 1) * 512],
                               mb.pb[yb[qc]][:, :], nlam_c[:, 0:1],
                               ysb[:, qc * 512:(qc + 1) * 512],
                               ALU.mult, ALU.add, ("ysb", u),
                               [("pb", yb[qc]), ("nlam",), ("ysb", u)])
            mb.act(sqb[:, :], ysb[:, :], AF.Square, ("sqb1", u), [("ysb", u)])
            for qc in range(4):
                mb.mm(mb.pb[0][0:1, :], mb.cs("ones_col_bf")[:],
                      sqb[:, qc * 512:(qc + 1) * 512], ("pb", 0),
                      [("sqb1", u), ("c", "ones_col_bf")])
                mb.copy(ssrow[0:1, qc * 512:(qc + 1) * 512], mb.pb[0][0:1, :],
                        ("ssr", u), [("pb", 0)])
            mb.act(ssrow[0:1, :], ssrow[0:1, :], AF.Sqrt, ("ssr", u),
                   [("ssr", u), ("c", "eps_col")], scale=1.0 / 128.0,
                   bias=mb.cs("eps_col")[0:1, 0:1])
            mb.recip(ssrow[0:1, :], ssrow[0:1, :], ("ssr", u), [("ssr", u)])
            for qc in range(4):
                qcs = slice(qc * 512, qc * 512 + 512)
                mb.mm(mb.pb[1][:, :], mb.cs("ones_row")[:], ssrow[0:1, qcs],
                      ("pb", 1), [("ssr", u), ("c", "ones_row")])
                mb.stt(ystage[:, qcs], mb.pb[1][:, :], 1.0, ysb[:, qcs],
                       ALU.mult, ALU.mult, ("yst", u), [("pb", 1), ("ysb", u)])
            p.dma("sync", y_dram[s, h, :, :], ystage[:, :], reads=[("yst", u)],
                  writes=[("yd1", s, h)])
        mb.end_stage()

        mb.ph("b1.cw")
        u = mb.uid()
        cwk = mb.sbuf_scoped(f"b1cw{u}", [128, 16, 512], BF16)
        ybuf = [mb.sbuf_scoped(f"b1yb{u}_{i}", [128, 512], BF16) for i in range(2)]
        c0 = s * T
        for ocg in range(2):
            src = mb.wbf["w1_cT"][:, ocg * 512:(ocg + 1) * 512]
            src = src.rearrange("(hc p) c -> p hc c", p=128)
            p.dma("scalar", cwk[:], src, reads=[("wbf", "w1_cT")],
                  writes=[("cw", u)])
            mb.ts(cwk[:, :, :], cwk[:, :, :], sc_c[:, 0:1], None, ALU.mult,
                  None, ("cw", u), [("cw", u), ("scc",)])
            for tc in range(4):
                for hc in range(16):
                    yb_ = ybuf[hc % 2]
                    p.dma("scalar", yb_[:],
                          y_dram[s, hc, :, tc * 512:(tc + 1) * 512],
                          reads=[("yd1", s, hc)], writes=[("yb1", u, hc % 2)])
                    for o4 in range(4):
                        mb.mm(mb.pb[o4][:, :], cwk[:, hc, o4 * 128:(o4 + 1) * 128],
                              yb_[:], ("pb", o4), [("cw", u), ("yb1", u, hc % 2)],
                              first=(hc == 0), last=(hc == 15))
                for o4 in range(4):
                    oc = ocg * 4 + o4
                    cols = slice(c0 + tc * 512, c0 + tc * 512 + 512)
                    mb.stt(mb.x[oc][:, cols], mb.pb[o4][:, :], 1.0,
                           mb.x[oc][:, cols], ALU.mult, ALU.add, ("x", oc),
                           [("pb", o4), ("x", oc)])
        mb.end_stage()

        mb.ph("b1.ffn")
        u = mb.uid()
        wgs = mb.sbuf_scoped(f"b1wg{u}", [128, KC], BF16)
        p.dma("scalar", wgs[:], mb.wbf["w1_Wg"][:, :],
              reads=[("wbf", "w1_Wg")], writes=[("wgs", u)])
        xnb = [mb.sbuf_scoped(f"b1fx{u}_{i}", [128, T], BF16) for i in range(2)]
        gtm = mb.sbuf_scoped(f"b1gtm{u}", [128, 16], F32)
        biz = mb.sbuf_scoped(f"b1biz{u}", [1, 8], F32)
        thc = mb.sbuf_scoped(f"b1thc{u}", [128, 1], F32)
        sq_tm = mb.sbuf_scoped(f"b1sqm{u}", [128, 16 * 3], F32)
        augk = mb.sbuf_scoped(f"b1agk{u}", [3, T], BF16)
        augq = mb.sbuf_scoped(f"b1agq{u}", [3, T], BF16)
        penbf = mb.sbuf_scoped(f"b1pen{u}", [128, 16], BF16)
        for kc in range(KC):
            xb = xnb[kc % 2]
            p.dma("sync", xb[:], xn_dram[s, kc, :, :], reads=[("xnd1", s, kc)],
                  writes=[("fx", u, kc % 2)])
            for tt_ in range(16):
                mb.mm(mb.pb[3][:, tt_:tt_ + 1], xb[:, tt_ * 128:tt_ * 128 + 128],
                      wgs[:, kc:kc + 1], ("pb", 3), [("fx", u, kc % 2), ("wgs", u)],
                      first=(kc == 0 and tt_ == 0),
                      last=(kc == KC - 1 and tt_ == 15))
        mb.copy(gtm[:, :], mb.pb[3][:, 0:16], ("gtm", u), [("pb", 3)])
        B8 = lambda j: biz[0:1, j:j + 1]
        mb.memset(B8(0), -100.0, ("biz", u, 0))
        mb.memset(B8(1), 100.0, ("biz", u, 1))
        for it in range(40):
            mb.tt(B8(2), B8(0), B8(1), ALU.add, ("biz", u, 2),
                  [("biz", u, 0), ("biz", u, 1)])
            mb.ts(B8(2), B8(2), 0.5, None, ALU.mult, None, ("biz", u, 2),
                  [("biz", u, 2)])
            mb.mm(mb.pb[0][:, 0:1], mb.cs("ones_row")[:], B8(2), ("pb", 0),
                  [("biz", u, 2), ("c", "ones_row")])
            mb.copy(thc[:, :], mb.pb[0][:, 0:1], ("thc", u), [("pb", 0)])
            mb.ts(sq_tm[:, 0:16], gtm[:, :], thc[:, 0:1], None, ALU.is_ge,
                  None, ("sqm", u, 0), [("gtm", u), ("thc", u)])
            mb.red(sq_tm[:, 16:17], sq_tm[:, 0:16], ALU.add, ("sqm", u, 1),
                   [("sqm", u, 0)])
            mb.mm(mb.pb[1][0:1, 0:1], mb.cs("ones_col")[:], sq_tm[:, 16:17],
                  ("pb", 1), [("sqm", u, 1), ("c", "ones_col")])
            mb.copy(B8(3), mb.pb[1][0:1, 0:1], ("biz", u, 3), [("pb", 1)])
            mb.ts(B8(4), B8(3), 1024.0, None, ALU.is_ge, None, ("biz", u, 4),
                  [("biz", u, 3)])
            mb.tt(B8(5), B8(2), B8(0), ALU.subtract, ("biz", u, 5),
                  [("biz", u, 2), ("biz", u, 0)])
            mb.stt(B8(0), B8(5), B8(4)[0:1, 0:1], B8(0), ALU.mult, ALU.add,
                   ("biz", u, 0), [("biz", u, 5), ("biz", u, 4), ("biz", u, 0)])
            mb.ts(B8(6), B8(4), 1.0, -1.0, ALU.mult, ALU.add, ("biz", u, 6),
                  [("biz", u, 4)])
            mb.tt(B8(7), B8(1), B8(2), ALU.subtract, ("biz", u, 7),
                  [("biz", u, 1), ("biz", u, 2)])
            mb.stt(B8(1), B8(7), B8(6)[0:1, 0:1], B8(1), ALU.mult, ALU.add,
                   ("biz", u, 1), [("biz", u, 7), ("biz", u, 6), ("biz", u, 1)])
        mb.tt(B8(2), B8(0), B8(1), ALU.add, ("biz", u, 2),
              [("biz", u, 0), ("biz", u, 1)])
        mb.ts(B8(2), B8(2), 0.5, None, ALU.mult, None, ("biz", u, 2),
              [("biz", u, 2)])
        mb.mm(mb.pb[0][:, 0:1], mb.cs("ones_row")[:], B8(2), ("pb", 0),
              [("biz", u, 2), ("c", "ones_row")])
        mb.copy(thc[:, :], mb.pb[0][:, 0:1], ("thc", u), [("pb", 0)])
        mb.ts(sq_tm[:, 0:16], gtm[:, :], thc[:, 0:1], None, ALU.is_ge, None,
              ("sqm", u, 0), [("gtm", u), ("thc", u)])
        mb.ts(sq_tm[:, 32:48], sq_tm[:, 0:16], 1.0e9, -1.0e9, ALU.mult, ALU.add,
              ("sqm", u, 2), [("sqm", u, 0)])
        mb.act(sq_tm[:, 16:32], gtm[:, :], AF.Sigmoid, ("sqm", u, 1), [("gtm", u)])
        mb.ts(sq_tm[:, 16:32], sq_tm[:, 16:32], 1.0e-38, None, ALU.max, None,
              ("sqm", u, 1), [("sqm", u, 1)])
        mb.act(sq_tm[:, 16:32], sq_tm[:, 16:32], AF.Ln, ("sqm", u, 1),
               [("sqm", u, 1)])
        mb.tt(sq_tm[:, 16:32], sq_tm[:, 16:32], sq_tm[:, 32:48], ALU.add,
              ("sqm", u, 1), [("sqm", u, 1), ("sqm", u, 2)])
        mb.memset(augk[0:3, :], 1.0, ("augk", u))
        mb.memset(augq[0:3, :], 1.0, ("augq", u))
        mb.copy(penbf[:, :], sq_tm[:, 32:48], ("penbf", u), [("sqm", u, 2)])
        mb.pe_transpose(mb.pb["t"][0:16, 0:128], penbf[:, :], ("pbt",),
                        [("penbf", u)])
        ltr2 = mb.sbuf_scoped(f"b1lt2{u}", [16, 128], BF16)
        mb.copy(ltr2[:, :], mb.pb["t"][0:16, 0:128], ("ltr2", u), [("pbt",)])
        p.dma("sync", augk[0:1, :], ltr2[:, :], reads=[("ltr2", u)],
              writes=[("augk", u)])
        uG = u

        wq_ = mb.sbuf_scoped(f"b1fwq{u}", [128, KC, 128], BF16)
        wk_ = mb.sbuf_scoped(f"b1fwk{u}", [128, KC, 128], BF16)
        wv_ = mb.sbuf_scoped(f"b1fwv{u}", [128, KC, 128], BF16)
        xnb = [mb.sbuf_scoped(f"b1px{u}_{i}", [128, 512], BF16) for i in range(2)]
        QQ = mb.sbuf_scoped(f"b1fQ{u}", [128, T], BF16)
        KK = mb.sbuf_scoped(f"b1fK{u}", [128, T], BF16)
        vt = mb.sbuf_scoped(f"b1fv{u}", [128, 16, 128], BF16)
        pt = mb.sbuf_scoped(f"b1fp{u}", [128, T], BF16)
        ystage = mb.sbuf_scoped(f"b1fys{u}", [128, T], BF16)
        stats = mb.sbuf_scoped(f"b1fst{u}", [128, 16], F32)
        lrow = mb.sbuf_scoped(f"b1flr{u}", [128, 48], F32)
        lbf = mb.sbuf_scoped(f"b1flb{u}", [128, 32], BF16)
        ltr = mb.sbuf_scoped(f"b1flt{u}", [16, 128], BF16)
        trash = mb.sbuf_scoped(f"b1ftr{u}", [128, 512], BF16)
        for pair in range(8):
            for tile, wname in ((wq_, "w1_Wq"), (wk_, "w1_Wk"), (wv_, "w1_Wv")):
                src = mb.wbf[wname][:, 128 * pair:128 * pair + 128]
                src = src.rearrange("(kc p) c -> p kc c", p=128)
                p.dma("scalar", tile[:], src, reads=[("wbf", wname)],
                      writes=[("w1f", wname, u)])
            for qr in range(4):
                for kc in range(KC):
                    xb = xnb[kc % 2]
                    p.dma("sync", xb[:],
                          xn_dram[s, kc, :, qr * 512:qr * 512 + 512],
                          reads=[("xnd1", s, kc)], writes=[("px", u, kc % 2)])
                    mb.mm(mb.pb[0][:, :], wq_[:, kc, :], xb[:], ("pb", 0),
                          [("w1f", "w1_Wq", u), ("px", u, kc % 2)],
                          first=(kc == 0), last=(kc == KC - 1))
                    mb.mm(mb.pb[1][:, :], wk_[:, kc, :], xb[:], ("pb", 1),
                          [("w1f", "w1_Wk", u), ("px", u, kc % 2)],
                          first=(kc == 0), last=(kc == KC - 1))
                    for tt_ in range(4):
                        bslc = mb.pb[2][:, tt_ * 128:tt_ * 128 + 128]
                        mb.mm(bslc, xb[:, tt_ * 128:tt_ * 128 + 128],
                              wv_[:, kc, :], ("pb", 2),
                              [("px", u, kc % 2), ("w1f", "w1_Wv", u)],
                              first=(kc == 0 and tt_ == 0),
                              last=(kc == KC - 1 and tt_ == 3))
                lc = slice(qr * 512, qr * 512 + 512)
                mb.act(QQ[:, lc], mb.pb[0][:, :], AF.Identity, ("fQ", u),
                       [("pb", 0)], scale=0.125)
                mb.act(KK[:, lc], mb.pb[1][:, :], AF.Identity, ("fK", u),
                       [("pb", 1)])
                mb.act(vt[:, 4 * qr:4 * qr + 4, :], mb.pb[2][:, :], AF.Identity,
                       ("fv", u), [("pb", 2)])
            for hh in range(2):
                rows = slice(64 * hh, 64 * hh + 64)
                for qt in range(16):
                    for cch in range(4):
                        pbs = mb.pb[cch]
                        ccs = slice(cch * 512, cch * 512 + 512)
                        mb.mm(pbs[:, :], QQ[rows, qt * 128:qt * 128 + 128],
                              KK[rows, ccs], ("pb", cch),
                              [("fQ", u), ("fK", u)], first=True, last=False)
                        mb.mm(pbs[:, :], mb.cs("ones1b")[:, :], augk[0:1, ccs],
                              ("pb", cch), [("augk", u), ("c", "ones1b")],
                              first=False, last=True)
                        mb.red(stats[:, cch:cch + 1], pbs[:, :], ALU.max,
                               ("st2", u, cch), [("pb", cch)], negate=True)
                        mb.act(trash[:, :], pbs[:, :], AF.Exp, ("trash", u),
                               [("pb", cch), ("st2", u, cch)],
                               bias=stats[:, cch:cch + 1],
                               accum=stats[:, 4 + cch:5 + cch],
                               accum_res=("st2", u, 4 + cch))
                    _stats_combine(mb, u, stats, lrow, qt, 4, "st2")
                _lrows_emit(mb, u, lrow, lbf, ltr, augq[1:2, :], augq[2:3, :],
                            ("augq", u), extra=sq_tm[:, 16:32],
                            extra_res=[("sqm", uG, 1)])
                yb = {0: 3, 1: 4, 2: 5, 3: 6}
                for jt in range(16):
                    for qc in range(4):
                        pbs = mb.pb[qc % 3]
                        qcs = slice(qc * 512, qc * 512 + 512)
                        mb.mm(pbs[:, :], KK[rows, jt * 128:jt * 128 + 128],
                              QQ[rows, qcs], ("pb", qc % 3),
                              [("fQ", u), ("fK", u)], first=True, last=False)
                        mb.mm(pbs[:, :], augk[0:3, jt * 128:jt * 128 + 128],
                              augq[0:3, qcs], ("pb", qc % 3),
                              [("augk", u), ("augq", u)], first=False, last=True)
                        mb.act(pt[:, qcs], pbs[:, :], AF.Exp, ("fp", u, qc),
                               [("pb", qc % 3)])
                        mb.mm(mb.pb[yb[qc]][0:64, :], vt[:, jt, rows],
                              pt[:, qcs], ("pb", yb[qc]),
                              [("fv", u), ("fp", u, qc)],
                              first=(jt == 0), last=(jt == 15))
                for qc in range(4):
                    mb.act(ystage[rows, qc * 512:(qc + 1) * 512],
                           mb.pb[yb[qc]][0:64, :], AF.Identity, ("fys", u, hh),
                           [("pb", yb[qc])])
            p.dma("sync", yf_dram[s, pair, :, :], ystage[:, :],
                  reads=[("fys", u, 0), ("fys", u, 1)], writes=[("yfd", s, pair)])
        mb.end_stage()

        u = mb.uid()
        wok = mb.sbuf_scoped(f"b1wo{u}", [128, KC, 512], BF16)
        ybuf = [mb.sbuf_scoped(f"b1oyb{u}_{i}", [128, 512], BF16) for i in range(2)]
        for ocg in range(2):
            src = mb.wbf["w1_Wo"][:, ocg * 512:(ocg + 1) * 512]
            src = src.rearrange("(kc p) c -> p kc c", p=128)
            p.dma("scalar", wok[:], src, reads=[("wbf", "w1_Wo")],
                  writes=[("wo1", u)])
            for tc in range(4):
                for kc in range(KC):
                    yb_ = ybuf[kc % 2]
                    p.dma("scalar", yb_[:],
                          yf_dram[s, kc, :, tc * 512:(tc + 1) * 512],
                          reads=[("yfd", s, kc)], writes=[("oyb", u, kc % 2)])
                    for o4 in range(4):
                        mb.mm(mb.pb[o4][:, :], wok[:, kc, o4 * 128:(o4 + 1) * 128],
                              yb_[:], ("pb", o4), [("wo1", u), ("oyb", u, kc % 2)],
                              first=(kc == 0), last=(kc == KC - 1))
                for o4 in range(4):
                    oc = ocg * 4 + o4
                    cols = slice(c0 + tc * 512, c0 + tc * 512 + 512)
                    mb.stt(mb.x[oc][:, cols], mb.pb[o4][:, :], 1.0,
                           mb.x[oc][:, cols], ALU.mult, ALU.add, ("x", oc),
                           [("pb", o4), ("x", oc)])
        mb.end_stage()



def ln_src(mb, get_piece, sink, gain_col=None, bias_col=None, tag="",
           affine_reads=()):
    p = mb.p
    ones = mb.cs("ones_col")
    onesr = mb.cs("ones_row")
    u = mb.uid()
    xsq = [mb.sbuf_scoped(f"lsq{tag}{u}_{i}", [128, 512], F32) for i in range(2)]
    rows = mb.sbuf_scoped(f"lsr{tag}{u}", [1, 512 * 3], F32)
    ab = mb.sbuf_scoped(f"lsa{tag}{u}", [128, 512], F32)
    bb = mb.sbuf_scoped(f"lsb{tag}{u}", [128, 512], F32)
    t1 = mb.sbuf_scoped(f"lst{tag}{u}", [128, 512], F32)
    R = lambda j: rows[0:1, 512 * j:512 * (j + 1)]
    for tc in range(4):
        for kc in range(KC):
            ap, res = get_piece(kc, tc, 0)
            mb.mm(mb.pb[0][0:1, :], ones[:], ap, ("pb", 0),
                  [res, ("c", "ones_col")], first=(kc == 0), last=(kc == KC - 1))
            b = xsq[kc % 2]
            mb.act(b[:], ap, AF.Square, ("lsq", u, kc % 2), [res])
            mb.mm(mb.pb[1][0:1, :], ones[:], b[:], ("pb", 1),
                  [("lsq", u, kc % 2), ("c", "ones_col")],
                  first=(kc == 0), last=(kc == KC - 1))
        mb.ts(R(0), mb.pb[0][0:1, :], 1.0 / D, None, ALU.mult, None,
              ("lsr", u, 0), [("pb", 0)])
        mb.ts(R(1), mb.pb[1][0:1, :], 1.0 / D, None, ALU.mult, None,
              ("lsr", u, 1), [("pb", 1)])
        mb.act(R(2), R(0), AF.Square, ("lsr", u, 2), [("lsr", u, 0)])
        mb.tt(R(1), R(1), R(2), ALU.subtract, ("lsr", u, 1),
              [("lsr", u, 1), ("lsr", u, 2)])
        mb.act(R(1), R(1), AF.Sqrt, ("lsr", u, 1),
               [("lsr", u, 1), ("c", "eps_col")],
               bias=mb.cs("eps_col")[0:1, 0:1], scale=1.0)
        mb.recip(R(1), R(1), ("lsr", u, 1), [("lsr", u, 1)])
        mb.stt(R(2), R(0), -1.0, R(1), ALU.mult, ALU.mult, ("lsr", u, 2),
               [("lsr", u, 0), ("lsr", u, 1)])
        mb.mm(mb.pb[2][:, :], onesr[:], R(1), ("pb", 2),
              [("lsr", u, 1), ("c", "ones_row")])
        mb.mm(mb.pb[3][:, :], onesr[:], R(2), ("pb", 3),
              [("lsr", u, 2), ("c", "ones_row")])
        mb.copy(ab[:], mb.pb[2][:, :], ("lsa", u), [("pb", 2)])
        mb.copy(bb[:], mb.pb[3][:, :], ("lsb", u), [("pb", 3)])
        for kc in range(KC):
            ap, res = get_piece(kc, tc, 1)
            mb.tt(t1[:], ap, ab[:], ALU.mult, ("lst", u), [res, ("lsa", u)])
            sap, sres, post = sink(kc, tc)
            if gain_col is not None:
                mb.tt(t1[:], t1[:], bb[:], ALU.add, ("lst", u),
                      [("lst", u), ("lsb", u)])
                mb.ts(sap, t1[:], gain_col(kc), bias_col(kc), ALU.mult, ALU.add,
                      sres, [("lst", u)] + list(affine_reads))
            else:
                mb.tt(sap, t1[:], bb[:], ALU.add, sres,
                      [("lst", u), ("lsb", u)])
            post()


def pre_block2(mb: MB):
    p = mb.p
    lncol = mb.sbuf("b2ln", [128, 16], F32)
    p.dma("sync", lncol[:, 0:8], mb.dram["ln2a_g"][:, :], writes=[("lng",)])
    p.dma("sync", lncol[:, 8:16], mb.dram["ln2a_b"][:, :], writes=[("lnb",)])
    b2ac = mb.sbuf("b2ac", [128, 8], F32)
    p.dma("sync", b2ac[:], mb.dram["b2a_out"][:, :], writes=[("b2ac",)])
    fcol = mb.sbuf("b2fc", [128, 2 * (22 + 22 + 8 + 8 + 8 + 1 + 1)], F32)
    FC = {}
    off = 0
    for l in range(2):
        for nm, wdt in (("b1a", 22), ("b1g", 22), ("b2", 8), ("lg", 8),
                        ("lb", 8), ("qs", 1), ("ks", 1)):
            FC[(nm, l)] = fcol[:, off:off + wdt]
            p.dma("sync", fcol[:, off:off + wdt], mb.dram[f"b2f_{nm}{l}"][:, :],
                  writes=[("fc",)])
            off += wdt
    mb.fc2 = FC


def emit_block2(mb: MB):
    p = mb.p
    xf_dram = mb.dtmp("b2_xf", [NSEQ, KC, 128, T], F32)
    xnh_dram = mb.dtmp("b2_xnh", [NSEQ, KC, 128, T], BF16)
    y_dram = mb.dtmp("b2_y", [NSEQ, KC, 128, T], BF16)
    xspill = mb.dtmp("b2_xs", [KC, 128, NSEQ * T], F32)
    sxb_dram = mb.dtmp("b2_sxb", [KC, 128, T], BF16)
    sxm_dram = mb.dtmp("b2_sxm", [KC, 128, T], BF16)
    y2f_dram = mb.dtmp("b2_y2f", [4, 128, T], BF16)
    for nm in (["w2a_qkvT", "w2a_outT"]
               + [f"{b}{l}" for l in range(2) for b in
                  ("w2f_qkv", "w2f_wo", "w2f_w1a", "w2f_w1g", "w2f_w2")]):
        mb.stage_weight_bf16(nm)
    lncol, b2ac, FC = mb.sb["b2ln"], mb.sb["b2ac"], mb.fc2

    mb.ph("b2.ln")
    for s in range(NSEQ):
        uL = mb.uid()
        xst = [mb.sbuf_scoped(f"b2xst{uL}_{i}", [128, 512], F32) for i in range(2)]

        def xn_sink(kc, tc, s=s, uL=uL):
            i = (kc + tc) % 2
            res = ("xst2", uL, i)

            def post(kc=kc, tc=tc, i=i, res=res):
                p.dma("sync", xf_dram[s, kc, :, tc * 512:(tc + 1) * 512],
                      xst[i][:, :], reads=[res], writes=[("xfd", s, kc)])
            return xst[i][:, :], res, post
        mb.ln_fm(s, xn_sink, add_xn_into_x=False)
        mb.end_stage()
        uL = mb.uid()
        xpc = [mb.sbuf_scoped(f"b2xp{uL}_{i}", [128, 512], F32) for i in range(2)]
        xob = [mb.sbuf_scoped(f"b2xo{uL}_{i}", [128, 512], BF16) for i in range(2)]

        def get_piece(kc, tc, phase, s=s, uL=uL):
            i = (kc + tc + phase) % 2
            res = ("xpc", uL, i)
            p.dma("sync", xpc[i][:, :],
                  xf_dram[s, kc, :, tc * 512:(tc + 1) * 512],
                  reads=[("xfd", s, kc)], writes=[res])
            return xpc[i][:, :], res

        def sink(kc, tc, s=s, uL=uL):
            i = (kc + tc) % 2
            res = ("xob", uL, i)

            def post(kc=kc, tc=tc, i=i, res=res):
                p.dma("sync", xnh_dram[s, kc, :, tc * 512:(tc + 1) * 512],
                      xob[i][:, :], reads=[res], writes=[("xnhd", s, kc)])
            return xob[i][:, :], res, post
        ln_src(mb, get_piece, sink,
               gain_col=lambda kc: lncol[:, kc:kc + 1],
               bias_col=lambda kc: lncol[:, 8 + kc:9 + kc], tag="b2i",
               affine_reads=[("lng",), ("lnb",)])
        mb.end_stage()

        mb.ph("b2.attn")
        wqkv = mb.wbf["w2a_qkvT"]
        u = mb.uid()
        ropeC_t = mb.load_const_scoped("rope2_C")
        ropeS_t = mb.load_const_scoped("rope2_S")
        wq = [mb.sbuf_scoped(f"b2wq{u}_0", [128, KC, 128], BF16)]
        wk = [mb.sbuf_scoped(f"b2wk{u}_0", [128, KC, 128], BF16)]
        wv = [mb.sbuf_scoped(f"b2wv{u}_0", [128, KC, 128], BF16)]
        xnb = [mb.sbuf_scoped(f"b2xnb{u}_{i}", [128, 1024], BF16) for i in range(2)]
        qp = mb.sbuf_scoped(f"b2qp{u}", [128, 1024], BF16)
        kp = mb.sbuf_scoped(f"b2kp{u}", [128, 1024], BF16)
        vp = mb.sbuf_scoped(f"b2vp{u}", [128, 16, 128], BF16)
        QQ = mb.sbuf_scoped(f"b2QQ{u}", [128, T], BF16)
        KK = mb.sbuf_scoped(f"b2KK{u}", [128, T], BF16)
        lr2 = mb.sbuf_scoped(f"b2lr2{u}", [2, T], BF16)
        pt = mb.sbuf_scoped(f"b2pt{u}", [128, T], BF16)
        rtmp = mb.sbuf_scoped(f"b2rt{u}", [128, 512], BF16)
        ystage = mb.sbuf_scoped(f"b2ys{u}", [128, T], BF16)
        stats = mb.sbuf_scoped(f"b2st{u}", [128, 16], F32)
        lrow = mb.sbuf_scoped(f"b2lr{u}", [128, 48], F32)
        lbf = mb.sbuf_scoped(f"b2lbf{u}", [128, 32], BF16)
        ltr = mb.sbuf_scoped(f"b2ltr{u}", [16, 128], BF16)
        trash = mb.sbuf_scoped(f"b2tr{u}", [128, 512], BF16)
        for pair in range(8):
            for kind, tile, woff in (("q", wq[0], 0), ("k", wk[0], 1024),
                                     ("v", wv[0], 2048)):
                src = wqkv[:, woff + 128 * pair: woff + 128 * pair + 128]
                src = src.rearrange("(kc p) c -> p kc c", p=128)
                p.dma("scalar", tile[:], src, reads=[("wbf", "w2a_qkvT")],
                      writes=[("w2s", kind, u)])
            for half in range(2):
                for kc in range(KC):
                    xb = xnb[kc % 2]
                    p.dma("sync", xb[:],
                          xnh_dram[s, kc, :, half * 1024:half * 1024 + 1024],
                          reads=[("xnhd", s, kc)], writes=[("xnb2", u, kc % 2)])
                    for t2_ in range(2):
                        lc = slice(t2_ * 512, t2_ * 512 + 512)
                        mb.mm(mb.pb[t2_][:, :], wq[0][:, kc, :], xb[:, lc],
                              ("pb", t2_), [("w2s", "q", u), ("xnb2", u, kc % 2)],
                              first=(kc == 0), last=(kc == KC - 1))
                        mb.mm(mb.pb[2 + t2_][:, :], wk[0][:, kc, :], xb[:, lc],
                              ("pb", 2 + t2_), [("w2s", "k", u), ("xnb2", u, kc % 2)],
                              first=(kc == 0), last=(kc == KC - 1))
                    for tt_ in range(8):
                        trows = slice(tt_ * 128, tt_ * 128 + 128)
                        bank = mb.pb[4 + tt_ // 4]
                        bslc = bank[:, (tt_ % 4) * 128:(tt_ % 4) * 128 + 128]
                        mb.mm(bslc, xb[:, trows], wv[0][:, kc, :],
                              ("pb", 4 + tt_ // 4),
                              [("xnb2", u, kc % 2), ("w2s", "v", u)],
                              first=(kc == 0 and tt_ % 4 == 0),
                              last=(kc == KC - 1 and tt_ % 4 == 3))
                for t2_ in range(2):
                    lc = slice(t2_ * 512, t2_ * 512 + 512)
                    mb.act(qp[:, lc], mb.pb[t2_][:, :], AF.Identity, ("qp2", u),
                           [("pb", t2_)], scale=0.125)
                    mb.act(kp[:, lc], mb.pb[2 + t2_][:, :], AF.Identity,
                           ("kp2", u), [("pb", 2 + t2_)])
                for tt_ in range(8):
                    bank = mb.pb[4 + tt_ // 4]
                    bslc = bank[:, (tt_ % 4) * 128:(tt_ % 4) * 128 + 128]
                    mb.act(vp[:, half * 8 + tt_, :], bslc, AF.Identity,
                           ("vp2", u), [("pb", 4 + tt_ // 4)])
                for which, src_, srcres, dst, dres in (
                        ("q", qp, ("qp2", u), QQ, ("QQ2", u)),
                        ("k", kp, ("kp2", u), KK, ("KK2", u))):
                    for t2_ in range(2):
                        lc = slice(t2_ * 512, t2_ * 512 + 512)
                        gcols = slice(half * 1024 + t2_ * 512,
                                      half * 1024 + t2_ * 512 + 512)
                        mb.mm(mb.pb[6][:, :], mb.cs("rope2_Rt")[:], src_[:, lc],
                              ("pb", 6), [srcres, ("c", "rope2_Rt")])
                        mb.tt(rtmp[:, :], src_[:, lc], ropeC_t[:, gcols],
                              ALU.mult, ("rt2", u), [srcres, ("c", "rope2_C")])
                        mb.tt(trash[:, :], mb.pb[6][:, :], ropeS_t[:, gcols],
                              ALU.mult, ("tr2", u), [("pb", 6), ("c", "rope2_S")])
                        mb.tt(dst[:, gcols], rtmp[:, :], trash[:, :], ALU.add,
                              dres, [("rt2", u), ("tr2", u)])
            for hh in range(2):
                rows = slice(64 * hh, 64 * hh + 64)
                for qt in range(16):
                    for cch in range(4):
                        pbs = mb.pb[cch ]
                        ccs = slice(cch * 512, cch * 512 + 512)
                        mb.mm(pbs[:, :], QQ[rows, qt * 128:qt * 128 + 128],
                              KK[rows, ccs], ("pb", cch ),
                              [("QQ2", u), ("KK2", u)])
                        mb.red(stats[:, cch:cch + 1], pbs[:, :], ALU.max,
                               ("st3", u, cch), [("pb", cch )], negate=True)
                        mb.act(trash[:, :], pbs[:, :], AF.Exp, ("tr2", u),
                               [("pb", cch ), ("st3", u, cch)],
                               bias=stats[:, cch:cch + 1],
                               accum=stats[:, 4 + cch:5 + cch],
                               accum_res=("st3", u, 4 + cch))
                    _stats_combine(mb, u, stats, lrow, qt, 4, "st3")
                _lrows_emit(mb, u, lrow, lbf, ltr, lr2[0:1, :], lr2[1:2, :],
                            ("lr22", u))
                yb = {0: 3, 1: 4, 2: 5, 3: 6}
                for jt in range(16):
                    for qc in range(4):
                        pbs = mb.pb[qc % 3]
                        qcs = slice(qc * 512, qc * 512 + 512)
                        mb.mm(pbs[:, :], KK[rows, jt * 128:jt * 128 + 128],
                              QQ[rows, qcs], ("pb", qc % 3),
                              [("QQ2", u), ("KK2", u)], first=True, last=False)
                        mb.mm(pbs[:, :], mb.cs("ones2b")[:, :], lr2[:, qcs],
                              ("pb", qc % 3), [("lr22", u), ("c", "ones2b")],
                              first=False, last=True)
                        mb.act(pt[:, qcs], pbs[:, :], AF.Exp, ("pt2", u, qc),
                               [("pb", qc % 3)])
                        mb.mm(mb.pb[yb[qc]][0:64, :],
                              vp[:, jt, 64 * hh:64 * hh + 64], pt[:, qcs],
                              ("pb", yb[qc]), [("vp2", u), ("pt2", u, qc)],
                              first=(jt == 0), last=(jt == 15))
                for qc in range(4):
                    mb.act(ystage[rows, qc * 512:(qc + 1) * 512],
                           mb.pb[yb[qc]][0:64, :], AF.Identity, ("ys2", u, hh),
                           [("pb", yb[qc])])
            p.dma("sync", y_dram[s, pair, :, :], ystage[:, :],
                  reads=[("ys2", u, 0), ("ys2", u, 1)], writes=[("y2d", s, pair)])
        mb.end_stage()

        u = mb.uid()
        wpk = mb.sbuf_scoped(f"b2wp{u}", [128, KC, 512], BF16)
        ybuf = [mb.sbuf_scoped(f"b2yb{u}_{i}", [128, 512], BF16) for i in range(2)]
        c0 = s * T
        for ocg in range(2):
            src = mb.wbf["w2a_outT"][:, ocg * 512:(ocg + 1) * 512]
            src = src.rearrange("(kc p) c -> p kc c", p=128)
            p.dma("scalar", wpk[:], src, reads=[("wbf", "w2a_outT")],
                  writes=[("wp2", u)])
            for tc in range(4):
                for kc in range(KC):
                    yb_ = ybuf[kc % 2]
                    p.dma("scalar", yb_[:],
                          y_dram[s, kc, :, tc * 512:(tc + 1) * 512],
                          reads=[("y2d", s, kc)], writes=[("yb2", u, kc % 2)])
                    for o4 in range(4):
                        mb.mm(mb.pb[o4][:, :], wpk[:, kc, o4 * 128:(o4 + 1) * 128],
                              yb_[:], ("pb", o4), [("wp2", u), ("yb2", u, kc % 2)],
                              first=(kc == 0), last=(kc == KC - 1))
                for o4 in range(4):
                    oc = ocg * 4 + o4
                    cols = slice(c0 + tc * 512, c0 + tc * 512 + 512)
                    mb.stt(mb.x[oc][:, cols], mb.pb[o4][:, :],
                           b2ac[:, oc:oc + 1], mb.x[oc][:, cols],
                           ALU.add, ALU.add, ("x", oc),
                           [("pb", o4), ("b2ac",), ("x", oc)])
        mb.end_stage()

    mb.ph("b2.ffn")
    for k in range(KC):
        p.dma("sync", xspill[k, :, :], mb.x[k][:], reads=[("x", k)],
              writes=[("xsp", k)])
    mb.free_x()
    for s in range(NSEQ):
        u0 = mb.uid()
        xf_cms = [mb.nc.sbuf_tensor(f"b2xf{u0}_{k}", [128, T + 8], F32)
                  for k in range(KC)]
        xf = [cm.__enter__() for cm in xf_cms]
        for k in range(KC):
            mb.memset(xf[k][:, 0:1], 0.0, ("xf", k))
            p.dma("sync", xf[k][:, 1:T + 1], xf_dram[s, k, :, :],
                  reads=[("xfd", s, k), ("xsp", k), ("xf", k)],
                  writes=[("xf", k)])
        for l in range(2):
            uL = mb.uid()
            sxp = [mb.sbuf_scoped(f"b2sx{uL}_{i}", [128, 512], BF16)
                   for i in range(2)]
            for kc in range(KC):
                base = 1 if kc < 4 else 0
                for tc in range(4):
                    i = (kc + tc) % 2
                    mb.copy(sxp[i][:, :],
                            xf[kc][:, base + tc * 512:base + tc * 512 + 512],
                            ("sxp", uL, i), [("xf", kc)])
                    p.dma("sync", sxb_dram[kc, :, tc * 512:(tc + 1) * 512],
                          sxp[i][:, :], reads=[("sxp", uL, i)],
                          writes=[("sxbd", kc)])
            u = mb.uid()
            xq_t = mb.load_const_scoped("xpq_C")
            xqs_t = mb.load_const_scoped("xpq_S")
            xk_t = mb.load_const_scoped("xpk_C")
            xks_t = mb.load_const_scoped("xpk_S")
            wq = mb.sbuf_scoped(f"b2fwq{u}", [128, KC, 128], BF16)
            wk = mb.sbuf_scoped(f"b2fwk{u}", [128, KC, 128], BF16)
            wv = mb.sbuf_scoped(f"b2fwv{u}", [128, KC, 128], BF16)
            xnb = [mb.sbuf_scoped(f"b2fxn{u}_{i}", [128, 512], BF16)
                   for i in range(2)]
            QQ = mb.sbuf_scoped(f"b2fQ{u}", [128, T], BF16)
            KK = mb.sbuf_scoped(f"b2fK{u}", [128, T], BF16)
            vt = mb.sbuf_scoped(f"b2fv{u}", [128, 16, 128], BF16)
            sqb = mb.sbuf_scoped(f"b2fsq{u}", [128, T], BF16)
            ssr = [mb.sbuf_scoped(f"b2fss{u}_{i}", [1, T], F32) for i in range(2)]
            RB = mb.sbuf_scoped(f"b2fRB{u}", [128, 512], F32)
            qn = mb.sbuf_scoped(f"b2fqn{u}", [128, 512], BF16)
            lr2 = mb.sbuf_scoped(f"b2flr2{u}", [2, T], BF16)
            pt = mb.sbuf_scoped(f"b2fpt{u}", [128, 1024], BF16)
            rtmp = mb.sbuf_scoped(f"b2frt{u}", [128, 512], BF16)
            ystage = mb.sbuf_scoped(f"b2fys{u}", [128, T], BF16)
            stats = mb.sbuf_scoped(f"b2fst{u}", [128, 16], F32)
            lrow = mb.sbuf_scoped(f"b2flr{u}", [128, 48], F32)
            lbf = mb.sbuf_scoped(f"b2flb{u}", [128, 32], BF16)
            ltr = mb.sbuf_scoped(f"b2flt{u}", [16, 128], BF16)
            trash = mb.sbuf_scoped(f"b2ftr{u}", [128, 640], F32)
            for pair in range(4):
                for tile, woff in ((wq, 0), (wk, 512), (wv, 1024)):
                    src = mb.wbf[f"w2f_qkv{l}"][:, woff + 128 * pair:
                                                woff + 128 * pair + 128]
                    src = src.rearrange("(kc p) c -> p kc c", p=128)
                    p.dma("scalar", tile[:], src,
                          reads=[("wbf", f"w2f_qkv{l}")], writes=[("w2f", id(tile), u)])
                for qr in range(4):
                    for kc in range(KC):
                        xb = xnb[kc % 2]
                        p.dma("sync", xb[:],
                              sxb_dram[kc, :, qr * 512:qr * 512 + 512],
                              reads=[("sxbd", kc)], writes=[("fxn", u, kc % 2)])
                        mb.mm(mb.pb[0][:, :], wq[:, kc, :], xb[:], ("pb", 0),
                              [("w2f", id(wq), u), ("fxn", u, kc % 2)],
                              first=(kc == 0), last=(kc == KC - 1))
                        mb.mm(mb.pb[1][:, :], wk[:, kc, :], xb[:], ("pb", 1),
                              [("w2f", id(wk), u), ("fxn", u, kc % 2)],
                              first=(kc == 0), last=(kc == KC - 1))
                        for tt_ in range(4):
                            bslc = mb.pb[2][:, tt_ * 128:tt_ * 128 + 128]
                            mb.mm(bslc, xb[:, tt_ * 128:tt_ * 128 + 128],
                                  wv[:, kc, :], ("pb", 2),
                                  [("fxn", u, kc % 2), ("w2f", id(wv), u)],
                                  first=(kc == 0 and tt_ == 0),
                                  last=(kc == KC - 1 and tt_ == 3))
                    lc = slice(qr * 512, qr * 512 + 512)
                    mb.act(QQ[:, lc], mb.pb[0][:, :], AF.Identity, ("fQr", u),
                           [("pb", 0)])
                    mb.act(KK[:, lc], mb.pb[1][:, :], AF.Identity, ("fKr", u),
                           [("pb", 1)])
                    mb.act(vt[:, 4 * qr:4 * qr + 4, :], mb.pb[2][:, :],
                           AF.Identity, ("fv2", u), [("pb", 2)])
                for which, SRC, sres, dres, xc, xs, xcn, xsn, sccol, qksc in (
                        ("q", QQ, ("fQr", u), ("fQ2", u), xq_t, xqs_t,
                         "xpq_C", "xpq_S", FC[("qs", l)], 8.0),
                        ("k", KK, ("fKr", u), ("fK2", u), xk_t, xks_t,
                         "xpk_C", "xpk_S", FC[("ks", l)], 1.0)):
                    mb.act(sqb[:, :], SRC[:, :], AF.Square, ("fsq", u), [sres])
                    for hh, ocol, ss_ in ((0, "top64colb", ssr[0]),
                                          (1, "bot64colb", ssr[1])):
                        for cch in range(4):
                            mb.mm(mb.pb[3][0:1, :], mb.cs(ocol)[:],
                                  sqb[:, cch * 512:(cch + 1) * 512], ("pb", 3),
                                  [("fsq", u), ("c", ocol)])
                            mb.copy(ss_[0:1, cch * 512:(cch + 1) * 512],
                                    mb.pb[3][0:1, :], ("fss", u, hh), [("pb", 3)])
                        mb.act(ss_[0:1, :], ss_[0:1, :], AF.Sqrt, ("fss", u, hh),
                               [("fss", u, hh)])
                        mb.ts(ss_[0:1, :], ss_[0:1, :], 1.0e-12, None, ALU.max,
                              None, ("fss", u, hh), [("fss", u, hh)])
                        mb.recip(ss_[0:1, :], ss_[0:1, :], ("fss", u, hh),
                                 [("fss", u, hh)])
                    for cch in range(4):
                        ccs = slice(cch * 512, cch * 512 + 512)
                        mb.mm(mb.pb[4][:, :], mb.cs("top64_row")[:],
                              ssr[0][0:1, ccs], ("pb", 4),
                              [("fss", u, 0), ("c", "top64_row")],
                              first=True, last=False)
                        mb.mm(mb.pb[4][:, :], mb.cs("bot64_row")[:],
                              ssr[1][0:1, ccs], ("pb", 4),
                              [("fss", u, 1), ("c", "bot64_row")],
                              first=False, last=True)
                        mb.ts(RB[:, :], mb.pb[4][:, :], sccol[:, 0:1], qksc,
                              ALU.mult, ALU.mult, ("fRB", u),
                              [("pb", 4), ("fc",)])
                        mb.tt(qn[:, :], SRC[:, ccs], RB[:, :], ALU.mult,
                              ("fqn", u), [sres, ("fRB", u)])
                        mb.mm(mb.pb[5][:, :], mb.cs("rope2_Rt")[:], qn[:, :],
                              ("pb", 5), [("fqn", u), ("c", "rope2_Rt")])
                        mb.tt(rtmp[:, :], qn[:, :], xc[:, ccs], ALU.mult,
                              ("frt", u), [("fqn", u), ("c", xcn)])
                        mb.tt(trash[:, 0:512], mb.pb[5][:, :], xs[:, ccs],
                              ALU.mult, ("ftr", u), [("pb", 5), ("c", xsn)])
                        mb.tt(SRC[:, ccs], rtmp[:, :], trash[:, 0:512],
                              ALU.add, dres, [("frt", u), ("ftr", u)])
                for hh in range(2):
                    rows = slice(64 * hh, 64 * hh + 64)
                    for qt in range(16):
                        kvs = max(0, qt * 128 - 512)
                        w_all = qt * 128 + 128 - kvs
                        nch = cdiv(w_all, 512)
                        for cch in range(nch):
                            off = kvs + cch * 512
                            w = min(512, qt * 128 + 128 - off)
                            pbs = mb.pb[cch % 3]
                            mb.mm(pbs[:, 0:w], QQ[rows, qt * 128:qt * 128 + 128],
                                  KK[rows, off:off + w], ("pb", cch % 3),
                                  [("fQ2", u), ("fK2", u)])
                            if off == kvs and qt >= 4:
                                mb.tt(pbs[:, 0:128], pbs[:, 0:128],
                                      mb.cs("mask_add_fl")[:, :], ALU.add,
                                      ("pb", cch % 3),
                                      [("pb", cch % 3), ("c", "mask_add_fl")])
                            if off + w == qt * 128 + 128:
                                mb.tt(pbs[:, w - 128:w], pbs[:, w - 128:w],
                                      mb.cs("mask_add_qj")[:, :], ALU.add,
                                      ("pb", cch % 3),
                                      [("pb", cch % 3), ("c", "mask_add_qj")])
                            mb.red(stats[:, cch:cch + 1], pbs[:, 0:w], ALU.max,
                                   ("st4", u, cch), [("pb", cch % 3)],
                                   negate=True)
                            mb.act(trash[:, 0:w], pbs[:, 0:w], AF.Exp,
                                   ("ftr", u), [("pb", cch % 3), ("st4", u, cch)],
                                   bias=stats[:, cch:cch + 1],
                                   accum=stats[:, 4 + cch:5 + cch],
                                   accum_res=("st4", u, 4 + cch))
                        _stats_combine(mb, u, stats, lrow, qt, nch, "st4")
                    _lrows_emit(mb, u, lrow, lbf, ltr, lr2[0:1, :], lr2[1:2, :],
                                ("flr2", u))
                    for jt in range(16):
                        ci0 = jt // 4
                        ci1 = min(3, (jt * 128 + 639) // 512)
                        qs_ = 512 * ci0
                        ptres = []
                        for ci4 in range(ci0, ci1 + 1):
                            coff = 512 * ci4
                            pbs = mb.pb[ci4 % 3]
                            mb.mm(pbs[:, :], KK[rows, jt * 128:jt * 128 + 128],
                                  QQ[rows, coff:coff + 512], ("pb", ci4 % 3),
                                  [("fQ2", u), ("fK2", u)], first=True, last=False)
                            mb.mm(pbs[:, :], mb.cs("ones2b")[:, :],
                                  lr2[:, coff:coff + 512], ("pb", ci4 % 3),
                                  [("flr2", u), ("c", "ones2b")],
                                  first=False, last=True)
                            for st in range(4):
                                o_t = coff + 128 * st - jt * 128
                                psl = pbs[:, 128 * st:128 * st + 128]
                                ptsl = pt[:, coff - qs_ + 128 * st:
                                          coff - qs_ + 128 * st + 128]
                                if o_t < 0 or o_t > 512:
                                    mb.memset(ptsl, 0.0, ("fpt", u, ci4, st))
                                    ptres.append(("fpt", u, ci4, st))
                                    continue
                                if o_t == 0:
                                    mb.tt(psl, psl, mb.cs("mask_add_fl")[:, :],
                                          ALU.add, ("pb", ci4 % 3),
                                          [("pb", ci4 % 3), ("c", "mask_add_fl")])
                                elif o_t == 512:
                                    mb.tt(psl, psl, mb.cs("mask_add_qj")[:, :],
                                          ALU.add, ("pb", ci4 % 3),
                                          [("pb", ci4 % 3), ("c", "mask_add_qj")])
                                mb.act(ptsl, psl, AF.Exp, ("fpt", u, ci4, st),
                                       [("pb", ci4 % 3)])
                                ptres.append(("fpt", u, ci4, st))
                        for ci4 in range(ci0, ci1 + 1):
                            bank = mb.pb[{0: 3, 1: 4, 2: 5, 3: 6}[ci4]]
                            mb.mm(bank[0:64, :],
                                  vt[:, jt, 64 * hh:64 * hh + 64],
                                  pt[:, 512 * ci4 - qs_:512 * ci4 - qs_ + 512],
                                  ("pb", {0: 3, 1: 4, 2: 5, 3: 6}[ci4]),
                                  [("fv2", u)] + ptres,
                                  first=(jt == max(0, 4 * ci4 - 4)),
                                  last=(jt == min(15, 4 * ci4 + 3)))
                    for ci4 in range(4):
                        mb.act(ystage[rows, ci4 * 512:(ci4 + 1) * 512],
                               mb.pb[{0: 3, 1: 4, 2: 5, 3: 6}[ci4]][0:64, :],
                               AF.Identity, ("fys2", u, hh),
                               [("pb", {0: 3, 1: 4, 2: 5, 3: 6}[ci4])])
                p.dma("sync", y2f_dram[pair, :, :], ystage[:, :],
                      reads=[("fys2", u, 0), ("fys2", u, 1)],
                      writes=[("y2fd", pair)])
            mb.end_stage()
            u = mb.uid()
            wok = mb.sbuf_scoped(f"b2fwo{u}", [128, 4, 512], BF16)
            ybuf = [mb.sbuf_scoped(f"b2fyb{u}_{i}", [128, 512], BF16)
                    for i in range(2)]
            for ocg in range(2):
                src = mb.wbf[f"w2f_wo{l}"][:, ocg * 512:(ocg + 1) * 512]
                src = src.rearrange("(hc p) c -> p hc c", p=128)
                p.dma("scalar", wok[:], src, reads=[("wbf", f"w2f_wo{l}")],
                      writes=[("fwo", u)])
                for tc in range(4):
                    for hc in range(4):
                        yb_ = ybuf[hc % 2]
                        p.dma("scalar", yb_[:],
                              y2f_dram[hc, :, tc * 512:(tc + 1) * 512],
                              reads=[("y2fd", hc)], writes=[("fyb", u, hc % 2)])
                        for o4 in range(4):
                            mb.mm(mb.pb[o4][:, :],
                                  wok[:, hc, o4 * 128:(o4 + 1) * 128], yb_[:],
                                  ("pb", o4), [("fwo", u), ("fyb", u, hc % 2)],
                                  first=(hc == 0), last=(hc == 3))
                    for o4 in range(4):
                        oc = ocg * 4 + o4
                        cols = slice(1 + tc * 512, 1 + tc * 512 + 512)
                        mb.stt(xf[oc][:, cols], mb.pb[o4][:, :], 1.0,
                               xf[oc][:, cols], ALU.mult, ALU.add, ("xf", oc),
                               [("pb", o4), ("xf", oc)])
            mb.end_stage()
            u = mb.uid()
            sxm = mb.sbuf_scoped(f"b2sm{u}", [128, KC, T], BF16)

            def get_piece(kc, tc, phase, u=u):
                base = 1 if kc < 4 else 0
                return (xf[kc][:, base + tc * 512:base + tc * 512 + 512],
                        ("xf", kc))

            def sink(kc, tc, u=u):
                return (sxm[:, kc, tc * 512:(tc + 1) * 512], ("sxm", u, kc),
                        lambda: None)
            ln_src(mb, get_piece, sink,
                   gain_col=lambda kc, l=l: FC[("lg", l)][:, kc:kc + 1],
                   bias_col=lambda kc, l=l: FC[("lb", l)][:, kc:kc + 1],
                   tag="b2m", affine_reads=[("fc",)])
            w2h = mb.sbuf_scoped(f"b2w2{u}", [128, 22, 512], BF16)
            w1s = [mb.sbuf_scoped(f"b2w1{u}_{i}", [128, KC, 128], BF16)
                   for i in range(2)]
            ag = mb.sbuf_scoped(f"b2ag{u}", [128, 22, 1024], BF16)
            gb = mb.sbuf_scoped(f"b2gb{u}", [128, 512], BF16)
            for halftc in range(2):
                tcs = [2 * halftc, 2 * halftc + 1]
                for hc in range(22):
                    for iw, wn in ((0, f"w2f_w1a{l}"), (1, f"w2f_w1g{l}")):
                        src = mb.wbf[wn][:, hc * 128:(hc + 1) * 128]
                        src = src.rearrange("(kc p) c -> p kc c", p=128)
                        p.dma("scalar", w1s[iw][:], src, reads=[("wbf", wn)],
                              writes=[("w1s2", u, iw)])
                    for itc, tc in enumerate(tcs):
                        for kc in range(KC):
                            mb.mm(mb.pb[0][:, :], w1s[0][:, kc, :],
                                  sxm[:, kc, tc * 512:(tc + 1) * 512], ("pb", 0),
                                  [("w1s2", u, 0), ("sxm", u, kc)],
                                  first=(kc == 0), last=(kc == KC - 1))
                            mb.mm(mb.pb[1][:, :], w1s[1][:, kc, :],
                                  sxm[:, kc, tc * 512:(tc + 1) * 512], ("pb", 1),
                                  [("w1s2", u, 1), ("sxm", u, kc)],
                                  first=(kc == 0), last=(kc == KC - 1))
                        mb.act(gb[:, :], mb.pb[1][:, :], AF.Gelu, ("gb2", u),
                               [("pb", 1), ("fc",)],
                               bias=FC[("b1g", l)][:, hc:hc + 1])
                        acol = slice(itc * 512, itc * 512 + 512)
                        mb.act(ag[:, hc, acol], mb.pb[0][:, :], AF.Identity,
                               ("ag2", u, hc), [("pb", 0), ("fc",)],
                               bias=FC[("b1a", l)][:, hc:hc + 1])
                        mb.tt(ag[:, hc, acol], ag[:, hc, acol], gb[:, :],
                              ALU.mult, ("ag2", u, hc),
                              [("ag2", u, hc), ("gb2", u)])
                for ocg in range(2):
                    src = mb.wbf[f"w2f_w2{l}"][:, ocg * 512:(ocg + 1) * 512]
                    src = src.rearrange("(hc p) c -> p hc c", p=128)
                    p.dma("scalar", w2h[:], src, reads=[("wbf", f"w2f_w2{l}")],
                          writes=[("w2w2", u)])
                    for itc, tc in enumerate(tcs):
                        acol = slice(itc * 512, itc * 512 + 512)
                        for o4 in range(4):
                            for hc in range(22):
                                mb.mm(mb.pb[3 + o4][:, :],
                                      w2h[:, hc, o4 * 128:o4 * 128 + 128],
                                      ag[:, hc, acol], ("pb", 3 + o4),
                                      [("w2w2", u)] + [("ag2", u, hc)],
                                      first=(hc == 0), last=(hc == 21))
                        for o4 in range(4):
                            oc = ocg * 4 + o4
                            cols = slice(1 + tc * 512, 1 + tc * 512 + 512)
                            mb.stt(xf[oc][:, cols], mb.pb[3 + o4][:, :],
                                   FC[("b2", l)][:, oc:oc + 1], xf[oc][:, cols],
                                   ALU.add, ALU.add, ("xf", oc),
                                   [("pb", 3 + o4), ("fc",), ("xf", oc)])
            mb.end_stage()
        uE = mb.uid()
        xeb = [mb.sbuf_scoped(f"b2xe{uE}_{i}", [128, 512], F32) for i in range(2)]
        for kc in range(KC):
            for tc in range(4):
                i = (kc + tc) % 2
                mb.copy(xeb[i][:, :], xf[kc][:, 1 + tc * 512:1 + tc * 512 + 512],
                        ("xeb", uE, i), [("xf", kc)])
                p.dma("sync", xf_dram[s, kc, :, tc * 512:(tc + 1) * 512],
                      xeb[i][:, :], reads=[("xeb", uE, i)], writes=[("xfd2", s, kc)])
        mb.end_stage()
        for cm in reversed(xf_cms):
            cm.__exit__(None, None, None)
    mb.realloc_x()
    for k in range(KC):
        p.dma("sync", mb.x[k][:], xspill[k, :, :], reads=[("xsp", k)],
              writes=[("x", k)])
    uR = mb.uid()
    xrb = [mb.sbuf_scoped(f"b2xr{uR}_{i}", [128, 512], F32) for i in range(2)]
    for s in range(NSEQ):
        for kc in range(KC):
            for tc in range(4):
                i = (kc + tc) % 2
                p.dma("sync", xrb[i][:, :],
                      xf_dram[s, kc, :, tc * 512:(tc + 1) * 512],
                      reads=[("xfd2", s, kc)], writes=[("xrb", uR, i)])
                cols = slice(s * T + tc * 512, s * T + tc * 512 + 512)
                mb.tt(mb.x[kc][:, cols], mb.x[kc][:, cols], xrb[i][:, :],
                      ALU.add, ("x", kc), [("x", kc), ("xrb", uR, i)])
    mb.end_stage()



def pre_block3(mb: MB):
    p = mb.p
    t = mb.sbuf("b3bo", [128, 8], F32)
    p.dma("sync", t[:], mb.dram["b3_bo"][:, :], writes=[("b3bo",)])
    t = mb.sbuf("b3g1", [128, 2], F32)
    p.dma("sync", t[:], mb.dram["b3_bg1"][:, :], writes=[("b3g1",)])
    t = mb.sbuf("b3bt", [1, 1], F32)
    p.dma("sync", t[:], mb.dram["b3_beta"][:, :], writes=[("b3bt",)])


def emit_block3(mb: MB):
    p = mb.p
    xn_dram = mb.dtmp("b3_xn", [NSEQ, KC, 128, T], BF16)
    qtmp = mb.dtmp("b3_qt", [16, 128, 1024], F32)
    ktmp = mb.dtmp("b3_kt", [16, 128, 1024], F32)
    eqd = mb.dtmp("b3_eqd", [KC, 128, T], BF16)
    ekd = mb.dtmp("b3_ekd", [KC, 128, T], BF16)
    att_dram = mb.dtmp("b3_att", [NSEQ, KC, 128, T], F32)
    ffn_dram = mb.dtmp("b3_ffn", [NSEQ, KC, 128, T], F32)
    xspill = mb.dtmp("b3_xs", [KC, 128, NSEQ * T], F32)
    for nm in (["w3_wqT", "w3_wkT", "w3_wvT", "w3_woT", "w3_wg1", "w3_wg2"]
               + [f"w3_win{e}" for e in range(8)]
               + [f"w3_wout{e}" for e in range(8)]):
        mb.stage_weight_bf16(nm)
    mb.ph("b3.ln")
    for s in range(NSEQ):
        uL = mb.uid()
        xst = [mb.sbuf_scoped(f"b3xst{uL}_{i}", [128, 512], BF16) for i in range(2)]

        def xn_sink(kc, tc, s=s, uL=uL):
            i = (kc + tc) % 2

            def post(kc=kc, tc=tc, i=i):
                p.dma("sync", xn_dram[s, kc, :, tc * 512:(tc + 1) * 512],
                      xst[i][:, :], reads=[("xst3", uL, i)],
                      writes=[("xnd3", s, kc)])
            return xst[i][:, :], ("xst3", uL, i), post
        mb.ln_fm(s, xn_sink, add_xn_into_x=False)
        mb.end_stage()
    for k in range(KC):
        p.dma("sync", xspill[k, :, :], mb.x[k][:], reads=[("x", k)],
              writes=[("xsp3", k)])
    mb.free_x()

    for s in range(NSEQ):
        u = mb.uid()
        ll_cms = [mb.nc.sbuf_tensor(f"b3vt{u}", [128, 16, 1024], BF16),
                  mb.nc.sbuf_tensor(f"b3bvb{u}", [128, 1024], F32),
                  mb.nc.sbuf_tensor(f"b3qs{u}", [128, 1024], F32),
                  mb.nc.sbuf_tensor(f"b3ks{u}", [128, 1024], F32),
                  mb.nc.sbuf_tensor(f"b3mx{u}", [128, 40], F32),
                  mb.nc.sbuf_tensor(f"b3be{u}", [128, 4], F32)]
        v_tm, bvb, qsb, ksb, mxc, becol = [cm.__enter__() for cm in ll_cms]
        xnf = mb.sbuf_scoped(f"b3xn{u}", [128, KC, T], BF16)
        for kc in range(KC):
            p.dma("sync", xnf[:, kc, :], xn_dram[s, kc, :, :],
                  reads=[("xnd3", s, kc)], writes=[("xnf", u)])
        wq3 = mb.sbuf_scoped(f"b3wq{u}", [128, KC, 1024], BF16)
        wk3 = mb.sbuf_scoped(f"b3wk{u}", [128, KC, 1024], BF16)
        wv3 = mb.sbuf_scoped(f"b3wv{u}", [128, KC, 1024], BF16)
        for tile, wn in ((wq3, "w3_wqT"), (wk3, "w3_wkT"), (wv3, "w3_wvT")):
            src = mb.wbf[wn][:, :].rearrange("(kc p) c -> p kc c", p=128)
            p.dma("scalar", tile[:], src, reads=[("wbf", wn)],
                  writes=[("w3", wn, u)])
        brow = mb.sbuf_scoped(f"b3br{u}", [1, 1024], F32)
        bqb = mb.sbuf_scoped(f"b3bqb{u}", [128, 1024], F32)
        bkb = mb.sbuf_scoped(f"b3bkb{u}", [128, 1024], F32)
        for bn, bt in (("b3_bq", bqb), ("b3_bk", bkb), ("b3_bv", bvb)):
            p.dma("sync", brow[:, :], mb.dram[bn][:, :], writes=[("brow", u)])
            for c2 in range(2):
                mb.mm(mb.pb[5][:, :], mb.cs("ones_row")[:],
                      brow[0:1, c2 * 512:(c2 + 1) * 512], ("pb", 5),
                      [("brow", u), ("c", "ones_row")])
                mb.copy(bt[:, c2 * 512:(c2 + 1) * 512], mb.pb[5][:, :],
                        ("bb3", bn, u), [("pb", 5)])
        for tt_ in range(16):
            tsl = slice(tt_ * 128, tt_ * 128 + 128)
            for oc2 in range(2):
                osl = slice(oc2 * 512, oc2 * 512 + 512)
                for kc in range(KC):
                    mb.mm(mb.pb[0][:, :], xnf[:, kc, tsl], wq3[:, kc, osl],
                          ("pb", 0), [("xnf", u), ("w3", "w3_wqT", u)],
                          first=(kc == 0), last=(kc == KC - 1))
                    mb.mm(mb.pb[1][:, :], xnf[:, kc, tsl], wk3[:, kc, osl],
                          ("pb", 1), [("xnf", u), ("w3", "w3_wkT", u)],
                          first=(kc == 0), last=(kc == KC - 1))
                    mb.mm(mb.pb[2][:, :], xnf[:, kc, tsl], wv3[:, kc, osl],
                          ("pb", 2), [("xnf", u), ("w3", "w3_wvT", u)],
                          first=(kc == 0), last=(kc == KC - 1))
                mb.stt(qsb[:, osl], mb.pb[0][:, :], 1.0, bqb[:, osl],
                       ALU.mult, ALU.add, ("qsb", u),
                       [("pb", 0), ("bb3", "b3_bq", u)])
                mb.stt(ksb[:, osl], mb.pb[1][:, :], 1.0, bkb[:, osl],
                       ALU.mult, ALU.add, ("ksb", u),
                       [("pb", 1), ("bb3", "b3_bk", u)])
                mb.act(v_tm[:, tt_, osl], mb.pb[2][:, :], AF.Identity,
                       ("vtm", u), [("pb", 2)])
            mb.red(mxc[:, tt_:tt_ + 1], qsb[:, :], ALU.max, ("mxq", u, tt_),
                   [("qsb", u)])
            mb.red(mxc[:, 16 + tt_:17 + tt_], ksb[:, :], ALU.max,
                   ("mxk", u, tt_), [("ksb", u)])
            p.dma("sync", qtmp[tt_, :, :], qsb[:, :], reads=[("qsb", u)],
                  writes=[("qtd", u, tt_)])
            p.dma("sync", ktmp[tt_, :, :], ksb[:, :], reads=[("ksb", u)],
                  writes=[("ktd", u, tt_)])
        MX = lambda j: mxc[:, 32 + j:33 + j]
        mb.red(MX(0), mxc[:, 0:16], ALU.max, ("mx2", u, 0),
               [("mxq", u, i) for i in range(16)])
        mb.red(MX(1), mxc[:, 16:32], ALU.max, ("mx2", u, 1),
               [("mxk", u, i) for i in range(16)])
        mb.p.do("gpsimd", lambda h, a=mxc[0:1, 35:36], b=MX(0): h.tensor_reduce(
            a, b, axis=AX.C, op=ALU.max),
            reads=[("mx2", u, 0)], writes=[("mx2", u, 3)])
        mb.p.do("gpsimd", lambda h, a=mxc[0:1, 36:37], b=MX(1): h.tensor_reduce(
            a, b, axis=AX.C, op=ALU.max),
            reads=[("mx2", u, 1)], writes=[("mx2", u, 4)])
        mb.act(mxc[0:1, 37:38], mb.sb["b3bt"][0:1, 0:1], AF.Exp, ("mx2", u, 5),
               [("b3bt",)])
        for j, dst in ((0, 38), (1, 39)):
            mb.mm(mb.pb[5][:, 0:1], mb.cs("ones_row")[:],
                  mxc[0:1, 35 + j:36 + j], ("pb", 5),
                  [("mx2", u, 3 + j), ("c", "ones_row")])
            mb.copy(mxc[:, dst:dst + 1], mb.pb[5][:, 0:1], ("mx2", u, 6 + j),
                    [("pb", 5)])
        mb.mm(mb.pb[5][:, 0:1], mb.cs("ones_row")[:], mxc[0:1, 37:38],
              ("pb", 5), [("mx2", u, 5), ("c", "ones_row")])
        mb.copy(becol[:, 0:1], mb.pb[5][:, 0:1], ("bec", u), [("pb", 5)])
        mb.tt(becol[:, 1:2], becol[:, 0:1], mxc[:, 38:39], ALU.mult,
              ("bec", u), [("bec", u), ("mx2", u, 6)])
        mb.ts(becol[:, 1:2], becol[:, 1:2], -1.0, None, ALU.mult, None,
              ("bec", u), [("bec", u)])
        mb.tt(becol[:, 2:3], becol[:, 0:1], mxc[:, 39:40], ALU.mult,
              ("bec", u), [("bec", u), ("mx2", u, 7)])
        mb.ts(becol[:, 2:3], becol[:, 2:3], -1.0, None, ALU.mult, None,
              ("bec", u), [("bec", u)])
        mb.end_stage()
        ektm = mb.sbuf_scoped(f"b3ektm{u}", [128, 16, 1024], BF16)
        etf = mb.sbuf_scoped(f"b3etf{u}", [128, 128], BF16)
        v2 = mb.sbuf_scoped(f"b3v2{u}", [128, 16, 1088], BF16)
        et = mb.sbuf_scoped(f"b3et{u}", [128, 1024], BF16)
        zw = mb.sbuf_scoped(f"b3zw{u}", [128, 128], F32)
        for tt_ in range(16):
            p.dma("sync", qsb[:, :], qtmp[tt_, :, :], reads=[("qtd", u, tt_)],
                  writes=[("qsb", u)])
            mb.act(et[:, :], qsb[:, :], AF.Exp, ("et", u),
                   [("qsb", u), ("bec", u)], scale=becol[:, 0:1],
                   bias=becol[:, 1:2])
            for fc in range(KC):
                mb.pe_transpose(mb.pb["t"][:, 0:128],
                                et[:, fc * 128:fc * 128 + 128], ("pbt",),
                                [("et", u)])
                mb.copy(etf[:, :], mb.pb["t"][:, 0:128], ("etf", u), [("pbt",)])
                p.dma("sync", eqd[fc, :, tt_ * 128:tt_ * 128 + 128], etf[:, :],
                      reads=[("etf", u)], writes=[("eqd", u)])
            p.dma("sync", ksb[:, :], ktmp[tt_, :, :], reads=[("ktd", u, tt_)],
                  writes=[("ksb", u)])
            mb.act(et[:, :], ksb[:, :], AF.Exp, ("et", u),
                   [("ksb", u), ("bec", u)], scale=becol[:, 0:1],
                   bias=becol[:, 2:3])
            mb.copy(ektm[:, tt_, :], et[:, :], ("ektm", u), [("et", u)])
            for fc in range(KC):
                mb.pe_transpose(mb.pb["t"][:, 0:128],
                                et[:, fc * 128:fc * 128 + 128], ("pbt",),
                                [("et", u)])
                mb.copy(etf[:, :], mb.pb["t"][:, 0:128], ("etf", u), [("pbt",)])
                p.dma("sync", ekd[fc, :, tt_ * 128:tt_ * 128 + 128], etf[:, :],
                      reads=[("etf", u)], writes=[("ekd", u)])
            mb.red(zw[:, 0:64],
                   et[:, :].rearrange("p (h d) -> p h d", d=16), ALU.add,
                   ("zw", u), [("et", u)])
            mb.recip(zw[:, 0:64], zw[:, 0:64], ("zw", u), [("zw", u)])
            mb.copy(v2[:, tt_, :].rearrange("p (h c) -> p h c", c=17)[:, :, 0:1],
                    zw[:, 0:64], ("v2", u), [("zw", u)])
            for h_ in range(64):
                mb.ts(v2[:, tt_, 17 * h_ + 1:17 * h_ + 17],
                      v_tm[:, tt_, 16 * h_:16 * h_ + 16],
                      zw[:, h_:h_ + 1], None, ALU.mult, None, ("v2", u),
                      [("vtm", u), ("zw", u)])
        mb.ph("b3.attn")
        eqh = mb.sbuf_scoped(f"b3eqh{u}", [16, T], BF16)
        ekh = mb.sbuf_scoped(f"b3ekh{u}", [16, T], BF16)
        kvs = mb.sbuf_scoped(f"b3kvs{u}", [16, 17], BF16)
        kvf = mb.sbuf_scoped(f"b3kvf{u}", [16, 17], F32)
        stm = mb.sbuf_scoped(f"b3stm{u}", [128, 256], BF16)
        rc = mb.sbuf_scoped(f"b3rc{u}", [128, 2], F32)
        otm = mb.sbuf_scoped(f"b3otm{u}", [128, 16, 1024], BF16)
        for h_ in range(64):
            fc, ro = h_ // 8, 16 * (h_ % 8)
            p.dma("sync", eqh[:, :], eqd[fc, ro:ro + 16, :], reads=[("eqd", u)],
                  writes=[("eqh", u)])
            p.dma("sync", ekh[:, :], ekd[fc, ro:ro + 16, :], reads=[("ekd", u)],
                  writes=[("ekh", u)])
            mb.memset(kvf[:, :], 0.0, ("kvf", u))
            for c in range(16):
                csl = slice(c * 128, c * 128 + 128)
                if c > 0:
                    mb.copy(kvs[:, :], kvf[:, :], ("kvs", u), [("kvf", u)])
                if c < 15:
                    mb.mm(mb.pb[6][0:16, 0:17],
                          ektm[:, c, 16 * h_:16 * h_ + 16],
                          v2[:, c, 17 * h_:17 * h_ + 17], ("pb", 6),
                          [("ektm", u), ("v2", u)])
                    mb.tt(kvf[:, :], kvf[:, :], mb.pb[6][0:16, 0:17], ALU.add,
                          ("kvf", u), [("kvf", u), ("pb", 6)])
                pbs = mb.pb[c % 2]
                mb.mm(pbs[:, 0:128], ekh[:, csl], eqh[:, csl], ("pb", c % 2),
                      [("ekh", u), ("eqh", u)])
                sts = stm[:, (c % 2) * 128:(c % 2) * 128 + 128]
                mb.tt(sts, pbs[:, 0:128], mb.cs("mask_keep_jq")[:, :],
                      ALU.mult, ("stm", u, c % 2),
                      [("pb", c % 2), ("c", "mask_keep_jq")])
                pby = mb.pb[2 + c % 2]
                mb.mm(pby[:, 0:17], sts, v2[:, c, 17 * h_:17 * h_ + 17],
                      ("pb", 2 + c % 2), [("stm", u, c % 2), ("v2", u)],
                      first=True, last=(c == 0))
                if c > 0:
                    mb.mm(pby[:, 0:17], eqh[:, csl], kvs[:, :],
                          ("pb", 2 + c % 2), [("eqh", u), ("kvs", u)],
                          first=False, last=True)
                mb.recip(rc[:, c % 2:c % 2 + 1], pby[:, 0:1], ("rc", u, c % 2),
                         [("pb", 2 + c % 2)])
                mb.stt(otm[:, c, 16 * h_:16 * h_ + 16],
                       pby[:, 1:17], rc[:, c % 2:c % 2 + 1],
                       bvb[:, 16 * h_:16 * h_ + 16],
                       ALU.mult, ALU.add, ("otm", u, h_),
                       [("pb", 2 + c % 2), ("rc", u, c % 2),
                        ("bb3", "b3_bv", u)])
        mb.ph("b3.wo")
        wo3 = mb.sbuf_scoped(f"b3wo{u}", [128, KC, 1024], BF16)
        src = mb.wbf["w3_woT"][:, :].rearrange("(kc p) c -> p kc c", p=128)
        p.dma("scalar", wo3[:], src, reads=[("wbf", "w3_woT")],
              writes=[("wo3", u)])
        ofm = mb.sbuf_scoped(f"b3ofm{u}", [128, KC, 128], BF16)
        attb = mb.sbuf_scoped(f"b3ab{u}", [128, KC, 128], F32)
        ores = [("otm", u, h_) for h_ in range(64)]
        for tt_ in range(16):
            for fc in range(KC):
                mb.pe_transpose(mb.pb["t"][:, 0:128],
                                otm[:, tt_, fc * 128:fc * 128 + 128], ("pbt",),
                                ores)
                mb.copy(ofm[:, fc, :], mb.pb["t"][:, 0:128], ("ofm", u, fc),
                        [("pbt",)])
            for oc in range(KC):
                for kc in range(KC):
                    mb.mm(mb.pb[oc % 2][:, 0:128], wo3[:, kc, oc * 128:oc * 128 + 128],
                          ofm[:, kc, :], ("pb", oc % 2),
                          [("wo3", u), ("ofm", u, kc)],
                          first=(kc == 0), last=(kc == KC - 1))
                mb.ts(attb[:, oc, :], mb.pb[oc % 2][:, 0:128],
                      mb.sb["b3bo"][:, oc:oc + 1], None, ALU.add, None,
                      ("attb", u), [("pb", oc % 2), ("b3bo",)])
            p.dma("sync",
                  att_dram[s, :, :, tt_ * 128:tt_ * 128 + 128].rearrange(
                      "kc p t -> p kc t"),
                  attb[:, :, :], reads=[("attb", u)], writes=[("attd", s)])
        mb.end_stage()
        for cm in reversed(ll_cms):
            cm.__exit__(None, None, None)

        mb.ph("b3.ffn")
        u = mb.uid()
        wg1s = mb.sbuf_scoped(f"b3wg1{u}", [128, KC, 256], BF16)
        src = mb.wbf["w3_wg1"][:, :].rearrange("(kc p) c -> p kc c", p=128)
        p.dma("scalar", wg1s[:], src, reads=[("wbf", "w3_wg1")],
              writes=[("wg1", u)])
        wg2s = mb.sbuf_scoped(f"b3wg2{u}", [128, 2, 8], BF16)
        src = mb.wbf["w3_wg2"][:, :].rearrange("(kc p) c -> p kc c", p=128)
        p.dma("scalar", wg2s[:], src, reads=[("wbf", "w3_wg2")],
              writes=[("wg2", u)])
        xnp = [mb.sbuf_scoped(f"b3xp{u}_{i}", [128, 512], BF16) for i in range(2)]
        g1f = mb.sbuf_scoped(f"b3g1{u}", [128, 2, T], BF16)
        lgr = mb.sbuf_scoped(f"b3lg{u}", [8, 512], F32)
        lgtm = mb.sbuf_scoped(f"b3lgt{u}", [128, 128], F32)
        ltm = mb.sbuf_scoped(f"b3ltm{u}", [128, 32], F32)
        gfb = mb.sbuf_scoped(f"b3gfb{u}", [8, T], BF16)
        gbf16 = mb.sbuf_scoped(f"b3gb16{u}", [128, 8], BF16)
        for tc in range(4):
            for kc in range(KC):
                xb = xnp[kc % 2]
                p.dma("sync", xb[:], xn_dram[s, kc, :, tc * 512:(tc + 1) * 512],
                      reads=[("xnd3", s, kc)], writes=[("xp3", u, kc % 2)])
                for gc in range(2):
                    mb.mm(mb.pb[gc][:, :], wg1s[:, kc, gc * 128:gc * 128 + 128],
                          xb[:], ("pb", gc), [("wg1", u), ("xp3", u, kc % 2)],
                          first=(kc == 0), last=(kc == KC - 1))
            for gc in range(2):
                mb.act(g1f[:, gc, tc * 512:(tc + 1) * 512], mb.pb[gc][:, :],
                       AF.Gelu, ("g1f", u), [("pb", gc), ("b3g1",)],
                       bias=mb.sb["b3g1"][:, gc:gc + 1])
            for gc in range(2):
                mb.mm(mb.pb[2][0:8, :], wg2s[:, gc, :],
                      g1f[:, gc, tc * 512:(tc + 1) * 512], ("pb", 2),
                      [("wg2", u), ("g1f", u)], first=(gc == 0), last=(gc == 1))
            mb.copy(lgr[:, :], mb.pb[2][0:8, :], ("lgr", u), [("pb", 2)])
            for q in range(4):
                mb.p.do("tensor", lambda h, o=mb.pb[3][:, 0:8],
                        i=lgr[0:8, q * 128:q * 128 + 128],
                        idn=mb.cs("ident_f32")[0:8, 0:8]: h.transpose(o, i, idn),
                        reads=[("lgr", u), ("c", "ident_f32")], writes=[("pb", 3)])
                mb.copy(lgtm[:, (tc * 4 + q) * 8:(tc * 4 + q) * 8 + 8],
                        mb.pb[3][:, 0:8], ("lgtm", u), [("pb", 3)])
        LC = lambda j: ltm[:, 8 + j:9 + j]
        for tt_ in range(16):
            mb.copy(ltm[:, 0:8], lgtm[:, tt_ * 8:tt_ * 8 + 8], ("ltm", u),
                    [("lgtm", u)])
            mb.red(LC(0), ltm[:, 0:8], ALU.max, ("ltm", u), [("ltm", u)],
                   negate=True)
            mb.act(ltm[:, 16:24], ltm[:, 0:8], AF.Exp, ("ltm", u),
                   [("ltm", u)], bias=LC(0), accum=LC(1),
                   accum_res=("ltm", u))
            mb.ts(ltm[:, 24:32], ltm[:, 0:8], LC(0), None, ALU.add, None,
                  ("ltm", u), [("ltm", u)])
            mb.ts(ltm[:, 24:32], ltm[:, 24:32], 0.0, None, ALU.is_ge, None,
                  ("ltm", u), [("ltm", u)])
            mb.stt(ltm[:, 24:32], ltm[:, 24:32], -3.0e38, ltm[:, 0:8],
                   ALU.mult, ALU.add, ("ltm", u), [("ltm", u)])
            mb.red(LC(2), ltm[:, 24:32], ALU.max, ("ltm", u), [("ltm", u)])
            mb.ts(ltm[:, 24:32], ltm[:, 0:8], LC(2), None, ALU.is_ge, None,
                  ("ltm", u), [("ltm", u)])
            mb.recip(LC(1), LC(1), ("ltm", u), [("ltm", u)])
            mb.tt(ltm[:, 16:24], ltm[:, 16:24], ltm[:, 24:32], ALU.mult,
                  ("ltm", u), [("ltm", u)])
            mb.ts(ltm[:, 16:24], ltm[:, 16:24], LC(1), None, ALU.mult, None,
                  ("ltm", u), [("ltm", u)])
            mb.copy(gbf16[:, :], ltm[:, 16:24], ("gb16", u), [("ltm", u)])
            mb.pe_transpose(mb.pb["t"][0:8, 0:128], gbf16[:, :], ("pbt",),
                            [("gb16", u)])
            mb.copy(gfb[:, tt_ * 128:tt_ * 128 + 128], mb.pb["t"][0:8, 0:128],
                    ("gfb", u), [("pbt",)])
        win = mb.sbuf_scoped(f"b3wi{u}", [128, KC, 2048], BF16)
        xn8 = mb.sbuf_scoped(f"b3x8{u}", [128, KC, T], BF16)
        woh = mb.sbuf_scoped(f"b3wh{u}", [128, 32, 512], BF16)
        hcur = mb.sbuf_scoped(f"b3hc{u}", [128, 16, 512], BF16)
        for kc in range(KC):
            p.dma("sync", xn8[:, kc, :], xn_dram[s, kc, :, :],
                  reads=[("xnd3", s, kc)], writes=[("xn8", u, kc)])
        hb = mb.sbuf_scoped(f"b3hb{u}", [128, 512], BF16)
        Gb = mb.sbuf_scoped(f"b3Gb{u}", [128, 512], BF16)
        yacc = mb.sbuf_scoped(f"b3ya{u}", [128, KC, T], F32)
        for e in range(8):
            src = mb.wbf[f"w3_win{e}"][:, :].rearrange("(kc p) c -> p kc c", p=128)
            p.dma("scalar", win[:], src, reads=[("wbf", f"w3_win{e}")],
                  writes=[("wi3", u)])
            for half in range(2):
                src = mb.wbf[f"w3_wout{e}"][:, half * 512:(half + 1) * 512]
                src = src.rearrange("(hc p) c -> p hc c", p=128)
                p.dma("scalar", woh[:, 16 * half:16 * half + 16, :], src,
                      reads=[("wbf", f"w3_wout{e}")], writes=[("wh3", u, half)])
            for tc in range(4):
                mb.mm(mb.pb[4][:, :], mb.cs("IND8")[:, 128 * e:128 * e + 128],
                      gfb[:, tc * 512:(tc + 1) * 512], ("pb", 4),
                      [("gfb", u), ("c", "IND8")])
                mb.act(Gb[:, :], mb.pb[4][:, :], AF.Identity, ("Gb3", u),
                       [("pb", 4)])
                for hc in range(16):
                    for kc in range(KC):
                        mb.mm(mb.pb[hc % 2][:, :],
                              win[:, kc, hc * 128:hc * 128 + 128],
                              xn8[:, kc, tc * 512:(tc + 1) * 512], ("pb", hc % 2),
                              [("wi3", u), ("xn8", u, kc)],
                              first=(kc == 0), last=(kc == KC - 1))
                    mb.act(hb[:, :], mb.pb[hc % 2][:, :], AF.Gelu,
                           ("hb3", u), [("pb", hc % 2)])
                    mb.tt(hcur[:, hc, :], hb[:, :], Gb[:, :], ALU.mult,
                          ("hc3", u, hc), [("hb3", u), ("Gb3", u)])
                for ocg in range(2):
                    for o4 in range(4):
                        for hc in range(16):
                            mb.mm(mb.pb[3 + o4][:, :],
                                  woh[:, 16 * ocg + hc, o4 * 128:o4 * 128 + 128],
                                  hcur[:, hc, :], ("pb", 3 + o4),
                                  [("wh3", u, ocg)] + [("hc3", u, hc)],
                                  first=(hc == 0), last=(hc == 15))
                        oc = ocg * 4 + o4
                        ysl = yacc[:, oc, tc * 512:(tc + 1) * 512]
                        if e == 0:
                            mb.act(ysl, mb.pb[3 + o4][:, :], AF.Identity,
                                   ("ya3", u, oc), [("pb", 3 + o4)])
                        else:
                            mb.stt(ysl, mb.pb[3 + o4][:, :], 1.0, ysl,
                                   ALU.mult, ALU.add, ("ya3", u, oc),
                                   [("pb", 3 + o4), ("ya3", u, oc)])
        for oc in range(KC):
            p.dma("sync", ffn_dram[s, oc, :, :], yacc[:, oc, :],
                  reads=[("ya3", u, oc)], writes=[("ffnd", s)])
        mb.end_stage()

    mb.realloc_x()
    for k in range(KC):
        p.dma("sync", mb.x[k][:], xspill[k, :, :], reads=[("xsp3", k)],
              writes=[("x", k)])
    uR = mb.uid()
    xrb = [mb.sbuf_scoped(f"b3xr{uR}_{i}", [128, 512], F32) for i in range(4)]
    for s in range(NSEQ):
        for kc in range(KC):
            for tc in range(4):
                i = (kc + tc) % 2
                p.dma("sync", xrb[i][:, :],
                      att_dram[s, kc, :, tc * 512:(tc + 1) * 512],
                      reads=[("attd", s)], writes=[("xrb3", uR, i)])
                p.dma("sync", xrb[2 + i][:, :],
                      ffn_dram[s, kc, :, tc * 512:(tc + 1) * 512],
                      reads=[("ffnd", s)], writes=[("xrb3", uR, 2 + i)])
                cols = slice(s * T + tc * 512, s * T + tc * 512 + 512)
                mb.tt(mb.x[kc][:, cols], mb.x[kc][:, cols], xrb[i][:, :],
                      ALU.add, ("x", kc), [("x", kc), ("xrb3", uR, i)])
                mb.tt(mb.x[kc][:, cols], mb.x[kc][:, cols], xrb[2 + i][:, :],
                      ALU.add, ("x", kc), [("x", kc), ("xrb3", uR, 2 + i)])
    mb.end_stage()



BLOCK_EMITTERS = {0: emit_block0, 1: emit_block1, 2: emit_block2, 3: emit_block3}
BLOCK_PRE = {0: pre_block0, 1: pre_block1, 2: pre_block2, 3: pre_block3}


def build_program(block_ids, consts, weights):
    used = []
    for b in block_ids:
        used += BLOCK_WEIGHTS[b]
    mb = MB(consts, weights)
    mb.setup(used, block_ids)
    mb.load_x()
    for b in block_ids:
        BLOCK_EMITTERS[b](mb)
    mb.store_x()
    mb.p.build(mb.stack)
    return mb


LAUNCH_PLAN = [[0], [1], [2], [3]]


def kernel(**inputs):
    consts = host_constants()
    weights = host_weights(inputs)
    x = np.asarray(inputs["L0_attn_x"])
    shards = []
    for c in range(N_CORES):
        xs = x[2 * c:2 * c + 2]
        xf = xs.transpose(2, 0, 1).reshape(D, NSEQ * T)
        shards.append(np.ascontiguousarray(
            xf.reshape(KC, 128, NSEQ * T), dtype=np.float32))
    cur = shards
    for block_ids in LAUNCH_PLAN:
        mb = build_program(block_ids, consts, weights)
        in_maps = []
        for c in range(N_CORES):
            m = {"x_in": cur[c]}
            if 1 in block_ids and not SIM_NO_CC:
                assert block_ids[0] == 1
                m["xoth"] = np.ascontiguousarray(np.stack(
                    [np.asarray(cur[c2]) for c2 in range(N_CORES) if c2 != c]))
            for name, arr in consts.items():
                m["c_" + name] = arr
            for b in block_ids:
                for wn in BLOCK_WEIGHTS[b]:
                    m[wn] = weights[wn]
            in_maps.append(m)
        res = run_bass_kernel_spmd(mb.nc, in_maps, core_ids=list(range(N_CORES)))
        cur = [r["x_out"] for r in res.results]
        for cm in reversed(mb._x_cms):
            cm.__exit__(None, None, None)
        mb._x_cms = []
        mb.stack.close()
    out = np.empty((16, T, D), np.float32)
    for c in range(N_CORES):
        xf = np.asarray(cur[c]).reshape(D, NSEQ, T)
        out[2 * c:2 * c + 2] = xf.transpose(1, 2, 0)
    return out
